# version 06
# baseline (speedup 1.0000x reference)
import contextlib
import numpy as np
import concourse.bass as bass
import concourse.mybir as mybir
from concourse.bass_utils import run_bass_kernel_spmd

F32 = mybir.dt.float32
BF16 = mybir.dt.bfloat16
I32 = mybir.dt.int32
AF = mybir.ActivationFunctionType
ALU = mybir.AluOpType
AXX = mybir.AxisListType.X

S, D, H, DH, RR, C = 2048, 1024, 16, 64, 48, 64
NST, NDT, NCH, NCP = 16, 8, 32, 16
HID, NHT = 3072, 24
EPS_LN, EPS_RMS = 1e-5, 1e-6
SCALE = 1.0 / 127.0


class SemT:
    def __init__(self, h):
        self.h = h
        self.n = 0

    def inc(self, instr, k=1):
        instr.then_inc(self.h, k)
        self.n += k
        return self.n


def _build(taps=(), stop_phase=99, solve_heads=H, do_chain=True, solve_nsteps=99, p4_parts=99):
    nc = bass.Bass("TRN2", target_bir_lowering=False)
    taps = list(taps)

    x_h = nc.dram_tensor("x", [S, D], F32, kind="ExternalInput").ap()
    w = {}
    for nm, shp, dt in [
        ("Wq", [D, D], F32), ("Wk", [D, D], F32), ("Wv", [D, D], F32), ("Wo", [D, D], F32),
        ("Wrq", [D, D], F32), ("Wrk", [D, D], F32),
        ("Wfl", [D, 32], F32), ("bfl_f", [16, 1], F32), ("bfl_l", [16, 1], F32),
        ("cdq", [128, NDT * 4 * 32], F32), ("cdk", [128, NDT * 4 * 32], F32),
        ("cdv", [128, NDT * 4 * 32], F32),
        ("gs", [16, 1], F32), ("qnw", [64, 1], F32), ("knw", [64, 1], F32),
        ("onw", [128, NDT], F32),
        ("wgT", [HID, D], I32), ("wuT", [HID, D], I32), ("wdT", [256, 12288], I32),
        ("ident", [128, 128], F32),
        ("maskUi", [128, 64], F32), ("maskUs", [128, 64], F32), ("maskLs", [128, 64], F32),
        ("hv4", [128, 128], F32), ("bonesAll", [128, 8 * 32], F32),
        ("onesone", [1, 128], F32),
        ("selh", [16, 128], F32), ("hsel", [16, 8], F32),
    ]:
        w[nm] = nc.dram_tensor(nm, shp, dt, kind="ExternalInput").ap()
    y_h = nc.dram_tensor("y", [S, D], F32, kind="ExternalOutput").ap()
    xnT_scr = nc.dram_tensor("xnT_scr", [128, NDT * S], BF16, kind="Internal").ap()
    tap_h = {t: nc.dram_tensor("tap_" + t, shp, dt, kind="ExternalOutput").ap()
             for t, shp, dt in taps}
    tapnames = [t for t, _, _ in taps]

    st_ = contextlib.ExitStack()
    SB = lambda n, s, d: st_.enter_context(nc.sbuf_tensor(n, s, d))

    AR0 = SB("AR0", [128, NDT * S], BF16)
    AR1 = SB("AR1", [128, NDT * S], BF16)
    AR2 = SB("AR2", [128, NDT * S], BF16)
    AR3 = SB("AR3", [128, NST * D], BF16)
    AR4 = SB("AR4", [128, NST * D], BF16)
    scrB = SB("scrB", [128, 2048], BF16)
    ksumB = SB("ksumB", [128, 2048], BF16)
    sqB = SB("sqB", [128, 2048], BF16)
    preB = sqB

    wbuf = SB("wbuf", [128, NDT * D], BF16)
    wflT = SB("wflT", [128, NDT * 32], BF16)
    bflP = SB("bflP", [48, 1], F32)
    bfP = bflP[0:16, :]
    blP = bflP[32:48, :]
    onesONE = SB("onesONE", [1, 128], BF16)
    onescol = SB("onescol", [128, 1], BF16)
    cdiag = SB("cdiag", [128, 2 * 128], BF16)
    identB = SB("identB", [128, 128], BF16)
    identF = SB("identF", [128, 128], F32)
    maskUi = SB("maskUi_s", [128, 64], F32)
    maskUs = SB("maskUs_s", [128, 64], F32)
    maskLs = SB("maskLs_s", [128, 64], F32)
    hv4B = SB("hv4B", [128, 128], BF16)
    hv4F = SB("hv4F", [128, 128], F32)
    bonesAll = SB("bonesAll_s", [128, 8 * 32], BF16)
    selhB = SB("selhB", [16, 128], BF16)
    hselB = SB("hselB", [16, 8], F32)
    hselL = SB("hselL", [16, 8], BF16)
    qnwB = SB("qnwB", [128, 1], F32)
    knwB = SB("knwB", [128, 1], F32)
    lamB = SB("lamB", [16, 1], F32)
    lamrep = SB("lamrep", [128, NDT], F32)
    onwP = SB("onwP", [128, NDT], F32)

    hm = SB("hm", [128, 2048], F32)
    hstg = SB("hstg", [128, 1024], BF16)
    sm = SB("sm", [128, 3 * 256], BF16)
    gCrep2 = SB("gCrep2", [128, 8 * 32], F32)
    stat1 = SB("stat1", [128, 1], F32)
    stat2 = SB("stat2", [128, 1], F32)
    stat3 = SB("stat3", [128, 1], F32)
    stat4 = SB("stat4", [128, 1], F32)
    MTall = SB("MTall", [128, 8 * 64], BF16)

    scrV = scrB[:].bitcast(F32)
    ksumV = ksumB[:].bitcast(F32)
    sqV = sqB[:].bitcast(F32)
    xin = ksumV
    xin2 = scrV
    xnf = ksumV
    scrF = scrV
    ksumF = ksumV
    wgt = ksumB[:, 0:1024]
    wutl = ksumB[:, 1024:2048]
    wgt2 = scrB[:, 0:1024]
    wutl2 = scrB[:, 1024:2048]
    siluB = preB
    wdTm = AR0[:, 0:NHT * 512]
    wdrow = AR0[:, NHT * 512:NHT * 512 + 3072]
    ybuf = sqV[:, 0:512]
    ybuf2 = sqV[:, 512:1024]
    xfin = ksumV[:, 0:512]
    xfin2 = ksumV[:, 512:1024]
    woB = wbuf
    PcB = AR0[:, 0:1024]
    A1B = AR0[:, 1024:2048]
    A2B = AR0[:, 2048:3072]
    B1B = AR0[:, 3072:4096]
    B2B = AR0[:, 4096:5120]
    crepB = AR0[:, 5120:6144]
    XB = AR0[:, 6144:8192]
    GTp = AR0[:, 8192:10240]
    OlocTp = AR0[:, 10240:12288]
    YKnegp = AR0[:, 12288:14336]
    F0Tp = AR0[:, 14336:16384]

    psA = st_.enter_context(nc.psum_tensor("psA", [128, 2048], F32))
    psB = st_.enter_context(nc.psum_tensor("psB", [128, 1024], F32))
    psT = st_.enter_context(nc.psum_tensor("psT", [128, 2048], BF16))
    RA = psA[:, 0:1024]
    RB = psA[:, 1024:2048]
    RUPD = psB[:, 0:512]
    ROUT = psB[:, 512:1024]

    sD = SemT(st_.enter_context(nc.semaphore("sD")))
    sP = SemT(st_.enter_context(nc.semaphore("sP")))
    sV = SemT(st_.enter_context(nc.semaphore("sV")))
    sAc = SemT(st_.enter_context(nc.semaphore("sAc")))
    sO = SemT(st_.enter_context(nc.semaphore("sO")))
    sX = SemT(st_.enter_context(nc.semaphore("sX")))
    blk = st_.enter_context(nc.Block())
    gps, syn, vec, act, ten = blk.gpsimd, blk.sync, blk.vector, blk.scalar, blk.tensor

    def stub_out():
        def _(v):
            v.wait_ge(sP.h, sP.n)
            v.wait_ge(sAc.h, sAc.n)
            v.wait_ge(sD.h, sD.n)
            v.wait_ge(sX.h, sX.n)
            i = v.memset(sqB[:], 0.0)
            sV.inc(i)
        vec(_)
        sv = sV.n

        def _(s, sv=sv):
            s.wait_ge(sV.h, sv)
            n = 0
            for sti in range(NST):
                for dm in range(2):
                    i = s.dma_start(out=y_h[sti * 128:(sti + 1) * 128, dm * 512:(dm + 1) * 512],
                                    in_=sqB[:].bitcast(F32)[:, 0:512])
                    sO.inc(i, 16)
        syn(_)
        tot = sO.n

        def _(g, tot=tot):
            g.wait_ge(sO.h, tot)
        gps(_)

    free = {"psA": 0, "psT": 0}

    HMI = {"C": 0, "S1": 1, "S2": 2, "S3": 3,
           "LNA": 1, "AG": 1, "G": 1, "N2K": 1, "INVN": 1,
           "L": 2, "B": 2,
           "A": 3, "GC": 3, "MSQ": 3, "AQG": 3, "MSK": 3}

    def hmr(k, sub=slice(0, 16)):
        i = HMI[k]
        base = i * 32
        return hm[base + sub.start: base + sub.stop, :]

    fstage_pe_init = [0]

    SMI = {"C": 0, "INVN": 1, "AG": 2}

    def smr(k):
        return sm[:, SMI[k] * 256:(SMI[k] + 1) * 256]

    def tap(name, src_ap, wait_sem, wait_val):
        if name not in tap_h:
            return
        def _(g, name=name, src_ap=src_ap, wait_sem=wait_sem, wait_val=wait_val):
            g.wait_ge(wait_sem.h, wait_val)
            i = g.dma_start(out=tap_h[name][:], in_=src_ap)
            sO.inc(i, 16)
        gps(_)

    def _(g):
        for dst, src in [(identB, "ident"), (identF, "ident"), (hv4B, "hv4"),
                         (hv4F, "hv4"), (bonesAll, "bonesAll"),
                         (onesONE, "onesone"),
                         (maskUi, "maskUi"), (maskUs, "maskUs"), (maskLs, "maskLs"),
                         (selhB, "selh"), (hselB, "hsel"), (lamB, "gs")]:
            i = g.dma_start(out=dst[:], in_=w[src][:])
            sD.inc(i, 16)
        i = g.dma_start(out=wflT[:].rearrange("p (t c) -> p t c", c=32),
                        in_=w["Wfl"][:].rearrange("(t p) c -> p t c", p=128))
        sD.inc(i, 16)
        i = g.dma_start(out=bflP[0:16, :], in_=w["bfl_f"][:])
        sD.inc(i, 16)
        i = g.dma_start(out=bflP[32:48, :], in_=w["bfl_l"][:])
        sD.inc(i, 16)
        for dst_, src_ in [(qnwB[0:64, :], "qnw"), (qnwB[64:128, :], "qnw"),
                           (knwB[0:64, :], "knw"), (knwB[64:128, :], "knw")]:
            i = g.dma_start(out=dst_, in_=w[src_][:])
            sD.inc(i, 16)
        i = g.dma_start(out=onwP[:], in_=w["onw"][:])
        sD.inc(i, 16)
    gps(_)
    consts_v = sD.n

    def _(v):
        i = v.memset(onescol[:], 1.0)
        sV.inc(i)
    vec(_)
    ones_v = sV.n

    xnT = AR0
    ln_done_v = [0] * NST
    tr_evac_v = [0] * NST
    sq_read_v = [0] * NST
    for sti in range(NST):
        xb = xin if sti % 2 == 0 else xin2

        def _(s, sti=sti, xb=xb):
            if sti >= 2:
                s.wait_ge(sV.h, ln_done_v[sti - 2])
            i = s.dma_start(out=xb, in_=x_h[sti * 128:(sti + 1) * 128, :])
            sX.inc(i, 16)
        syn(_)
        dv = sX.n

        def _(a, dv=dv, xb=xb, sti=sti):
            a.wait_ge(sX.h, dv)
            if sti >= 1:
                a.wait_ge(sV.h, sq_read_v[sti - 1])
            i = a.activation(out=psB[:], in_=xb, func=AF.Square)
            sAc.inc(i)
        act(_)
        av = sAc.n

        def _(v, dv=dv, av=av, xb=xb, sti=sti):
            v.wait_ge(sX.h, dv)
            v.reduce_sum(out=stat1[:], in_=xb, axis=AXX)
            v.wait_ge(sAc.h, av)
            i = v.reduce_sum(out=stat2[:], in_=psB[:], axis=AXX)
            sq_read_v[sti] = sV.inc(i)
            v.tensor_scalar(out=stat1[:], in0=stat1[:], scalar1=1.0 / D, scalar2=None, op0=ALU.mult)
            v.tensor_scalar(out=stat2[:], in0=stat2[:], scalar1=1.0 / D, scalar2=None, op0=ALU.mult)
            v.tensor_tensor(out=stat3[:], in0=stat1[:], in1=stat1[:], op=ALU.mult)
            v.tensor_tensor(out=stat2[:], in0=stat2[:], in1=stat3[:], op=ALU.subtract)
            i = v.tensor_scalar(out=stat2[:], in0=stat2[:], scalar1=EPS_LN, scalar2=None, op0=ALU.add)
            sV.inc(i)
        vec(_)
        vv = sV.n

        def _(a, vv=vv):
            a.wait_ge(sV.h, vv)
            a.activation(out=stat3[:], in_=stat2[:], func=AF.Ln)
            i = a.activation(out=stat4[:], in_=stat3[:], func=AF.Exp, scale=-0.5)
            sAc.inc(i)
        act(_)
        av2 = sAc.n

        def _(v, av2=av2, xb=xb, sti=sti):
            v.wait_ge(sAc.h, av2)
            i = v.tensor_scalar(out=preB[:, 0:1024], in0=xb, scalar1=stat1[:, 0:1],
                                scalar2=stat4[:, 0:1], op0=ALU.subtract, op1=ALU.mult)
            ln_done_v[sti] = sV.inc(i)
        vec(_)
        xn_v = sV.n

        def _(t, xn_v=xn_v, sti=sti):
            t.wait_ge(sV.h, xn_v)
            t.wait_ge(sD.h, consts_v)
            if sti >= 1:
                t.wait_ge(sV.h, tr_evac_v[sti - 1])
            for dtb in range(NDT):
                i = t.transpose(out=psT[:, dtb * 128:(dtb + 1) * 128],
                                in_=preB[:, dtb * 128:(dtb + 1) * 128], identity=identB[:])
            sP.inc(i)
        ten(_)
        pv = sP.n

        def _(v, pv=pv, sti=sti):
            v.wait_ge(sP.h, pv)
            dst = xnT[:].rearrange("p (t s) -> p t s", t=NDT)[:, :, sti * 128:(sti + 1) * 128]
            i = v.tensor_copy(out=dst, in_=psT[:, 0:1024].rearrange("p (t s) -> p t s", s=128))
            tr_evac_v[sti] = sV.inc(i)
        vec(_)
    xnT_v = sV.n
    free["psA"] = xnT_v
    free["psT"] = xnT_v
    tap("xnT", xnT[:], sV, xnT_v)

    if stop_phase <= 1:
        stub_out()
        st_.close()
        return nc, tapnames

    def _(a):
        a.wait_ge(sD.h, consts_v)
        i = a.activation(out=lamB[:], in_=lamB[:], func=AF.Abs)
        sAc.inc(i)
    act(_)
    lam_av = sAc.n

    def _(v, lam_av=lam_av):
        v.wait_ge(sAc.h, lam_av)
        i = v.tensor_scalar(out=hselL[:], in0=hselB[:], scalar1=lamB[:, 0:1], scalar2=None, op0=ALU.mult)
        sV.inc(i)
    vec(_)
    lam_v = sV.n

    fl_evac = [0, 0, 0, 0]
    for sh in range(4):
        def _(t, sh=sh):
            t.wait_ge(sV.h, xnT_v)
            if sh >= 2:
                t.wait_ge(sAc.h, fl_evac[sh - 2])
            base = (sh % 2) * 512
            for kt in range(NDT):
                t.matmul(out=psB[0:16, base:base + 512],
                         lhsT=wflT[:, kt * 32:kt * 32 + 16],
                         rhs=xnT[:, kt * 2048 + sh * 512: kt * 2048 + sh * 512 + 512],
                         start=(kt == 0), stop=(kt == NDT - 1))
                i = t.matmul(out=psB[32:48, base:base + 512],
                             lhsT=wflT[:, kt * 32 + 16:(kt + 1) * 32],
                             rhs=xnT[:, kt * 2048 + sh * 512: kt * 2048 + sh * 512 + 512],
                             start=(kt == 0), stop=(kt == NDT - 1))
            sP.inc(i)
        ten(_)
        pv = sP.n

        def _(a, sh=sh, pv=pv):
            a.wait_ge(sP.h, pv)
            base = (sh % 2) * 512
            a.activation(out=hmr("A")[:, sh * 512:(sh + 1) * 512], in_=psB[0:16, base:base + 512],
                         func=AF.Sigmoid, bias=bfP)
            i = a.activation(out=hmr("B")[:, sh * 512:(sh + 1) * 512], in_=psB[32:48, base:base + 512],
                             func=AF.Sigmoid, bias=blP)
            fl_evac[sh] = sAc.inc(i)
        act(_)
    fl_done_a = fl_evac[3]

    def _(a):
        a.wait_ge(sAc.h, fl_done_a)
        i = a.activation(out=hmr("LNA"), in_=hmr("A"), func=AF.Ln)
        sAc.inc(i)
    act(_)
    lna_av = sAc.n

    def _(a):
        a.wait_ge(sAc.h, lna_av)
        a.wait_ge(sV.h, free["psA"])
        i = a.activation(out=psA[0:16, :], in_=hmr("B"), func=AF.Ln)
        sAc.inc(i)
    act(_)
    lnb_av = sAc.n

    def _(v, lnb_av=lnb_av):
        v.wait_ge(sAc.h, lnb_av)
        i = v.tensor_tensor(out=psA[0:16, :], in0=psA[0:16, :], in1=hmr("LNA"), op=ALU.subtract)
        sV.inc(i)
    vec(_)
    cdiff_v = sV.n

    def _(a, cdiff_v=cdiff_v):
        a.wait_ge(sV.h, cdiff_v)
        i = a.activation(out=hmr("C"), in_=psA[0:16, :], func=AF.Exp)
        sAc.inc(i)
    act(_)
    c_av = sAc.n

    def _(v, c_av=c_av):
        v.wait_ge(sAc.h, c_av)
        v.tensor_copy(out=hmr("S2"), in_=hmr("LNA"))
        cur, other = "S2", "S3"
        for k in [1, 2, 4, 8, 16, 32]:
            c_ = hmr(cur).rearrange("p (c f) -> p c f", f=64)
            o_ = hmr(other).rearrange("p (c f) -> p c f", f=64)
            v.tensor_copy(out=o_[:, :, 0:k], in_=c_[:, :, 0:k])
            v.tensor_tensor(out=o_[:, :, k:64], in0=c_[:, :, k:64], in1=c_[:, :, 0:64 - k], op=ALU.add)
            cur, other = other, cur
        v.tensor_copy(out=psA[32:48, :], in_=hmr("L"))
        i = v.tensor_tensor(out=hmr("S1"), in0=hmr("LNA"), in1=psA[32:48, :], op=ALU.subtract)
        sV.inc(i)
    vec(_)
    hm_vv = sV.n

    def _(a):
        a.wait_ge(sV.h, hm_vv)
        i = a.activation(out=hmr("S1"), in_=hmr("S1"), func=AF.Exp)
        sAc.inc(i)
    act(_)
    ag_av = sAc.n

    def _(v, ag_av=ag_av):
        v.wait_ge(sAc.h, ag_av)
        Lr = hmr("L").rearrange("p (c f) -> p c f", f=64)
        i = v.tensor_copy(out=hmr("S3").rearrange("p (c f) -> p c f", f=64),
                          in_=Lr[:, :, 63:64].broadcast_to([16, 32, 64]))
        sV.inc(i)
    vec(_)
    gcs_vv = sV.n

    def _(a, gcs_vv=gcs_vv):
        a.wait_ge(sV.h, gcs_vv)
        i = a.activation(out=hmr("GC"), in_=hmr("GC"), func=AF.Exp)
        sAc.inc(i)
    act(_)
    gc_av = sAc.n

    def _(t):
        t.wait_ge(sAc.h, gc_av)
        t.wait_ge(sV.h, max(free["psA"], lam_v))
        for chp in range(16):
            t.transpose(out=psA[:, chp * 16:(chp + 1) * 16],
                        in_=hmr("C")[:, chp * 128:(chp + 1) * 128], identity=identF[0:16, 0:16])
        for chp in range(16):
            t.transpose(out=psA[:, 256 + chp * 16: 256 + (chp + 1) * 16],
                        in_=hmr("AG")[:, chp * 128:(chp + 1) * 128], identity=identF[32:48, 32:48])
        i = t.matmul(out=psA[:, 768:776], lhsT=selhB[:], rhs=hselL[:], start=True, stop=True)
        sP.inc(i)
    ten(_)
    smt_pv = sP.n

    def _(g, smt_pv=smt_pv):
        g.wait_ge(sAc.h, gc_av)
        with nc.allow_non_contiguous_dma(reason="tiny gC sample gather"):
            for hp in range(8):
                src_ = hmr("GC", slice(2 * hp, 2 * hp + 2)).rearrange("p (c f) -> p c f", f=64)[:, :, 63:64].squeeze(2)
                i = g.dma_start(out=hstg[0:2, hp * 32:(hp + 1) * 32], in_=src_)
                sD.inc(i, 16)
    gps(_)
    gcs2_v = sD.n

    def _(t, gcs2_v=gcs2_v, smt_pv=smt_pv):
        t.wait_ge(sD.h, gcs2_v)
        t.wait_ge(sP.h, smt_pv)
        for hp in range(8):
            i = t.matmul(out=psA[:, 512 + hp * 32: 512 + (hp + 1) * 32], lhsT=hv4B[0:2, :],
                         rhs=hstg[0:2, hp * 32:(hp + 1) * 32], start=True, stop=True)
        fstage_pe_init[0] = sP.inc(i)
    ten(_)
    sm_pv = sP.n

    def _(v):
        v.wait_ge(sP.h, sm_pv)
        v.tensor_copy(out=smr("C"), in_=psA[:, 0:256])
        v.tensor_copy(out=smr("AG"), in_=psA[:, 256:512])
        v.tensor_copy(out=gCrep2[:], in_=psA[:, 512:768])
        i = v.tensor_copy(out=lamrep[:], in_=psA[:, 768:776])
        free["psA"] = sV.inc(i)
    vec(_)
    sm_vv = sV.n
    tap("hm", hm[:], sV, sm_vv)
    tap("sm", sm[:], sV, sm_vv)
    tap("gCrep2", gCrep2[:], sV, sm_vv)

    wload_v = {}

    def load_weight(name, wait_pe=0):
        def _(g, name=name, wait_pe=wait_pe):
            if wait_pe:
                g.wait_ge(sP.h, wait_pe)
            i = g.dma_start(out=wbuf[:].rearrange("p (t c) -> p t c", c=D),
                            in_=w[name][:].rearrange("(t p) c -> p t c", p=128))
            sD.inc(i, 16)
        gps(_)
        wload_v[name] = sD.n

    cd_used = {}

    def load_cdiag_dt(name, dt):
        def _(g, name=name, dt=dt):
            if (name, dt - 2) in cd_used:
                g.wait_ge(sP.h, cd_used[(name, dt - 2)])
            i = g.dma_start(out=cdiag[:, (dt % 2) * 128:(dt % 2) * 128 + 128],
                            in_=w[name][:, dt * 128:(dt + 1) * 128])
            sD.inc(i, 16)
        gps(_)
        wload_v[(name, dt)] = sD.n

    conv_evac_v = [0, 0]

    def proj_dt(dt, wname, out_sb, conv_name=None):
        def _(t, dt=dt, wname=wname):
            t.wait_ge(sD.h, wload_v[wname])
            t.wait_ge(sV.h, free["psA"])
            for sc in range(4):
                for kt in range(NDT):
                    i = t.matmul(out=psA[:, sc * 512:(sc + 1) * 512],
                                 lhsT=wbuf[:, kt * 1024 + dt * 128: kt * 1024 + (dt + 1) * 128],
                                 rhs=xnT[:, kt * 2048 + sc * 512: kt * 2048 + sc * 512 + 512],
                                 start=(kt == 0), stop=(kt == NDT - 1))
            sP.inc(i)
        ten(_)
        mm_pv = sP.n

        if conv_name is None:
            def _(v, dt=dt, mm_pv=mm_pv):
                v.wait_ge(sP.h, mm_pv)
                v.wait_ge(sV.h, sm_vv)
                i = v.tensor_scalar(out=out_sb[:, dt * 2048:(dt + 1) * 2048],
                                    in0=psA[:], scalar1=lamrep[:, dt:dt + 1], scalar2=None, op0=ALU.mult)
                free["psA"] = sV.inc(i)
            vec(_)
            return sV.n

        def _(v, mm_pv=mm_pv):
            v.wait_ge(sP.h, mm_pv)
            i = v.tensor_copy(out=preB[:], in_=psA[:])
            free["psA"] = sV.inc(i)
        vec(_)
        pre_vv = sV.n

        for half in range(2):
            def _(t, half=half, dt=dt, pre_vv=pre_vv):
                t.wait_ge(sV.h, pre_vv)
                t.wait_ge(sD.h, wload_v[(conv_name, dt)])
                t.wait_ge(sAc.h, fl_done_a)
                if conv_evac_v[1] or conv_evac_v[0]:
                    t.wait_ge(sV.h, max(conv_evac_v))
                for sc in range(2):
                    ofs = half * 1024 + sc * 512
                    for sub in range(4):
                        for j in [3, 2, 1, 0]:
                            delta = 3 - j
                            lo = max(0, delta - ofs)
                            i = t.matmul(
                                out=psB[32 * sub:32 * sub + 32, sc * 512 + lo:(sc + 1) * 512],
                                lhsT=cdiag[32 * sub:32 * sub + 32,
                                           (dt % 2) * 128 + j * 32: (dt % 2) * 128 + j * 32 + 32],
                                rhs=preB[32 * sub:32 * sub + 32,
                                         ofs + lo - delta: ofs + 512 - delta],
                                start=(j == 3), stop=(j == 0),
                                tile_position=(32 * sub, 32 * sub))
                cd_used[(conv_name, dt)] = sP.inc(i)
            ten(_)
            h_pv = sP.n

            def _(v, half=half, dt=dt, h_pv=h_pv):
                v.wait_ge(sP.h, h_pv)
                i = v.tensor_copy(out=out_sb[:, dt * 2048 + half * 1024: dt * 2048 + (half + 1) * 1024],
                                  in_=psB[:])
                conv_evac_v[half] = sV.inc(i)
            vec(_)
        return conv_evac_v[1]

    def stats_all(src_sb, slot, ready_v):
        sq_pv = [0]
        for dt in range(NDT):
            def _(v, dt=dt, ready_v=ready_v):
                v.wait_ge(sV.h, ready_v)
                if sq_pv[0]:
                    v.wait_ge(sP.h, sq_pv[0])
                elif dt == 0:
                    v.wait_ge(sV.h, free["psA"])
                i = v.tensor_tensor(out=sqB[:], in0=src_sb[:, dt * 2048:(dt + 1) * 2048],
                                    in1=src_sb[:, dt * 2048:(dt + 1) * 2048], op=ALU.mult)
                sV.inc(i)
            vec(_)
            sq_vv = sV.n

            def _(t, sq_vv=sq_vv, dt=dt):
                t.wait_ge(sV.h, sq_vv)
                if dt == 0:
                    t.wait_ge(sV.h, free["psA"])
                for sc in range(4):
                    i = t.matmul(out=psA[0:32, sc * 512:(sc + 1) * 512],
                                 lhsT=bonesAll[:, dt * 32:(dt + 1) * 32],
                                 rhs=sqB[:, sc * 512:(sc + 1) * 512],
                                 start=(dt == 0), stop=(dt == NDT - 1))
                sq_pv[0] = sP.inc(i)
            ten(_)
        st_pv = sP.n

        def _(v, slot=slot, st_pv=st_pv):
            v.wait_ge(sP.h, st_pv)
            i = v.tensor_copy(out=hmr(slot), in_=psA[0:16, :])
            free["psA"] = sV.inc(i)
        vec(_)
        return sV.n

    def rsqrt_pipe(src_slot, dst_slot, scale, eps, v_ready):
        def _(v, v_ready=v_ready):
            v.wait_ge(sV.h, max(v_ready, free["psA"]))
            i = v.tensor_scalar(out=psA[0:16, :], in0=hmr(src_slot), scalar1=scale, scalar2=eps,
                                op0=ALU.mult, op1=ALU.add)
            sV.inc(i)
        vec(_)
        t1v = sV.n

        def _(a, t1v=t1v):
            a.wait_ge(sV.h, t1v)
            a.activation(out=psA[32:48, :], in_=psA[0:16, :], func=AF.Ln)
            i = a.activation(out=hmr(dst_slot), in_=psA[32:48, :], func=AF.Exp, scale=-0.5)
            sAc.inc(i)
        act(_)
        ra = sAc.n

        def _(v, ra=ra):
            v.wait_ge(sAc.h, ra)
            i = v.nop()
            free["psA"] = sV.inc(i)
        vec(_)
        return sV.n

    fstage_pe = fstage_pe_init

    def field_mm(hm_slot, dt, ready_v):
        def _(g, dt=dt, ready_v=ready_v, hm_slot=hm_slot):
            g.wait_ge(sV.h, ready_v)
            if fstage_pe[0]:
                g.wait_ge(sP.h, fstage_pe[0])
            i = g.dma_start(out=hstg[0:2, :], in_=hmr(hm_slot, slice(2 * dt, 2 * dt + 2))[:, 0:1024])
            sD.inc(i, 16)
            i = g.dma_start(out=hstg[32:34, :], in_=hmr(hm_slot, slice(2 * dt, 2 * dt + 2))[:, 1024:2048])
            sD.inc(i, 16)
        gps(_)
        stg_v = sD.n

        def _(t, stg_v=stg_v):
            t.wait_ge(sD.h, stg_v)
            t.wait_ge(sV.h, free["psA"])
            for sc in range(4):
                b = (sc // 2) * 32
                rhs = hstg[b:b + 2, (sc % 2) * 512:(sc % 2) * 512 + 512]
                i = t.matmul(out=psA[:, sc * 512:(sc + 1) * 512],
                             lhsT=hv4B[b:b + 2, :], rhs=rhs, start=True, stop=True,
                             tile_position=(b, 0))
            fstage_pe[0] = sP.inc(i)
        ten(_)
        return sP.n

    if stop_phase <= 2:
        stub_out()
        st_.close()
        return nc, tapnames

    load_weight("Wrq")
    rq_v = 0
    for dt in range(NDT):
        rq_v = proj_dt(dt, "Wrq", AR4, None)

    if stop_phase <= 3:
        stub_out()
        st_.close()
        return nc, tapnames

    load_weight("Wq", wait_pe=sP.n)
    qcv = 0
    for dt in range(NDT):
        load_cdiag_dt("cdq", dt)
        qcv = proj_dt(dt, "Wq", AR2, "cdq")
    q_stat_v = stats_all(AR2, "MSQ", qcv)

    aq_v = rsqrt_pipe("MSQ", "MSQ", 1.0 / DH, EPS_RMS, q_stat_v)

    def _(a, aq_v=aq_v):
        a.wait_ge(sV.h, aq_v)
        i = a.activation(out=hmr("G"), in_=hmr("L"), func=AF.Exp)
        sAc.inc(i)
    act(_)
    g_av = sAc.n

    def _(v, g_av=g_av):
        v.wait_ge(sAc.h, g_av)
        v.wait_ge(sV.h, free["psA"])
        v.tensor_copy(out=psA[32:48, :], in_=hmr("G"))
        i = v.tensor_tensor(out=hmr("AQG"), in0=hmr("MSQ"), in1=psA[32:48, :], op=ALU.mult)
        free["psA"] = sV.inc(i)
    vec(_)
    aqg_v = sV.n

    for dt in range(NDT):
        f1 = field_mm("AQG", dt, aqg_v)

        def _(v, dt=dt, f1=f1):
            v.wait_ge(sP.h, f1)
            v.tensor_scalar(out=scrF[:, 0:1024], in0=AR2[:, dt * 2048:dt * 2048 + 1024],
                            scalar1=qnwB[:, 0:1], scalar2=None, op0=ALU.mult)
            v.tensor_scalar(out=xnf, in0=AR2[:, dt * 2048 + 1024:(dt + 1) * 2048],
                            scalar1=qnwB[:, 0:1], scalar2=None, op0=ALU.mult)
            v.tensor_tensor(out=scrF[:, 0:1024], in0=scrF[:, 0:1024], in1=psA[:, 0:1024], op=ALU.mult)
            i = v.tensor_tensor(out=xnf, in0=xnf, in1=psA[:, 1024:2048], op=ALU.mult)
            free["psA"] = sV.inc(i)
        vec(_)
        f2 = field_mm("G", dt, aqg_v)

        def _(v, dt=dt, f2=f2, rq_v=rq_v):
            v.wait_ge(sP.h, f2)
            v.wait_ge(sV.h, rq_v)
            v.tensor_tensor(out=sqB[:], in0=AR4[:, dt * 2048:(dt + 1) * 2048],
                            in1=psA[:], op=ALU.mult)
            v.tensor_tensor(out=AR1[:, dt * 2048:dt * 2048 + 1024],
                            in0=scrF[:, 0:1024], in1=sqB[:].bitcast(F32)[:, 0:1024] if False else sqB[:, 0:1024],
                            op=ALU.add)
            i = v.tensor_tensor(out=AR1[:, dt * 2048 + 1024:(dt + 1) * 2048],
                                in0=xnf, in1=sqB[:, 1024:2048], op=ALU.add)
            free["psA"] = sV.inc(i)
        vec(_)
    QD_v = sV.n
    tap("QD", AR1[:], sV, QD_v)

    if stop_phase <= 4:
        stub_out()
        st_.close()
        return nc, tapnames

    load_weight("Wv", wait_pe=sP.n)
    for dt in range(NDT):
        load_cdiag_dt("cdv", dt)
        cv = proj_dt(dt, "Wv", AR2, "cdv")

        def _(t, dt=dt, cv=cv):
            t.wait_ge(sV.h, cv)
            if free["psT"]:
                t.wait_ge(sV.h, free["psT"])
            for sb in range(16):
                i = t.transpose(out=psT[:, sb * 128:(sb + 1) * 128],
                                in_=AR2[:, dt * 2048 + sb * 128: dt * 2048 + (sb + 1) * 128],
                                identity=identB[:])
            sP.inc(i)
        ten(_)
        tr_pv = sP.n

        def _(v, dt=dt, tr_pv=tr_pv):
            v.wait_ge(sP.h, tr_pv)
            v.wait_ge(sV.h, sm_vv)
            dst = AR3[:].rearrange("p (t c) -> p t c", c=1024)[:, :, dt * 128:(dt + 1) * 128] \
                .rearrange("p t (h f) -> p t h f", f=64)
            src = psT[:].rearrange("p (t c) -> p t c", c=128).rearrange("p t (h f) -> p t h f", f=64)
            in1 = smr("AG").rearrange("p (t h) -> p t h", h=16)[:, :, 2 * dt:2 * dt + 2] \
                .unsqueeze(3).broadcast_to([128, 16, 2, 64])
            i = v.tensor_tensor(out=dst, in0=src, in1=in1, op=ALU.mult)
            free["psT"] = sV.inc(i)
        vec(_)
    VagT_v = sV.n
    tap("VagT", AR3[:], sV, VagT_v)

    if stop_phase <= 5:
        stub_out()
        st_.close()
        return nc, tapnames

    load_weight("Wrk", wait_pe=sP.n)
    rk_v = 0
    for dt in range(NDT):
        rk_v = proj_dt(dt, "Wrk", AR4, None)

    if stop_phase <= 6:
        stub_out()
        st_.close()
        return nc, tapnames

    load_weight("Wk", wait_pe=sP.n)
    kcv = 0
    for dt in range(NDT):
        load_cdiag_dt("cdk", dt)
        kcv = proj_dt(dt, "Wk", AR2, "cdk")
    k_stat_v = stats_all(AR2, "MSK", kcv)

    def _(s):
        s.wait_ge(sP.h, sP.n)
        i = s.dma_start(out=xnT_scr[:], in_=AR0[:])
        sX.inc(i, 16)
    syn(_)
    xnT_dump_v = sX.n

    ak_v = rsqrt_pipe("MSK", "MSK", 1.0 / DH, EPS_RMS, k_stat_v)
    akb_v = ak_v

    ksum_done = 0
    for dt in range(NDT):
        f1 = field_mm("MSK", dt, akb_v)

        def _(v, dt=dt, f1=f1, rk_v=rk_v):
            v.wait_ge(sP.h, f1)
            v.wait_ge(sV.h, rk_v)
            if dt == 0:
                v.wait_ge(sX.h, xnT_dump_v)
            for hf in range(2):
                o = dt * 2048 + hf * 1024
                v.tensor_scalar(out=ksumF[:], in0=AR2[:, o:o + 1024],
                                scalar1=knwB[:, 0:1], scalar2=None, op0=ALU.mult)
                v.tensor_tensor(out=ksumF[:], in0=ksumF[:],
                                in1=psA[:, hf * 1024:(hf + 1) * 1024], op=ALU.mult)
                v.tensor_tensor(out=ksumF[:], in0=ksumF[:], in1=AR4[:, o:o + 1024], op=ALU.add)
                i = v.tensor_copy(out=AR0[:, o:o + 1024], in_=ksumF[:])
            free["psA"] = sV.inc(i)
        vec(_)
        ksum_done = sV.n
    ksum_v = ksum_done
    n2_v = stats_all(AR0, "N2K", ksum_done)

    invn_v = rsqrt_pipe("N2K", "N2K", 1.0, 1e-24, n2_v)
    invnb_v = invn_v

    def _(t, invn_v=invn_v):
        t.wait_ge(sV.h, max(invn_v, free["psA"]))
        for chp in range(16):
            i = t.transpose(out=psA[:, chp * 16:(chp + 1) * 16],
                            in_=hmr("N2K")[:, chp * 128:(chp + 1) * 128], identity=identF[32:48, 32:48])
        sP.inc(i)
    ten(_)
    ivs_pv = sP.n

    def _(v, ivs_pv=ivs_pv):
        v.wait_ge(sP.h, ivs_pv)
        i = v.tensor_copy(out=smr("INVN"), in_=psA[:, 0:256])
        free["psA"] = sV.inc(i)
    vec(_)
    invnsm_v = sV.n

    for dt in range(NDT):
        f1 = field_mm("N2K", dt, invnb_v)

        def _(v, dt=dt, f1=f1):
            v.wait_ge(sP.h, f1)
            i = v.tensor_tensor(out=AR2[:, dt * 2048:(dt + 1) * 2048],
                                in0=AR0[:, dt * 2048:(dt + 1) * 2048], in1=psA[:], op=ALU.mult)
            free["psA"] = sV.inc(i)
        vec(_)

        def _(t, dt=dt):
            t.wait_ge(sV.h, max(invnsm_v, ksum_v, free["psT"]))
            for sb in range(16):
                i = t.transpose(out=psT[:, sb * 128:(sb + 1) * 128],
                                in_=AR0[:, dt * 2048 + sb * 128: dt * 2048 + (sb + 1) * 128],
                                identity=identB[:])
            sP.inc(i)
        ten(_)
        kt_pv = sP.n

        def _(v, dt=dt, kt_pv=kt_pv):
            v.wait_ge(sP.h, kt_pv)
            dst = AR4[:].rearrange("p (t c) -> p t c", c=1024)[:, :, dt * 128:(dt + 1) * 128] \
                .rearrange("p t (h f) -> p t h f", f=64)
            src = psT[:].rearrange("p (t c) -> p t c", c=128).rearrange("p t (h f) -> p t h f", f=64)
            in1 = smr("INVN").rearrange("p (t h) -> p t h", h=16)[:, :, 2 * dt:2 * dt + 2] \
                .unsqueeze(3).broadcast_to([128, 16, 2, 64])
            i = v.tensor_tensor(out=dst, in0=src, in1=in1, op=ALU.mult)
            free["psT"] = sV.inc(i)
        vec(_)
    KD_v = KT_v = sV.n
    tap("KD", AR2[:], sV, KD_v)
    tap("KT", AR4[:], sV, KT_v)

    if stop_phase <= 7:
        stub_out()
        st_.close()
        return nc, tapnames

    KD_, QD_, KT_, VagT_ = AR2, AR1, AR4, AR3
    consts_ready = max(KD_v, KT_v, QD_v, VagT_v, sm_vv, invnsm_v, hm_vv)

    def _(v):
        i = v.memset(MTall[:], 0.0)
        sV.inc(i)
    vec(_)
    mt0_v = sV.n

    last_evac = {"RA": free["psA"], "RB": free["psA"]}
    last_evacA = {"RA": 0, "RB": 0}
    freeA = {"psT": 0}
    crep_pe = [0]
    head_state = {}
    chain_upd_v = {}
    chain_out_evac = {}

    def kd_sl(h, ch):
        return KD_[(h % 2) * 64:(h % 2) * 64 + 64,
                   (h // 2) * 2048 + ch * 64:(h // 2) * 2048 + (ch + 1) * 64]

    def qd_sl(h, ch):
        return QD_[(h % 2) * 64:(h % 2) * 64 + 64,
                   (h // 2) * 2048 + ch * 64:(h // 2) * 2048 + (ch + 1) * 64]

    def kt_sl(h, ch):
        return KT_[(ch % 2) * 64:(ch % 2) * 64 + 64,
                   (ch // 2) * 1024 + h * 64:(ch // 2) * 1024 + (h + 1) * 64]

    def x_sl(ch, j0, j1):
        return XB[(ch % 2) * 64:(ch % 2) * 64 + 64,
                  (ch // 2) * 128 + j0:(ch // 2) * 128 + j1]

    def pw_sl(buf, ch):
        return buf[(ch % 2) * 64:(ch % 2) * 64 + 64, (ch // 2) * 64:(ch // 2) * 64 + 64]

    maskUi_b = maskUi[:].unsqueeze(1).broadcast_to([128, 16, 64])
    maskUs_b = maskUs[:].unsqueeze(1).broadcast_to([128, 16, 64])
    maskLs_b = maskLs[:].unsqueeze(1).broadcast_to([128, 16, 64])

    def solve_steps(h):
        crep_av_h = [0]

        def s_crep():
            def _(g, h=h):
                g.wait_ge(sV.h, consts_ready)
                if crep_pe[0]:
                    g.wait_ge(sP.h, crep_pe[0])
                crow = hmr("C", slice(h, h + 1)).rearrange("p (c f) -> p c f", f=128)
                i = g.dma_start(out=hstg[64:65, :].rearrange("p (c f) -> p c f", f=64),
                                in_=crow[:, :, 0:64])
                sD.inc(i, 16)
                i = g.dma_start(out=hstg[65:66, :].rearrange("p (c f) -> p c f", f=64),
                                in_=crow[:, :, 64:128])
                sD.inc(i, 16)
            gps(_)
            stg_v = sD.n

            def _(t, stg_v=stg_v):
                t.wait_ge(sD.h, stg_v)
                t.wait_ge(sV.h, last_evac["RA"])
                if last_evacA["RA"]:
                    t.wait_ge(sAc.h, last_evacA["RA"])
                for cc_ in range(2):
                    i = t.matmul(out=RA[:, cc_ * 512:(cc_ + 1) * 512],
                                 lhsT=hv4B[64:66, :], rhs=hstg[64:66, cc_ * 512:(cc_ + 1) * 512],
                                 start=True, stop=True, tile_position=(64, 0))
                crep_pe[0] = sP.inc(i)
            ten(_)
            pv = sP.n

            def _(a, pv=pv):
                a.wait_ge(sP.h, pv)
                i = a.activation(out=crepB[:], in_=RA, func=AF.Copy)
                last_evacA["RA"] = sAc.inc(i)
            act(_)
            crep_av_h[0] = sAc.n
        yield s_crep

        def s_p1():
            def _(t, h=h):
                t.wait_ge(sV.h, max(consts_ready, last_evac["RA"], last_evac["RB"]))
                if last_evacA["RA"] or last_evacA["RB"]:
                    t.wait_ge(sAc.h, max(last_evacA["RA"], last_evacA["RB"]))
                for ch in range(NCH):
                    tp = ((h % 2) * 64, (ch % 2) * 64)
                    t.matmul(out=pw_sl(RA, ch), lhsT=kd_sl(h, ch), rhs=qd_sl(h, ch),
                             start=True, stop=True, tile_position=tp)
                    i = t.matmul(out=pw_sl(RB, ch), lhsT=kd_sl(h, ch), rhs=kd_sl(h, ch),
                                 start=True, stop=True, tile_position=tp)
                sP.inc(i)
            ten(_)
            pv = sP.n

            def _(v, pv=pv, h=h):
                v.wait_ge(sP.h, pv)
                if crep_av_h[0]:
                    v.wait_ge(sAc.h, crep_av_h[0])
                c_sm_b = smr("C").rearrange("p (t j) -> p t j", j=16)[:, :, h:h + 1].broadcast_to([128, 16, 64])
                RAr = RA.rearrange("p (t f) -> p t f", f=64)
                RBr = RB.rearrange("p (t f) -> p t f", f=64)
                v.tensor_tensor(out=PcB[:].rearrange("p (t f) -> p t f", f=64), in0=RAr, in1=maskUi_b, op=ALU.mult)
                v.tensor_tensor(out=A1B[:].rearrange("p (t f) -> p t f", f=64), in0=RBr, in1=maskUs_b, op=ALU.mult)
                v.tensor_tensor(out=A1B[:].rearrange("p (t f) -> p t f", f=64),
                                in0=A1B[:].rearrange("p (t f) -> p t f", f=64), in1=c_sm_b, op=ALU.mult)
                v.tensor_tensor(out=B1B[:].rearrange("p (t f) -> p t f", f=64), in0=RBr, in1=maskLs_b, op=ALU.mult)
                i = v.tensor_tensor(out=B1B[:], in0=B1B[:], in1=crepB[:], op=ALU.mult)
                last_evac["RA"] = last_evac["RB"] = sV.inc(i)
            vec(_)

            def _(v, h=h):
                dstv = XB[:].rearrange("p (t f) -> p t f", f=128)[:, :, 0:64]
                dstk = XB[:].rearrange("p (t f) -> p t f", f=128)[:, :, 64:128]
                srcv = VagT_[:].rearrange("p (t f) -> p t f", f=1024)[:, :, h * 64:(h + 1) * 64]
                srck = KT_[:].rearrange("p (t f) -> p t f", f=1024)[:, :, h * 64:(h + 1) * 64]
                v.tensor_copy(out=dstv, in_=srcv)
                i = v.tensor_copy(out=dstk, in_=srck)
                sV.inc(i)
            vec(_)
        yield s_p1

        state = {"x_v": None, "a_ready": None, "a_ready_a": 0,
                 "bufs": (A1B, B1B, A2B, B2B)}

        def mk_level(lv):
            def s_level():
                Acur, Bcur, Anxt, Bnxt = state["bufs"]
                x_v = state["x_v"] if state["x_v"] is not None else sV.n
                a_ready = state["a_ready"] if state["a_ready"] is not None else sV.n
                a_ready_a = state["a_ready_a"]

                def _(t, Acur=Acur, x_v=x_v, a_ready=a_ready, a_ready_a=a_ready_a):
                    t.wait_ge(sV.h, max(a_ready, x_v, last_evac["RA"], last_evac["RB"]))
                    wa = max(a_ready_a, last_evacA["RA"], last_evacA["RB"])
                    if wa:
                        t.wait_ge(sAc.h, wa)
                    for ch in range(NCH):
                        reg = RA if ch < 16 else RB
                        cc = ch % 16
                        out = reg[(ch % 2) * 64:(ch % 2) * 64 + 64, (cc // 2) * 128:(cc // 2) * 128 + 128]
                        i = t.matmul(out=out, lhsT=pw_sl(Acur, ch), rhs=x_sl(ch, 0, 128),
                                     start=True, stop=True, tile_position=((ch % 2) * 64, (ch % 2) * 64))
                    sP.inc(i)
                ten(_)
                pv = sP.n
                op = ALU.subtract if lv == 0 else ALU.add

                def _(v, pv=pv, op=op):
                    v.wait_ge(sP.h, pv)
                    v.tensor_tensor(out=XB[:, 0:1024], in0=XB[:, 0:1024], in1=RA, op=op)
                    i = v.tensor_tensor(out=XB[:, 1024:2048], in0=XB[:, 1024:2048], in1=RB, op=op)
                    state["x_v"] = last_evac["RA"] = last_evac["RB"] = sV.inc(i)
                vec(_)

                if lv < 5:
                    def _(t, Acur=Acur, Bcur=Bcur, a_ready_a=a_ready_a):
                        t.wait_ge(sV.h, state["x_v"])
                        if a_ready_a:
                            t.wait_ge(sAc.h, a_ready_a)
                        for ch in range(NCH):
                            tp = ((ch % 2) * 64, (ch % 2) * 64)
                            t.matmul(out=pw_sl(RA, ch), lhsT=pw_sl(Acur, ch), rhs=pw_sl(Bcur, ch),
                                     start=True, stop=True, tile_position=tp)
                            i = t.matmul(out=pw_sl(RB, ch), lhsT=pw_sl(Bcur, ch), rhs=pw_sl(Acur, ch),
                                         start=True, stop=True, tile_position=tp)
                        sP.inc(i)
                    ten(_)
                    pv2 = sP.n

                    def _(a, pv2=pv2, Anxt=Anxt, Bnxt=Bnxt):
                        a.wait_ge(sP.h, pv2)
                        a.activation(out=Bnxt[:], in_=RA, func=AF.Copy)
                        i = a.activation(out=Anxt[:], in_=RB, func=AF.Copy)
                        state["a_ready_a"] = last_evacA["RA"] = last_evacA["RB"] = sAc.inc(i)
                    act(_)
                    state["bufs"] = (Anxt, Bnxt, Acur, Bcur)
            return s_level

        for lv in range(6):
            yield mk_level(lv)

        def s_p4():
            if p4_parts < 1:
                return
            def _(v, h=h):
                v.wait_ge(sV.h, state["x_v"])
                c_b = smr("C").rearrange("p (t j) -> p t j", j=16)[:, :, h:h + 1].broadcast_to([128, 16, 128])
                i = v.tensor_tensor(out=XB[:].rearrange("p (t f) -> p t f", f=128),
                                    in0=XB[:].rearrange("p (t f) -> p t f", f=128), in1=c_b, op=ALU.mult)
                sV.inc(i)
            vec(_)
            xc_v = sV.n
            if p4_parts < 2:
                return

            def _(t, xc_v=xc_v, h=h):
                t.wait_ge(sV.h, max(xc_v, last_evac["RA"], last_evac["RB"]))
                if last_evacA["RA"] or last_evacA["RB"]:
                    t.wait_ge(sAc.h, max(last_evacA["RA"], last_evacA["RB"]))
                for ch in range(NCH):
                    tp = ((ch % 2) * 64, (ch % 2) * 64)
                    t.matmul(out=pw_sl(RA, ch), lhsT=x_sl(ch, 0, 64), rhs=pw_sl(PcB, ch),
                             start=True, stop=True, tile_position=tp)
                    i = t.matmul(out=pw_sl(RB, ch), lhsT=x_sl(ch, 64, 128), rhs=pw_sl(PcB, ch),
                                 start=True, stop=True, tile_position=tp)
                sP.inc(i)
            ten(_)
            pv = sP.n

            psTF = psT[:].bitcast(F32)

            def _(t, h=h):
                t.wait_ge(sV.h, free["psT"])
                if freeA["psT"]:
                    t.wait_ge(sAc.h, freeA["psT"])
                for ch in range(NCH):
                    tp = ((ch % 2) * 64, (ch % 2) * 64)
                    i = t.matmul(out=pw_sl(psTF, ch), lhsT=x_sl(ch, 64, 128), rhs=kt_sl(h, ch),
                                 start=True, stop=True, tile_position=tp)
                sP.inc(i)
            ten(_)
            pvYK = sP.n
            if p4_parts < 3:
                return

            def _(a, pvYK=pvYK):
                a.wait_ge(sP.h, pvYK)
                i = a.activation(out=hstg[:], in_=psTF, func=AF.Copy, scale=-1.0)
                sAc.inc(i)
            act(_)
            yk_a = sAc.n

            def _(g, yk_a=yk_a, h=h):
                g.wait_ge(sAc.h, yk_a)
                hh = (h % 2) * 64
                for par in range(2):
                    src_y = hstg[par * 64:par * 64 + 64, :].rearrange("p (c f) -> p c f", f=64)
                    dst_y = YKnegp[hh:hh + 64, :].rearrange("p (c f) -> p c f", f=64)[:, par::2, :]
                    i = g.dma_start(out=dst_y, in_=src_y)
                    sD.inc(i, 16)
            gps(_)
            rpy_v = sD.n

            def _(t, yk_a=yk_a, h=h):
                t.wait_ge(sAc.h, yk_a)
                for ch in range(NCH):
                    tp = ((ch % 2) * 64, (ch % 2) * 64)
                    i = t.matmul(out=pw_sl(psTF, ch), lhsT=kt_sl(h, ch), rhs=x_sl(ch, 0, 64),
                                 start=True, stop=True, tile_position=tp)
                sP.inc(i)
            ten(_)
            pvF0 = sP.n

            def _(a, pvF0=pvF0, rpy_v=rpy_v):
                a.wait_ge(sP.h, pvF0)
                a.wait_ge(sD.h, rpy_v)
                i = a.activation(out=hstg[:], in_=psTF, func=AF.Copy)
                freeA["psT"] = sAc.inc(i)
            act(_)
            f0_a = sAc.n

            def _(g, f0_a=f0_a, h=h):
                g.wait_ge(sAc.h, f0_a)
                hh = (h % 2) * 64
                for par in range(2):
                    src_f = hstg[par * 64:par * 64 + 64, :].rearrange("p (c f) -> p c f", f=64)
                    dst_f = F0Tp[hh:hh + 64, :].rearrange("p (c f) -> p c f", f=64)[:, par::2, :]
                    i = g.dma_start(out=dst_f, in_=src_f)
                    sD.inc(i, 16)
            gps(_)
            rpf_v = sD.n

            def _(a, pvF0=pvF0):
                a.wait_ge(sP.h, pvF0)
                a.activation(out=XB[:, 0:1024], in_=RA, func=AF.Copy)
                i = a.activation(out=XB[:, 1024:2048], in_=RB, func=AF.Copy)
                last_evacA["RA"] = last_evacA["RB"] = sAc.inc(i)
            act(_)
            oz_a = sAc.n

            def _(g, oz_a=oz_a, h=h):
                g.wait_ge(sAc.h, oz_a)
                hh = (h % 2) * 64
                for par in range(2):
                    src_o = XB[par * 64:par * 64 + 64, 0:1024].rearrange("p (c f) -> p c f", f=64)
                    dst_o = OlocTp[hh:hh + 64, :].rearrange("p (c f) -> p c f", f=64)[:, par::2, :]
                    i = g.dma_start(out=dst_o, in_=src_o)
                    sD.inc(i, 16)
                    src_z = XB[par * 64:par * 64 + 64, 1024:2048].rearrange("p (c f) -> p c f", f=64)
                    dst_z = GTp[hh:hh + 64, :].rearrange("p (c f) -> p c f", f=64)[:, par::2, :]
                    i = g.dma_start(out=dst_z, in_=src_z)
                    sD.inc(i, 16)
            gps(_)
            rp1_v = sD.n

            def _(v, rp1_v=rp1_v, h=h):
                v.wait_ge(sD.h, rp1_v)
                hh = (h % 2) * 64
                qsl = QD_[hh:hh + 64, (h // 2) * 2048:(h // 2) * 2048 + 2048]
                i = v.tensor_tensor(out=GTp[hh:hh + 64, :], in0=qsl,
                                    in1=GTp[hh:hh + 64, :], op=ALU.subtract)
                sV.inc(i)
            vec(_)
            gt_v = sV.n
            if p4_parts < 4:
                return

            def _(v, rp1_v=rp1_v, h=h):
                v.wait_ge(sD.h, rp1_v)
                i = v.nop()
                last_evac["RA"] = max(last_evac["RA"], sV.inc(i))
                head_state[h] = sV.n
            vec(_)
        yield s_p4

    def chain_step(h, ch):
        hh = (h % 2) * 64
        hp = h // 2
        mt = MTall[hh:hh + 64, hp * 64:(hp + 1) * 64]
        slot_o = ROUT[hh:hh + 64, (ch % 8) * 64:(ch % 8) * 64 + 64]
        slot_u = RUPD[hh:hh + 64, (ch % 2) * 64:(ch % 2) * 64 + 64]

        def _(t, h=h, ch=ch):
            if ch == 0:
                t.wait_ge(sV.h, max(head_state[h], mt0_v))
                if h > 0:
                    t.wait_ge(sV.h, chain_out_evac[(h - 1, 3)])
                    t.wait_ge(sV.h, chain_upd_v[(h - 1, NCH - 1)])
            else:
                t.wait_ge(sV.h, chain_upd_v[(h, ch - 1)])
            if ch >= 8:
                t.wait_ge(sV.h, chain_out_evac[(h, ch // 8 - 1)])
            t.matmul(out=slot_o, lhsT=mt, rhs=GTp[hh:hh + 64, ch * 64:(ch + 1) * 64],
                     start=True, stop=True, tile_position=(hh, hh))
            t.matmul(out=slot_u, lhsT=YKnegp[hh:hh + 64, ch * 64:(ch + 1) * 64],
                     rhs=mt, start=True, stop=False, tile_position=(hh, hh))
            t.matmul(out=slot_u, lhsT=identB[hh:hh + 64, hh:hh + 64],
                     rhs=mt, start=False, stop=False, tile_position=(hh, hh))
            i = t.matmul(out=slot_u, lhsT=identB[hh:hh + 64, hh:hh + 64],
                         rhs=F0Tp[hh:hh + 64, ch * 64:(ch + 1) * 64],
                         start=False, stop=True, tile_position=(hh, hh))
            sP.inc(i)
        ten(_)
        pv = sP.n

        def _(v, h=h, ch=ch, pv=pv):
            v.wait_ge(sP.h, pv)
            gc = gCrep2[hh:hh + 64, hp * 32 + ch: hp * 32 + ch + 1]
            i = v.tensor_scalar(out=mt, in0=slot_u, scalar1=gc, scalar2=None, op0=ALU.mult)
            chain_upd_v[(h, ch)] = sV.inc(i)
            if ch % 8 == 7:
                base = ch - 7
                dst = AR1[hh:hh + 64, hp * 2048 + base * 64: hp * 2048 + (base + 8) * 64]
                i = v.tensor_tensor(out=dst, in0=ROUT[hh:hh + 64, :],
                                    in1=OlocTp[hh:hh + 64, base * 64:(base + 8) * 64], op=ALU.add)
                chain_out_evac[(h, ch // 8)] = sV.inc(i)
        vec(_)

    prev = None
    for h in range(solve_heads):
        steps = list(solve_steps(h))[:solve_nsteps]
        per = (NCH + len(steps) - 1) // len(steps) if (prev is not None and do_chain) else 0
        ci = 0
        for s in steps:
            s()
            if prev is not None and do_chain:
                for _k in range(min(per, NCH - ci)):
                    chain_step(prev, ci)
                    ci += 1
        while prev is not None and do_chain and ci < NCH:
            chain_step(prev, ci)
            ci += 1
        prev = h
    if do_chain:
        for ci in range(NCH):
            chain_step(solve_heads - 1, ci)
    attn_mix_v = sV.n
    tap("GTp", GTp[:], sV, attn_mix_v)
    tap("OlocTp", OlocTp[:], sV, attn_mix_v)
    tap("YKnegp", YKnegp[:], sV, attn_mix_v)
    tap("F0Tp", F0Tp[:], sV, attn_mix_v)
    if stop_phase <= 7.5:
        stub_out()
        st_.close()
        return nc, tapnames
    tap("outG", AR1[:], sV, attn_mix_v)
    tap("MTall", MTall[:], sV, attn_mix_v)
    tap("gCrep2end", gCrep2[:], sV, attn_mix_v)

    if stop_phase <= 8:
        stub_out()
        st_.close()
        return nc, tapnames

    sq_pe = [0]
    ro_pv = 0
    for dt in range(NDT):
        def _(v, dt=dt, attn_mix_v=attn_mix_v):
            v.wait_ge(sV.h, attn_mix_v)
            if sq_pe[0]:
                v.wait_ge(sP.h, sq_pe[0])
            i = v.tensor_tensor(out=sqB[:], in0=AR1[:, dt * 2048:(dt + 1) * 2048],
                                in1=AR1[:, dt * 2048:(dt + 1) * 2048], op=ALU.mult)
            sV.inc(i)
        vec(_)
        sqv = sV.n

        def _(t, dt=dt, sqv=sqv):
            t.wait_ge(sV.h, max(sqv, ones_v))
            if dt == 0:
                t.wait_ge(sV.h, max(last_evac["RA"], last_evac["RB"]))
                if last_evacA["RA"] or last_evacA["RB"]:
                    t.wait_ge(sAc.h, max(last_evacA["RA"], last_evacA["RB"]))
            for sc in range(4):
                i = t.matmul(out=psA[0:1, sc * 512:(sc + 1) * 512], lhsT=onescol[:],
                             rhs=sqB[:, sc * 512:(sc + 1) * 512],
                             start=(dt == 0), stop=(dt == NDT - 1))
            sq_pe[0] = sP.inc(i)
        ten(_)
        ro_pv = sP.n

    def _(v, ro_pv=ro_pv):
        v.wait_ge(sP.h, ro_pv)
        i = v.tensor_scalar(out=psA[0:1, :], in0=psA[0:1, :], scalar1=1.0 / D, scalar2=EPS_RMS,
                            op0=ALU.mult, op1=ALU.add)
        sV.inc(i)
    vec(_)
    ro1_v = sV.n

    def _(a, ro1_v=ro1_v):
        a.wait_ge(sV.h, ro1_v)
        a.activation(out=psA[32:33, :], in_=psA[0:1, :], func=AF.Ln)
        i = a.activation(out=sqB[0:1, 0:2048], in_=psA[32:33, :], func=AF.Exp, scale=-0.5)
        sAc.inc(i)
    act(_)
    ro2_v = sAc.n

    def _(v, ro2_v=ro2_v):
        v.wait_ge(sAc.h, ro2_v)
        i = v.nop()
        free["psA"] = sV.inc(i)
    vec(_)
    ro_v = sV.n

    mixn_prev = [ro_v]
    for dt in range(NDT):
        def _(t, dt=dt):
            t.wait_ge(sV.h, mixn_prev[0])
            for sc in range(4):
                i = t.matmul(out=psA[:, sc * 512:(sc + 1) * 512], lhsT=onesONE[:],
                             rhs=sqB[0:1, sc * 512:(sc + 1) * 512], start=True, stop=True)
            sP.inc(i)
        ten(_)
        rof_pv = sP.n

        def _(v, dt=dt, rof_pv=rof_pv):
            v.wait_ge(sP.h, rof_pv)
            i = v.tensor_tensor(out=AR1[:, dt * 2048:(dt + 1) * 2048],
                                in0=AR1[:, dt * 2048:(dt + 1) * 2048], in1=psA[:], op=ALU.mult)
            mixn_prev[0] = free["psA"] = sV.inc(i)
        vec(_)
    mixn_v = sV.n
    tap("mixn", AR1[:], sV, mixn_v)

    if stop_phase <= 9:
        stub_out()
        st_.close()
        return nc, tapnames

    def _(s, mixn_v=mixn_v):
        s.wait_ge(sV.h, mixn_v)
        i = s.dma_start(out=AR0[:], in_=xnT_scr[:])
        sX.inc(i, 16)
    syn(_)
    xnT2_v = sX.n

    hid_ar = [AR2, AR3, AR4]
    wT_done = {}
    hidmm_done = {}
    hid_v = 0
    for ht in range(NHT):
        wgt_b = wgt if ht % 2 == 0 else wgt2
        wut_b = wutl if ht % 2 == 0 else wutl2

        def _(g, ht=ht, wgt_b=wgt_b, wut_b=wut_b, mixn_v=mixn_v):
            if ht < 2:
                g.wait_ge(sV.h, mixn_v)
            if ht >= 2:
                g.wait_ge(sP.h, hidmm_done.get(ht - 2, 0))
            i = g.dma_start(out=wgt_b, in_=w["wgT"][ht * 128:(ht + 1) * 128, :])
            sD.inc(i, 16)
            i = g.dma_start(out=wut_b, in_=w["wuT"][ht * 128:(ht + 1) * 128, :])
            sD.inc(i, 16)
        gps(_)
        wl_v = sD.n

        def _(t, wl_v=wl_v, ht=ht, wgt_b=wgt_b):
            t.wait_ge(sD.h, wl_v)
            t.wait_ge(sV.h, free["psA"])
            if ht == 0:
                t.wait_ge(sX.h, xnT2_v)
            for sc in range(4):
                for kt in range(NDT):
                    i = t.matmul(out=psA[:, sc * 512:(sc + 1) * 512],
                                 lhsT=wgt_b[:, kt * 128:(kt + 1) * 128],
                                 rhs=AR0[:, kt * 2048 + sc * 512: kt * 2048 + sc * 512 + 512],
                                 start=(kt == 0), stop=(kt == NDT - 1))
            sP.inc(i)
        ten(_)
        g_pv = sP.n

        def _(a, g_pv=g_pv):
            a.wait_ge(sP.h, g_pv)
            i = a.activation(out=siluB[:], in_=psA[:], func=AF.Silu, scale=SCALE)
            sAc.inc(i)
        act(_)
        silu_v = sAc.n

        def _(t, silu_v=silu_v, ht=ht, wut_b=wut_b):
            t.wait_ge(sAc.h, silu_v)
            for sc in range(4):
                for kt in range(NDT):
                    i = t.matmul(out=psA[:, sc * 512:(sc + 1) * 512],
                                 lhsT=wut_b[:, kt * 128:(kt + 1) * 128],
                                 rhs=AR0[:, kt * 2048 + sc * 512: kt * 2048 + sc * 512 + 512],
                                 start=(kt == 0), stop=(kt == NDT - 1))
            hidmm_done[ht] = sP.inc(i)
        ten(_)
        u_pv = sP.n

        def _(v, u_pv=u_pv, ht=ht):
            v.wait_ge(sP.h, u_pv)
            dst = hid_ar[ht // 8][:, (ht % 8) * 2048:((ht % 8) + 1) * 2048]
            i = v.tensor_tensor(out=dst, in0=siluB[:], in1=psA[:], op=ALU.mult)
            free["psA"] = sV.inc(i)
        vec(_)
        hid_v = sV.n

    tap("xnT2", AR0[:], sX, xnT2_v)
    tap("hidden0", AR2[:], sV, hid_v)
    tap("hidden1", AR3[:], sV, hid_v)
    tap("hidden2", AR4[:], sV, hid_v)

    if stop_phase <= 10:
        stub_out()
        st_.close()
        return nc, tapnames

    def _(g, hid_v=hid_v):
        g.wait_ge(sV.h, hid_v)
        i = g.dma_start(out=wbuf[:].rearrange("p (t c) -> p t c", c=D),
                        in_=w["Wo"][:].rearrange("(t p) c -> p t c", p=128))
        sD.inc(i, 16)
    gps(_)
    wo_v = sD.n

    def _(v, wo_v=wo_v):
        v.wait_ge(sD.h, wo_v)
        in1 = onwP[:].unsqueeze(2).broadcast_to([128, NDT, D])
        i = v.tensor_tensor(out=wbuf[:].rearrange("p (t c) -> p t c", c=D),
                            in0=wbuf[:].rearrange("p (t c) -> p t c", c=D), in1=in1, op=ALU.mult)
        sV.inc(i)
    vec(_)
    wofold_v = sV.n

    y_evac = {}
    out_dma = {}
    ymm_last = {}
    for dmac in range(2):
        def _(g, dmac=dmac, hid_v=hid_v):
            g.wait_ge(sV.h, hid_v)
            if dmac == 1:
                g.wait_ge(sP.h, ymm_last[0])
            i = g.dma_start(out=wdTm, in_=w["wdT"][dmac * 128:(dmac + 1) * 128, :])
            sD.inc(i, 16)
        gps(_)
        wdl_v = sD.n

        def _(a, wdl_v=wdl_v):
            a.wait_ge(sD.h, wdl_v)
            i = a.activation(out=wdTm, in_=wdTm, func=AF.Copy, scale=SCALE * SCALE)
            sAc.inc(i)
        act(_)
        wdT_a = sAc.n

        for sti in range(NST):
            slot = sti % 2
            yb = ybuf if slot == 0 else ybuf2
            xb = xfin if slot == 0 else xfin2

            def _(s, sti=sti, dmac=dmac, xb=xb, hid_v=hid_v):
                if dmac == 0 and sti < 2:
                    s.wait_ge(sV.h, hid_v)
                key = (dmac, sti - 2) if sti >= 2 else ((dmac - 1, NST - 2 + sti) if dmac else None)
                if key in y_evac:
                    s.wait_ge(sV.h, y_evac[key])
                i = s.dma_start(out=xb, in_=x_h[sti * 128:(sti + 1) * 128,
                                                dmac * 512:(dmac + 1) * 512])
                sX.inc(i, 16)
            syn(_)
            x_v = sX.n

            def _(t, sti=sti, dmac=dmac, slot=slot, wdT_a=wdT_a):
                t.wait_ge(sAc.h, wdT_a)
                if dmac == 0 and sti == 0:
                    t.wait_ge(sV.h, max(wofold_v, hid_v, mixn_v))
                key = (dmac, sti - 2) if sti >= 2 else ((dmac - 1, NST - 2 + sti) if dmac else None)
                if key in y_evac:
                    t.wait_ge(sV.h, y_evac[key])
                reg = psB[:, slot * 512:(slot + 1) * 512]
                for kt in range(NDT):
                    t.matmul(out=reg,
                             lhsT=AR1[:, kt * 2048 + sti * 128: kt * 2048 + (sti + 1) * 128],
                             rhs=wbuf[:, kt * 1024 + dmac * 512: kt * 1024 + (dmac + 1) * 512],
                             start=(kt == 0), stop=False)
                for b in range(NHT):
                    i = t.matmul(out=reg,
                                 lhsT=hid_ar[b // 8][:, (b % 8) * 2048 + sti * 128:
                                                     (b % 8) * 2048 + (sti + 1) * 128],
                                 rhs=wdTm[:, b * 512:(b + 1) * 512],
                                 start=False, stop=(b == NHT - 1))
                ymm_last[dmac] = sP.inc(i)
            ten(_)
            ymm_pv = sP.n

            def _(v, ymm_pv=ymm_pv, x_v=x_v, slot=slot, yb=yb, xb=xb, dmac=dmac, sti=sti):
                v.wait_ge(sP.h, ymm_pv)
                v.wait_ge(sX.h, x_v)
                key = (dmac, sti - 2) if sti >= 2 else ((dmac - 1, NST - 2 + sti) if dmac else None)
                if key in out_dma:
                    v.wait_ge(sO.h, out_dma[key])
                reg = psB[:, slot * 512:(slot + 1) * 512]
                i = v.tensor_tensor(out=yb, in0=reg, in1=xb, op=ALU.add)
                y_evac[(dmac, sti)] = sV.inc(i)
            vec(_)
            ye_v = sV.n

            def _(s, sti=sti, dmac=dmac, ye_v=ye_v, yb=yb):
                s.wait_ge(sV.h, ye_v)
                i = s.dma_start(out=y_h[sti * 128:(sti + 1) * 128, dmac * 512:(dmac + 1) * 512],
                                in_=yb)
                out_dma[(dmac, sti)] = sO.inc(i, 16)
            syn(_)

    total_out = sO.n

    def _(g, total_out=total_out):
        g.wait_ge(sO.h, total_out)
    gps(_)

    st_.close()
    return nc, tapnames


_cache = {}


def _get_nc(tapkey=()):
    if tapkey not in _cache:
        _cache[tapkey] = _build([(t, list(s), d) for t, s, d in tapkey])
    return _cache[tapkey]


def _host_inputs(inputs, b):
    f = lambda k: np.ascontiguousarray(np.asarray(inputs[k], dtype=np.float32))
    Wrq = np.asarray(inputs["L0_attn_Wrq"], np.float32)
    Wrk = np.asarray(inputs["L0_attn_Wrk"], np.float32)
    Wrq_p = np.zeros((D, 1024), np.float32)
    Wrk_p = np.zeros((D, 1024), np.float32)
    for h in range(H):
        Wrq_p[:, h * 64:h * 64 + RR] = Wrq[:, h * RR:(h + 1) * RR]
        Wrk_p[:, h * 64:h * 64 + RR] = Wrk[:, h * RR:(h + 1) * RR]
    Wfl = np.concatenate([np.asarray(inputs["L0_attn_Wf"], np.float32),
                          np.asarray(inputs["L0_attn_Wl"], np.float32)], axis=1)
    bfl = np.concatenate([np.asarray(inputs["L0_attn_bf"], np.float32),
                          np.asarray(inputs["L0_attn_bl"], np.float32)])[None, :]

    def cdiag_of(name):
        cw = np.asarray(inputs[name], np.float32)
        out = np.zeros((128, NDT, 4, 32), np.float32)
        for d in range(D):
            p, dt = d % 128, d // 128
            out[p, dt, :, p % 32] = cw[d, :]
        return np.ascontiguousarray(out.reshape(128, NDT * 4 * 32))

    onw = np.ascontiguousarray(np.asarray(inputs["L0_attn_on_w"], np.float32).reshape(NDT, 128).T)
    ident = np.eye(128, dtype=np.float32)
    t = np.arange(64)
    maskUi = np.ascontiguousarray(np.tile((t[:, None] <= t[None, :]).astype(np.float32), (2, 1)))
    maskUs = np.ascontiguousarray(np.tile((t[:, None] < t[None, :]).astype(np.float32), (2, 1)))
    maskLs = np.ascontiguousarray(np.tile((t[:, None] > t[None, :]).astype(np.float32), (2, 1)))
    hv4 = np.zeros((128, 128), np.float32)
    for bb_ in range(0, 128, 32):
        hv4[bb_, 0:64] = 1.0
        hv4[bb_ + 1, 64:128] = 1.0
    hv4[64, :] = 0.0
    hv4[65, :] = 0.0
    hv4[64, 0:64] = 1.0
    hv4[65, 64:128] = 1.0
    bonesAll = np.zeros((128, 8, 32), np.float32)
    for dt_ in range(8):
        bonesAll[0:64, dt_, 2 * dt_] = 1.0
        bonesAll[64:128, dt_, 2 * dt_ + 1] = 1.0
    bonesAll = np.ascontiguousarray(bonesAll.reshape(128, 256))
    p = np.arange(128)
    hsl = np.arange(16)
    selh = ((p[None, :] >= 64).astype(np.int32) == (hsl[:, None] % 2)).astype(np.float32)
    hseldt = (np.arange(8)[None, :] == (hsl[:, None] // 2)).astype(np.float32)

    return {
        "x": np.ascontiguousarray(np.asarray(inputs["L0_attn_x"], np.float32)[b]),
        "Wq": f("L0_attn_Wq"), "Wk": f("L0_attn_Wk"), "Wv": f("L0_attn_Wv"),
        "Wo": f("L0_attn_Wo"),
        "Wrq": Wrq_p, "Wrk": Wrk_p, "Wfl": np.ascontiguousarray(Wfl),
        "bfl_f": np.ascontiguousarray(bfl.reshape(32, 1)[0:16]),
        "bfl_l": np.ascontiguousarray(bfl.reshape(32, 1)[16:32]),
        "cdq": cdiag_of("L0_attn_conv_q"), "cdk": cdiag_of("L0_attn_conv_k"),
        "cdv": cdiag_of("L0_attn_conv_v"),
        "gs": np.ascontiguousarray(np.asarray(inputs["L0_attn_graph_scale"], np.float32).reshape(16, 1)),
        "qnw": np.ascontiguousarray(np.asarray(inputs["L0_attn_qn_w"], np.float32).reshape(64, 1)),
        "knw": np.ascontiguousarray(np.asarray(inputs["L0_attn_kn_w"], np.float32).reshape(64, 1)),
        "onw": onw,
        "wgT": np.ascontiguousarray(
            np.asarray(inputs["L0_ffn_w_gate"], np.int32)
            .reshape(24, 128, 8, 128).transpose(0, 3, 2, 1).reshape(3072, 1024)),
        "wuT": np.ascontiguousarray(
            np.asarray(inputs["L0_ffn_w_up"], np.int32)
            .reshape(24, 128, 8, 128).transpose(0, 3, 2, 1).reshape(3072, 1024)),
        "wdT": np.ascontiguousarray(
            np.asarray(inputs["L0_ffn_w_down"], np.int32)
            .reshape(2, 512, 24, 128).transpose(0, 3, 2, 1).reshape(256, 12288)),
        "ident": ident, "maskUi": maskUi, "maskUs": maskUs, "maskLs": maskLs,
        "hv4": hv4, "bonesAll": bonesAll,
        "onesone": np.ones((1, 128), np.float32),
        "selh": np.ascontiguousarray(selh), "hsel": np.ascontiguousarray(hseldt),
    }


def _run(inputs, taps=()):
    tapkey = tuple((t, tuple(s), d) for t, s, d in taps)
    nc, tapnames = _get_nc(tapkey)
    in_maps = [_host_inputs(inputs, b) for b in range(8)]
    res = run_bass_kernel_spmd(nc, in_maps, core_ids=list(range(8)))
    y = np.stack([r["y"] for r in res.results], axis=0)
    tapd = [{t: r["tap_" + t] for t in tapnames} for r in res.results]
    return y, tapd


def kernel(**inputs):
    y, _ = _run(inputs)
    return y


# version 07
# speedup vs baseline: 1.1974x; 1.1974x over previous
import contextlib
import numpy as np
import concourse.bass as bass
import concourse.mybir as mybir
from concourse.bass_utils import run_bass_kernel_spmd

F32 = mybir.dt.float32
BF16 = mybir.dt.bfloat16
I32 = mybir.dt.int32
AF = mybir.ActivationFunctionType
ALU = mybir.AluOpType
AXX = mybir.AxisListType.X

S, D, H, DH, RR, C = 2048, 1024, 16, 64, 48, 64
NST, NDT, NCH, NCP = 16, 8, 32, 16
HID, NHT = 3072, 24
EPS_LN, EPS_RMS = 1e-5, 1e-6
SCALE = 1.0 / 127.0


class SemT:
    def __init__(self, h):
        self.h = h
        self.n = 0

    def inc(self, instr, k=1):
        instr.then_inc(self.h, k)
        self.n += k
        return self.n


def _build(taps=(), stop_phase=99, solve_heads=H, do_chain=True, solve_nsteps=99, p4_parts=99):
    nc = bass.Bass("TRN2", target_bir_lowering=False)
    taps = list(taps)

    x_h = nc.dram_tensor("x", [S, D], F32, kind="ExternalInput").ap()
    w = {}
    for nm, shp, dt in [
        ("Wq", [D, D], F32), ("Wk", [D, D], F32), ("Wv", [D, D], F32), ("Wo", [D, D], F32),
        ("Wrq", [D, D], F32), ("Wrk", [D, D], F32),
        ("Wfl", [D, 32], F32), ("bfl_f", [16, 1], F32), ("bfl_l", [16, 1], F32),
        ("cdq", [128, NDT * 4 * 32], F32), ("cdk", [128, NDT * 4 * 32], F32),
        ("cdv", [128, NDT * 4 * 32], F32),
        ("gs", [16, 1], F32), ("qnw", [64, 1], F32), ("knw", [64, 1], F32),
        ("onw", [128, NDT], F32),
        ("wgT", [HID, D], I32), ("wuT", [HID, D], I32), ("wdT", [256, 12288], I32),
        ("ident", [128, 128], F32),
        ("maskUi", [128, 64], F32), ("maskUs", [128, 64], F32), ("maskLs", [128, 64], F32),
        ("hv4", [128, 128], F32), ("bonesAll", [128, 8 * 32], F32),
        ("onesone", [1, 128], F32),
        ("selh", [16, 128], F32), ("hsel", [16, 8], F32),
    ]:
        w[nm] = nc.dram_tensor(nm, shp, dt, kind="ExternalInput").ap()
    y_h = nc.dram_tensor("y", [S, D], F32, kind="ExternalOutput").ap()
    xnT_scr = nc.dram_tensor("xnT_scr", [128, NDT * S], BF16, kind="Internal").ap()
    tap_h = {t: nc.dram_tensor("tap_" + t, shp, dt, kind="ExternalOutput").ap()
             for t, shp, dt in taps}
    tapnames = [t for t, _, _ in taps]

    st_ = contextlib.ExitStack()
    SB = lambda n, s, d: st_.enter_context(nc.sbuf_tensor(n, s, d))

    AR0 = SB("AR0", [128, NDT * S], BF16)
    AR1 = SB("AR1", [128, NDT * S], BF16)
    AR2 = SB("AR2", [128, NDT * S], BF16)
    AR3 = SB("AR3", [128, NST * D], BF16)
    AR4 = SB("AR4", [128, NST * D], BF16)
    scrB = SB("scrB", [128, 2048], BF16)
    ksumB = SB("ksumB", [128, 2048], BF16)
    sqB = SB("sqB", [128, 2048], BF16)
    preB = sqB

    wbuf = SB("wbuf", [128, NDT * D], BF16)
    wflT = SB("wflT", [128, NDT * 32], BF16)
    bflP = SB("bflP", [48, 1], F32)
    bfP = bflP[0:16, :]
    blP = bflP[32:48, :]
    onesONE = SB("onesONE", [1, 128], BF16)
    onescol = SB("onescol", [128, 1], BF16)
    cdiag = SB("cdiag", [128, 2 * 128], BF16)
    identB = SB("identB", [128, 128], BF16)
    identF = SB("identF", [128, 128], F32)
    maskUi = SB("maskUi_s", [128, 64], F32)
    maskUs = SB("maskUs_s", [128, 64], F32)
    maskLs = SB("maskLs_s", [128, 64], F32)
    hv4B = SB("hv4B", [128, 128], BF16)
    hv4F = SB("hv4F", [128, 128], F32)
    bonesAll = SB("bonesAll_s", [128, 8 * 32], BF16)
    selhB = SB("selhB", [16, 128], BF16)
    hselB = SB("hselB", [16, 8], F32)
    hselL = SB("hselL", [16, 8], BF16)
    qnwB = SB("qnwB", [128, 1], F32)
    knwB = SB("knwB", [128, 1], F32)
    lamB = SB("lamB", [16, 1], F32)
    lamrep = SB("lamrep", [128, NDT], F32)
    onwP = SB("onwP", [128, NDT], F32)

    hm = SB("hm", [128, 2048], F32)
    hstg = SB("hstg", [128, 1024], BF16)
    sm = SB("sm", [128, 3 * 256], BF16)
    gCrep2 = SB("gCrep2", [128, 8 * 32], F32)
    stat1 = SB("stat1", [128, 1], F32)
    stat2 = SB("stat2", [128, 1], F32)
    stat3 = SB("stat3", [128, 1], F32)
    stat4 = SB("stat4", [128, 1], F32)
    MTall = SB("MTall", [128, 8 * 64], BF16)

    scrV = scrB[:].bitcast(F32)
    ksumV = ksumB[:].bitcast(F32)
    sqV = sqB[:].bitcast(F32)
    xin = ksumV
    xin2 = scrV
    xnf = ksumV
    scrF = scrV
    ksumF = ksumV
    wgt = ksumB[:, 0:1024]
    wutl = ksumB[:, 1024:2048]
    wgt2 = scrB[:, 0:1024]
    wutl2 = scrB[:, 1024:2048]
    siluB = preB
    wdTm = AR0[:, 0:NHT * 512]
    wdrow = AR0[:, NHT * 512:NHT * 512 + 3072]
    ybuf = sqV[:, 0:512]
    ybuf2 = sqV[:, 512:1024]
    xfin = ksumV[:, 0:512]
    xfin2 = ksumV[:, 512:1024]
    woB = wbuf
    PcB = AR0[:, 0:1024]
    A1B = AR0[:, 1024:2048]
    A2B = AR0[:, 2048:3072]
    B1B = AR0[:, 3072:4096]
    B2B = AR0[:, 4096:5120]
    crepB = AR0[:, 5120:6144]
    XB = AR0[:, 6144:8192]
    GTp = AR0[:, 8192:10240]
    OlocTp = AR0[:, 10240:12288]
    YKnegp = AR0[:, 12288:14336]
    F0Tp = AR0[:, 14336:16384]

    psA = st_.enter_context(nc.psum_tensor("psA", [128, 2048], F32))
    psB = st_.enter_context(nc.psum_tensor("psB", [128, 1024], F32))
    psT = st_.enter_context(nc.psum_tensor("psT", [128, 2048], BF16))
    RA = psA[:, 0:1024]
    RB = psA[:, 1024:2048]
    RUPD = psB[:, 0:512]
    ROUT = psB[:, 512:1024]

    sD = SemT(st_.enter_context(nc.semaphore("sD")))
    sP = SemT(st_.enter_context(nc.semaphore("sP")))
    sV = SemT(st_.enter_context(nc.semaphore("sV")))
    sAc = SemT(st_.enter_context(nc.semaphore("sAc")))
    sO = SemT(st_.enter_context(nc.semaphore("sO")))
    sX = SemT(st_.enter_context(nc.semaphore("sX")))
    blk = st_.enter_context(nc.Block())
    gps, syn, vec, act, ten = blk.gpsimd, blk.sync, blk.vector, blk.scalar, blk.tensor

    def stub_out():
        def _(v):
            v.wait_ge(sP.h, sP.n)
            v.wait_ge(sAc.h, sAc.n)
            v.wait_ge(sD.h, sD.n)
            v.wait_ge(sX.h, sX.n)
            i = v.memset(sqB[:], 0.0)
            sV.inc(i)
        vec(_)
        sv = sV.n

        def _(s, sv=sv):
            s.wait_ge(sV.h, sv)
            n = 0
            for sti in range(NST):
                for dm in range(2):
                    i = s.dma_start(out=y_h[sti * 128:(sti + 1) * 128, dm * 512:(dm + 1) * 512],
                                    in_=sqB[:].bitcast(F32)[:, 0:512])
                    sO.inc(i, 16)
        syn(_)
        tot = sO.n

        def _(g, tot=tot):
            g.wait_ge(sO.h, tot)
        gps(_)

    free = {"psA": 0, "psT": 0}

    HMI = {"C": 0, "S1": 1, "S2": 2, "S3": 3,
           "LNA": 1, "AG": 1, "G": 1, "N2K": 1, "INVN": 1,
           "L": 2, "B": 2,
           "A": 3, "GC": 3, "MSQ": 3, "AQG": 3, "MSK": 3}

    def hmr(k, sub=slice(0, 16)):
        i = HMI[k]
        base = i * 32
        return hm[base + sub.start: base + sub.stop, :]

    fstage_pe_init = [0]

    SMI = {"C": 0, "INVN": 1, "AG": 2}

    def smr(k):
        return sm[:, SMI[k] * 256:(SMI[k] + 1) * 256]

    def tap(name, src_ap, wait_sem, wait_val):
        if name not in tap_h:
            return
        def _(g, name=name, src_ap=src_ap, wait_sem=wait_sem, wait_val=wait_val):
            g.wait_ge(wait_sem.h, wait_val)
            i = g.dma_start(out=tap_h[name][:], in_=src_ap)
            sO.inc(i, 16)
        gps(_)

    def _(g):
        for dst, src in [(identB, "ident"), (identF, "ident"), (hv4B, "hv4"),
                         (hv4F, "hv4"), (bonesAll, "bonesAll"),
                         (onesONE, "onesone"),
                         (maskUi, "maskUi"), (maskUs, "maskUs"), (maskLs, "maskLs"),
                         (selhB, "selh"), (hselB, "hsel"), (lamB, "gs")]:
            i = g.dma_start(out=dst[:], in_=w[src][:])
            sD.inc(i, 16)
        i = g.dma_start(out=wflT[:].rearrange("p (t c) -> p t c", c=32),
                        in_=w["Wfl"][:].rearrange("(t p) c -> p t c", p=128))
        sD.inc(i, 16)
        i = g.dma_start(out=bflP[0:16, :], in_=w["bfl_f"][:])
        sD.inc(i, 16)
        i = g.dma_start(out=bflP[32:48, :], in_=w["bfl_l"][:])
        sD.inc(i, 16)
        for dst_, src_ in [(qnwB[0:64, :], "qnw"), (qnwB[64:128, :], "qnw"),
                           (knwB[0:64, :], "knw"), (knwB[64:128, :], "knw")]:
            i = g.dma_start(out=dst_, in_=w[src_][:])
            sD.inc(i, 16)
        i = g.dma_start(out=onwP[:], in_=w["onw"][:])
        sD.inc(i, 16)
    gps(_)
    consts_v = sD.n

    def _(v):
        i = v.memset(onescol[:], 1.0)
        sV.inc(i)
    vec(_)
    ones_v = sV.n

    xnT = AR0
    ln_done_v = [0] * NST
    tr_evac_v = [0] * NST
    sq_read_v = [0] * NST
    for sti in range(NST):
        xb = xin if sti % 2 == 0 else xin2

        def _(s, sti=sti, xb=xb):
            if sti >= 2:
                s.wait_ge(sV.h, ln_done_v[sti - 2])
            i = s.dma_start(out=xb, in_=x_h[sti * 128:(sti + 1) * 128, :])
            sX.inc(i, 16)
        syn(_)
        dv = sX.n

        def _(a, dv=dv, xb=xb, sti=sti):
            a.wait_ge(sX.h, dv)
            if sti >= 1:
                a.wait_ge(sV.h, sq_read_v[sti - 1])
            i = a.activation(out=psB[:], in_=xb, func=AF.Square)
            sAc.inc(i)
        act(_)
        av = sAc.n

        def _(v, dv=dv, av=av, xb=xb, sti=sti):
            v.wait_ge(sX.h, dv)
            v.reduce_sum(out=stat1[:], in_=xb, axis=AXX)
            v.wait_ge(sAc.h, av)
            i = v.reduce_sum(out=stat2[:], in_=psB[:], axis=AXX)
            sq_read_v[sti] = sV.inc(i)
            v.tensor_scalar(out=stat1[:], in0=stat1[:], scalar1=1.0 / D, scalar2=None, op0=ALU.mult)
            v.tensor_scalar(out=stat2[:], in0=stat2[:], scalar1=1.0 / D, scalar2=None, op0=ALU.mult)
            v.tensor_tensor(out=stat3[:], in0=stat1[:], in1=stat1[:], op=ALU.mult)
            v.tensor_tensor(out=stat2[:], in0=stat2[:], in1=stat3[:], op=ALU.subtract)
            i = v.tensor_scalar(out=stat2[:], in0=stat2[:], scalar1=EPS_LN, scalar2=None, op0=ALU.add)
            sV.inc(i)
        vec(_)
        vv = sV.n

        def _(a, vv=vv):
            a.wait_ge(sV.h, vv)
            a.activation(out=stat3[:], in_=stat2[:], func=AF.Ln)
            i = a.activation(out=stat4[:], in_=stat3[:], func=AF.Exp, scale=-0.5)
            sAc.inc(i)
        act(_)
        av2 = sAc.n

        def _(v, av2=av2, xb=xb, sti=sti):
            v.wait_ge(sAc.h, av2)
            i = v.tensor_scalar(out=preB[:, 0:1024], in0=xb, scalar1=stat1[:, 0:1],
                                scalar2=stat4[:, 0:1], op0=ALU.subtract, op1=ALU.mult)
            ln_done_v[sti] = sV.inc(i)
        vec(_)
        xn_v = sV.n

        def _(t, xn_v=xn_v, sti=sti):
            t.wait_ge(sV.h, xn_v)
            t.wait_ge(sD.h, consts_v)
            if sti >= 1:
                t.wait_ge(sV.h, tr_evac_v[sti - 1])
            for dtb in range(NDT):
                i = t.transpose(out=psT[:, dtb * 128:(dtb + 1) * 128],
                                in_=preB[:, dtb * 128:(dtb + 1) * 128], identity=identB[:])
            sP.inc(i)
        ten(_)
        pv = sP.n

        def _(v, pv=pv, sti=sti):
            v.wait_ge(sP.h, pv)
            dst = xnT[:].rearrange("p (t s) -> p t s", t=NDT)[:, :, sti * 128:(sti + 1) * 128]
            i = v.tensor_copy(out=dst, in_=psT[:, 0:1024].rearrange("p (t s) -> p t s", s=128))
            tr_evac_v[sti] = sV.inc(i)
        vec(_)
    xnT_v = sV.n
    free["psA"] = xnT_v
    free["psT"] = xnT_v
    tap("xnT", xnT[:], sV, xnT_v)

    if stop_phase <= 1:
        stub_out()
        st_.close()
        return nc, tapnames

    def _(a):
        a.wait_ge(sD.h, consts_v)
        i = a.activation(out=lamB[:], in_=lamB[:], func=AF.Abs)
        sAc.inc(i)
    act(_)
    lam_av = sAc.n

    def _(v, lam_av=lam_av):
        v.wait_ge(sAc.h, lam_av)
        i = v.tensor_scalar(out=hselL[:], in0=hselB[:], scalar1=lamB[:, 0:1], scalar2=None, op0=ALU.mult)
        sV.inc(i)
    vec(_)
    lam_v = sV.n

    fl_evac = [0, 0, 0, 0]
    for sh in range(4):
        def _(t, sh=sh):
            t.wait_ge(sV.h, xnT_v)
            if sh >= 2:
                t.wait_ge(sAc.h, fl_evac[sh - 2])
            base = (sh % 2) * 512
            for kt in range(NDT):
                t.matmul(out=psB[0:16, base:base + 512],
                         lhsT=wflT[:, kt * 32:kt * 32 + 16],
                         rhs=xnT[:, kt * 2048 + sh * 512: kt * 2048 + sh * 512 + 512],
                         start=(kt == 0), stop=(kt == NDT - 1))
                i = t.matmul(out=psB[32:48, base:base + 512],
                             lhsT=wflT[:, kt * 32 + 16:(kt + 1) * 32],
                             rhs=xnT[:, kt * 2048 + sh * 512: kt * 2048 + sh * 512 + 512],
                             start=(kt == 0), stop=(kt == NDT - 1))
            sP.inc(i)
        ten(_)
        pv = sP.n

        def _(a, sh=sh, pv=pv):
            a.wait_ge(sP.h, pv)
            base = (sh % 2) * 512
            a.activation(out=hmr("A")[:, sh * 512:(sh + 1) * 512], in_=psB[0:16, base:base + 512],
                         func=AF.Sigmoid, bias=bfP)
            i = a.activation(out=hmr("B")[:, sh * 512:(sh + 1) * 512], in_=psB[32:48, base:base + 512],
                             func=AF.Sigmoid, bias=blP)
            fl_evac[sh] = sAc.inc(i)
        act(_)
    fl_done_a = fl_evac[3]

    def _(a):
        a.wait_ge(sAc.h, fl_done_a)
        i = a.activation(out=hmr("LNA"), in_=hmr("A"), func=AF.Ln)
        sAc.inc(i)
    act(_)
    lna_av = sAc.n

    def _(a):
        a.wait_ge(sAc.h, lna_av)
        a.wait_ge(sV.h, free["psA"])
        i = a.activation(out=psA[0:16, :], in_=hmr("B"), func=AF.Ln)
        sAc.inc(i)
    act(_)
    lnb_av = sAc.n

    def _(v, lnb_av=lnb_av):
        v.wait_ge(sAc.h, lnb_av)
        i = v.tensor_tensor(out=psA[0:16, :], in0=psA[0:16, :], in1=hmr("LNA"), op=ALU.subtract)
        sV.inc(i)
    vec(_)
    cdiff_v = sV.n

    def _(a, cdiff_v=cdiff_v):
        a.wait_ge(sV.h, cdiff_v)
        i = a.activation(out=hmr("C"), in_=psA[0:16, :], func=AF.Exp)
        sAc.inc(i)
    act(_)
    c_av = sAc.n

    def _(v, c_av=c_av):
        v.wait_ge(sAc.h, c_av)
        v.tensor_copy(out=hmr("S2"), in_=hmr("LNA"))
        cur, other = "S2", "S3"
        for k in [1, 2, 4, 8, 16, 32]:
            c_ = hmr(cur).rearrange("p (c f) -> p c f", f=64)
            o_ = hmr(other).rearrange("p (c f) -> p c f", f=64)
            v.tensor_copy(out=o_[:, :, 0:k], in_=c_[:, :, 0:k])
            v.tensor_tensor(out=o_[:, :, k:64], in0=c_[:, :, k:64], in1=c_[:, :, 0:64 - k], op=ALU.add)
            cur, other = other, cur
        v.tensor_copy(out=psA[32:48, :], in_=hmr("L"))
        i = v.tensor_tensor(out=hmr("S1"), in0=hmr("LNA"), in1=psA[32:48, :], op=ALU.subtract)
        sV.inc(i)
    vec(_)
    hm_vv = sV.n

    def _(a):
        a.wait_ge(sV.h, hm_vv)
        i = a.activation(out=hmr("S1"), in_=hmr("S1"), func=AF.Exp)
        sAc.inc(i)
    act(_)
    ag_av = sAc.n

    def _(v, ag_av=ag_av):
        v.wait_ge(sAc.h, ag_av)
        Lr = hmr("L").rearrange("p (c f) -> p c f", f=64)
        i = v.tensor_copy(out=hmr("S3").rearrange("p (c f) -> p c f", f=64),
                          in_=Lr[:, :, 63:64].broadcast_to([16, 32, 64]))
        sV.inc(i)
    vec(_)
    gcs_vv = sV.n

    def _(a, gcs_vv=gcs_vv):
        a.wait_ge(sV.h, gcs_vv)
        i = a.activation(out=hmr("GC"), in_=hmr("GC"), func=AF.Exp)
        sAc.inc(i)
    act(_)
    gc_av = sAc.n

    def _(t):
        t.wait_ge(sAc.h, gc_av)
        t.wait_ge(sV.h, max(free["psA"], lam_v))
        for chp in range(16):
            t.transpose(out=psA[:, chp * 16:(chp + 1) * 16],
                        in_=hmr("C")[:, chp * 128:(chp + 1) * 128], identity=identF[0:16, 0:16])
        for chp in range(16):
            t.transpose(out=psA[:, 256 + chp * 16: 256 + (chp + 1) * 16],
                        in_=hmr("AG")[:, chp * 128:(chp + 1) * 128], identity=identF[32:48, 32:48])
        i = t.matmul(out=psA[:, 768:776], lhsT=selhB[:], rhs=hselL[:], start=True, stop=True)
        sP.inc(i)
    ten(_)
    smt_pv = sP.n

    def _(g, smt_pv=smt_pv):
        g.wait_ge(sAc.h, gc_av)
        with nc.allow_non_contiguous_dma(reason="tiny gC sample gather"):
            for hp in range(8):
                src_ = hmr("GC", slice(2 * hp, 2 * hp + 2)).rearrange("p (c f) -> p c f", f=64)[:, :, 63:64].squeeze(2)
                i = g.dma_start(out=hstg[0:2, hp * 32:(hp + 1) * 32], in_=src_)
                sD.inc(i, 16)
    gps(_)
    gcs2_v = sD.n

    def _(t, gcs2_v=gcs2_v, smt_pv=smt_pv):
        t.wait_ge(sD.h, gcs2_v)
        t.wait_ge(sP.h, smt_pv)
        for hp in range(8):
            i = t.matmul(out=psA[:, 512 + hp * 32: 512 + (hp + 1) * 32], lhsT=hv4B[0:2, :],
                         rhs=hstg[0:2, hp * 32:(hp + 1) * 32], start=True, stop=True)
        fstage_pe_init[0] = sP.inc(i)
    ten(_)
    sm_pv = sP.n

    def _(v):
        v.wait_ge(sP.h, sm_pv)
        v.tensor_copy(out=smr("C"), in_=psA[:, 0:256])
        v.tensor_copy(out=smr("AG"), in_=psA[:, 256:512])
        v.tensor_copy(out=gCrep2[:], in_=psA[:, 512:768])
        i = v.tensor_copy(out=lamrep[:], in_=psA[:, 768:776])
        free["psA"] = sV.inc(i)
    vec(_)
    sm_vv = sV.n
    tap("hm", hm[:], sV, sm_vv)
    tap("sm", sm[:], sV, sm_vv)
    tap("gCrep2", gCrep2[:], sV, sm_vv)

    wload_v = {}

    def load_weight(name, wait_pe=0):
        def _(g, name=name, wait_pe=wait_pe):
            if wait_pe:
                g.wait_ge(sP.h, wait_pe)
            i = g.dma_start(out=wbuf[:].rearrange("p (t c) -> p t c", c=D),
                            in_=w[name][:].rearrange("(t p) c -> p t c", p=128))
            sD.inc(i, 16)
        gps(_)
        wload_v[name] = sD.n

    cd_used = {}

    def load_cdiag_dt(name, dt):
        def _(g, name=name, dt=dt):
            if (name, dt - 2) in cd_used:
                g.wait_ge(sP.h, cd_used[(name, dt - 2)])
            i = g.dma_start(out=cdiag[:, (dt % 2) * 128:(dt % 2) * 128 + 128],
                            in_=w[name][:, dt * 128:(dt + 1) * 128])
            sD.inc(i, 16)
        gps(_)
        wload_v[(name, dt)] = sD.n

    conv_evac_v = [0, 0]

    def proj_dt(dt, wname, out_sb, conv_name=None):
        def _(t, dt=dt, wname=wname):
            t.wait_ge(sD.h, wload_v[wname])
            t.wait_ge(sV.h, free["psA"])
            for sc in range(4):
                for kt in range(NDT):
                    i = t.matmul(out=psA[:, sc * 512:(sc + 1) * 512],
                                 lhsT=wbuf[:, kt * 1024 + dt * 128: kt * 1024 + (dt + 1) * 128],
                                 rhs=xnT[:, kt * 2048 + sc * 512: kt * 2048 + sc * 512 + 512],
                                 start=(kt == 0), stop=(kt == NDT - 1))
            sP.inc(i)
        ten(_)
        mm_pv = sP.n

        if conv_name is None:
            def _(v, dt=dt, mm_pv=mm_pv):
                v.wait_ge(sP.h, mm_pv)
                v.wait_ge(sV.h, sm_vv)
                i = v.tensor_scalar(out=out_sb[:, dt * 2048:(dt + 1) * 2048],
                                    in0=psA[:], scalar1=lamrep[:, dt:dt + 1], scalar2=None, op0=ALU.mult)
                free["psA"] = sV.inc(i)
            vec(_)
            return sV.n

        def _(v, mm_pv=mm_pv):
            v.wait_ge(sP.h, mm_pv)
            i = v.tensor_copy(out=preB[:], in_=psA[:])
            free["psA"] = sV.inc(i)
        vec(_)
        pre_vv = sV.n

        for half in range(2):
            def _(t, half=half, dt=dt, pre_vv=pre_vv):
                t.wait_ge(sV.h, pre_vv)
                t.wait_ge(sD.h, wload_v[(conv_name, dt)])
                t.wait_ge(sAc.h, fl_done_a)
                if conv_evac_v[1] or conv_evac_v[0]:
                    t.wait_ge(sV.h, max(conv_evac_v))
                for sc in range(2):
                    ofs = half * 1024 + sc * 512
                    for sub in range(4):
                        for j in [3, 2, 1, 0]:
                            delta = 3 - j
                            lo = max(0, delta - ofs)
                            i = t.matmul(
                                out=psB[32 * sub:32 * sub + 32, sc * 512 + lo:(sc + 1) * 512],
                                lhsT=cdiag[32 * sub:32 * sub + 32,
                                           (dt % 2) * 128 + j * 32: (dt % 2) * 128 + j * 32 + 32],
                                rhs=preB[32 * sub:32 * sub + 32,
                                         ofs + lo - delta: ofs + 512 - delta],
                                start=(j == 3), stop=(j == 0),
                                tile_position=(32 * sub, 32 * sub))
                cd_used[(conv_name, dt)] = sP.inc(i)
            ten(_)
            h_pv = sP.n

            def _(v, half=half, dt=dt, h_pv=h_pv):
                v.wait_ge(sP.h, h_pv)
                i = v.tensor_copy(out=out_sb[:, dt * 2048 + half * 1024: dt * 2048 + (half + 1) * 1024],
                                  in_=psB[:])
                conv_evac_v[half] = sV.inc(i)
            vec(_)
        return conv_evac_v[1]

    def stats_all(src_sb, slot, ready_v):
        sq_pv = [0]
        for dt in range(NDT):
            def _(v, dt=dt, ready_v=ready_v):
                v.wait_ge(sV.h, ready_v)
                if sq_pv[0]:
                    v.wait_ge(sP.h, sq_pv[0])
                elif dt == 0:
                    v.wait_ge(sV.h, free["psA"])
                i = v.tensor_tensor(out=sqB[:], in0=src_sb[:, dt * 2048:(dt + 1) * 2048],
                                    in1=src_sb[:, dt * 2048:(dt + 1) * 2048], op=ALU.mult)
                sV.inc(i)
            vec(_)
            sq_vv = sV.n

            def _(t, sq_vv=sq_vv, dt=dt):
                t.wait_ge(sV.h, sq_vv)
                if dt == 0:
                    t.wait_ge(sV.h, free["psA"])
                for sc in range(4):
                    i = t.matmul(out=psA[0:32, sc * 512:(sc + 1) * 512],
                                 lhsT=bonesAll[:, dt * 32:(dt + 1) * 32],
                                 rhs=sqB[:, sc * 512:(sc + 1) * 512],
                                 start=(dt == 0), stop=(dt == NDT - 1))
                sq_pv[0] = sP.inc(i)
            ten(_)
        st_pv = sP.n

        def _(v, slot=slot, st_pv=st_pv):
            v.wait_ge(sP.h, st_pv)
            i = v.tensor_copy(out=hmr(slot), in_=psA[0:16, :])
            free["psA"] = sV.inc(i)
        vec(_)
        return sV.n

    def rsqrt_pipe(src_slot, dst_slot, scale, eps, v_ready):
        def _(v, v_ready=v_ready):
            v.wait_ge(sV.h, max(v_ready, free["psA"]))
            i = v.tensor_scalar(out=psA[0:16, :], in0=hmr(src_slot), scalar1=scale, scalar2=eps,
                                op0=ALU.mult, op1=ALU.add)
            sV.inc(i)
        vec(_)
        t1v = sV.n

        def _(a, t1v=t1v):
            a.wait_ge(sV.h, t1v)
            a.activation(out=psA[32:48, :], in_=psA[0:16, :], func=AF.Ln)
            i = a.activation(out=hmr(dst_slot), in_=psA[32:48, :], func=AF.Exp, scale=-0.5)
            sAc.inc(i)
        act(_)
        ra = sAc.n

        def _(v, ra=ra):
            v.wait_ge(sAc.h, ra)
            i = v.nop()
            free["psA"] = sV.inc(i)
        vec(_)
        return sV.n

    fstage_pe = fstage_pe_init

    def field_mm(hm_slot, dt, ready_v):
        def _(g, dt=dt, ready_v=ready_v, hm_slot=hm_slot):
            g.wait_ge(sV.h, ready_v)
            if fstage_pe[0]:
                g.wait_ge(sP.h, fstage_pe[0])
            i = g.dma_start(out=hstg[0:2, :], in_=hmr(hm_slot, slice(2 * dt, 2 * dt + 2))[:, 0:1024])
            sD.inc(i, 16)
            i = g.dma_start(out=hstg[32:34, :], in_=hmr(hm_slot, slice(2 * dt, 2 * dt + 2))[:, 1024:2048])
            sD.inc(i, 16)
        gps(_)
        stg_v = sD.n

        def _(t, stg_v=stg_v):
            t.wait_ge(sD.h, stg_v)
            t.wait_ge(sV.h, free["psA"])
            for sc in range(4):
                b = (sc // 2) * 32
                rhs = hstg[b:b + 2, (sc % 2) * 512:(sc % 2) * 512 + 512]
                i = t.matmul(out=psA[:, sc * 512:(sc + 1) * 512],
                             lhsT=hv4B[b:b + 2, :], rhs=rhs, start=True, stop=True,
                             tile_position=(b, 0))
            fstage_pe[0] = sP.inc(i)
        ten(_)
        return sP.n

    if stop_phase <= 2:
        stub_out()
        st_.close()
        return nc, tapnames

    load_weight("Wrq")
    rq_v = 0
    for dt in range(NDT):
        rq_v = proj_dt(dt, "Wrq", AR4, None)

    if stop_phase <= 3:
        stub_out()
        st_.close()
        return nc, tapnames

    load_weight("Wq", wait_pe=sP.n)
    qcv = 0
    for dt in range(NDT):
        load_cdiag_dt("cdq", dt)
        qcv = proj_dt(dt, "Wq", AR2, "cdq")
    q_stat_v = stats_all(AR2, "MSQ", qcv)

    aq_v = rsqrt_pipe("MSQ", "MSQ", 1.0 / DH, EPS_RMS, q_stat_v)

    def _(a, aq_v=aq_v):
        a.wait_ge(sV.h, aq_v)
        i = a.activation(out=hmr("G"), in_=hmr("L"), func=AF.Exp)
        sAc.inc(i)
    act(_)
    g_av = sAc.n

    def _(v, g_av=g_av):
        v.wait_ge(sAc.h, g_av)
        v.wait_ge(sV.h, free["psA"])
        v.tensor_copy(out=psA[32:48, :], in_=hmr("G"))
        i = v.tensor_tensor(out=hmr("AQG"), in0=hmr("MSQ"), in1=psA[32:48, :], op=ALU.mult)
        free["psA"] = sV.inc(i)
    vec(_)
    aqg_v = sV.n

    for dt in range(NDT):
        f1 = field_mm("AQG", dt, aqg_v)

        def _(v, dt=dt, f1=f1):
            v.wait_ge(sP.h, f1)
            v.tensor_scalar(out=scrF[:, 0:1024], in0=AR2[:, dt * 2048:dt * 2048 + 1024],
                            scalar1=qnwB[:, 0:1], scalar2=None, op0=ALU.mult)
            v.tensor_scalar(out=xnf, in0=AR2[:, dt * 2048 + 1024:(dt + 1) * 2048],
                            scalar1=qnwB[:, 0:1], scalar2=None, op0=ALU.mult)
            v.tensor_tensor(out=scrF[:, 0:1024], in0=scrF[:, 0:1024], in1=psA[:, 0:1024], op=ALU.mult)
            i = v.tensor_tensor(out=xnf, in0=xnf, in1=psA[:, 1024:2048], op=ALU.mult)
            free["psA"] = sV.inc(i)
        vec(_)
        f2 = field_mm("G", dt, aqg_v)

        def _(v, dt=dt, f2=f2, rq_v=rq_v):
            v.wait_ge(sP.h, f2)
            v.wait_ge(sV.h, rq_v)
            v.tensor_tensor(out=sqB[:], in0=AR4[:, dt * 2048:(dt + 1) * 2048],
                            in1=psA[:], op=ALU.mult)
            v.tensor_tensor(out=AR1[:, dt * 2048:dt * 2048 + 1024],
                            in0=scrF[:, 0:1024], in1=sqB[:].bitcast(F32)[:, 0:1024] if False else sqB[:, 0:1024],
                            op=ALU.add)
            i = v.tensor_tensor(out=AR1[:, dt * 2048 + 1024:(dt + 1) * 2048],
                                in0=xnf, in1=sqB[:, 1024:2048], op=ALU.add)
            free["psA"] = sV.inc(i)
        vec(_)
    QD_v = sV.n
    tap("QD", AR1[:], sV, QD_v)

    if stop_phase <= 4:
        stub_out()
        st_.close()
        return nc, tapnames

    load_weight("Wv", wait_pe=sP.n)
    for dt in range(NDT):
        load_cdiag_dt("cdv", dt)
        cv = proj_dt(dt, "Wv", AR2, "cdv")

        def _(t, dt=dt, cv=cv):
            t.wait_ge(sV.h, cv)
            if free["psT"]:
                t.wait_ge(sV.h, free["psT"])
            for sb in range(16):
                i = t.transpose(out=psT[:, sb * 128:(sb + 1) * 128],
                                in_=AR2[:, dt * 2048 + sb * 128: dt * 2048 + (sb + 1) * 128],
                                identity=identB[:])
            sP.inc(i)
        ten(_)
        tr_pv = sP.n

        def _(v, dt=dt, tr_pv=tr_pv):
            v.wait_ge(sP.h, tr_pv)
            v.wait_ge(sV.h, sm_vv)
            dst = AR3[:].rearrange("p (t c) -> p t c", c=1024)[:, :, dt * 128:(dt + 1) * 128] \
                .rearrange("p t (h f) -> p t h f", f=64)
            src = psT[:].rearrange("p (t c) -> p t c", c=128).rearrange("p t (h f) -> p t h f", f=64)
            in1 = smr("AG").rearrange("p (t h) -> p t h", h=16)[:, :, 2 * dt:2 * dt + 2] \
                .unsqueeze(3).broadcast_to([128, 16, 2, 64])
            i = v.tensor_tensor(out=dst, in0=src, in1=in1, op=ALU.mult)
            free["psT"] = sV.inc(i)
        vec(_)
    VagT_v = sV.n
    tap("VagT", AR3[:], sV, VagT_v)

    if stop_phase <= 5:
        stub_out()
        st_.close()
        return nc, tapnames

    load_weight("Wrk", wait_pe=sP.n)
    rk_v = 0
    for dt in range(NDT):
        rk_v = proj_dt(dt, "Wrk", AR4, None)

    if stop_phase <= 6:
        stub_out()
        st_.close()
        return nc, tapnames

    load_weight("Wk", wait_pe=sP.n)
    kcv = 0
    for dt in range(NDT):
        load_cdiag_dt("cdk", dt)
        kcv = proj_dt(dt, "Wk", AR2, "cdk")
    k_stat_v = stats_all(AR2, "MSK", kcv)

    def _(s):
        s.wait_ge(sP.h, sP.n)
        i = s.dma_start(out=xnT_scr[:], in_=AR0[:])
        sX.inc(i, 16)
    syn(_)
    xnT_dump_v = sX.n

    ak_v = rsqrt_pipe("MSK", "MSK", 1.0 / DH, EPS_RMS, k_stat_v)
    akb_v = ak_v

    ksum_done = 0
    for dt in range(NDT):
        f1 = field_mm("MSK", dt, akb_v)

        def _(v, dt=dt, f1=f1, rk_v=rk_v):
            v.wait_ge(sP.h, f1)
            v.wait_ge(sV.h, rk_v)
            if dt == 0:
                v.wait_ge(sX.h, xnT_dump_v)
            for hf in range(2):
                o = dt * 2048 + hf * 1024
                v.tensor_scalar(out=ksumF[:], in0=AR2[:, o:o + 1024],
                                scalar1=knwB[:, 0:1], scalar2=None, op0=ALU.mult)
                v.tensor_tensor(out=ksumF[:], in0=ksumF[:],
                                in1=psA[:, hf * 1024:(hf + 1) * 1024], op=ALU.mult)
                v.tensor_tensor(out=ksumF[:], in0=ksumF[:], in1=AR4[:, o:o + 1024], op=ALU.add)
                i = v.tensor_copy(out=AR0[:, o:o + 1024], in_=ksumF[:])
            free["psA"] = sV.inc(i)
        vec(_)
        ksum_done = sV.n
    ksum_v = ksum_done
    n2_v = stats_all(AR0, "N2K", ksum_done)

    invn_v = rsqrt_pipe("N2K", "N2K", 1.0, 1e-24, n2_v)
    invnb_v = invn_v

    def _(t, invn_v=invn_v):
        t.wait_ge(sV.h, max(invn_v, free["psA"]))
        for chp in range(16):
            i = t.transpose(out=psA[:, chp * 16:(chp + 1) * 16],
                            in_=hmr("N2K")[:, chp * 128:(chp + 1) * 128], identity=identF[32:48, 32:48])
        sP.inc(i)
    ten(_)
    ivs_pv = sP.n

    def _(v, ivs_pv=ivs_pv):
        v.wait_ge(sP.h, ivs_pv)
        i = v.tensor_copy(out=smr("INVN"), in_=psA[:, 0:256])
        free["psA"] = sV.inc(i)
    vec(_)
    invnsm_v = sV.n

    for dt in range(NDT):
        f1 = field_mm("N2K", dt, invnb_v)

        def _(v, dt=dt, f1=f1):
            v.wait_ge(sP.h, f1)
            i = v.tensor_tensor(out=AR2[:, dt * 2048:(dt + 1) * 2048],
                                in0=AR0[:, dt * 2048:(dt + 1) * 2048], in1=psA[:], op=ALU.mult)
            free["psA"] = sV.inc(i)
        vec(_)

        def _(t, dt=dt):
            t.wait_ge(sV.h, max(invnsm_v, ksum_v, free["psT"]))
            for sb in range(16):
                i = t.transpose(out=psT[:, sb * 128:(sb + 1) * 128],
                                in_=AR0[:, dt * 2048 + sb * 128: dt * 2048 + (sb + 1) * 128],
                                identity=identB[:])
            sP.inc(i)
        ten(_)
        kt_pv = sP.n

        def _(v, dt=dt, kt_pv=kt_pv):
            v.wait_ge(sP.h, kt_pv)
            dst = AR4[:].rearrange("p (t c) -> p t c", c=1024)[:, :, dt * 128:(dt + 1) * 128] \
                .rearrange("p t (h f) -> p t h f", f=64)
            src = psT[:].rearrange("p (t c) -> p t c", c=128).rearrange("p t (h f) -> p t h f", f=64)
            in1 = smr("INVN").rearrange("p (t h) -> p t h", h=16)[:, :, 2 * dt:2 * dt + 2] \
                .unsqueeze(3).broadcast_to([128, 16, 2, 64])
            i = v.tensor_tensor(out=dst, in0=src, in1=in1, op=ALU.mult)
            free["psT"] = sV.inc(i)
        vec(_)
    KD_v = KT_v = sV.n
    tap("KD", AR2[:], sV, KD_v)
    tap("KT", AR4[:], sV, KT_v)

    if stop_phase <= 7:
        stub_out()
        st_.close()
        return nc, tapnames

    KD_, QD_, KT_, VagT_ = AR2, AR1, AR4, AR3
    consts_ready = max(KD_v, KT_v, QD_v, VagT_v, sm_vv, invnsm_v, hm_vv)

    def _(v):
        i = v.memset(MTall[:], 0.0)
        sV.inc(i)
    vec(_)
    mt0_v = sV.n

    last_evac = {"RA": free["psA"], "RB": free["psA"]}
    last_evacA = {"RA": 0, "RB": 0}
    freeA = {"psT": 0}
    crep_pe = [0]
    head_state = {}
    chain_upd_v = {}
    chain_out_evac = {}
    last_upd_a = [0]

    PERS = [
        {"GT": GTp, "OlocT": OlocTp, "YKneg": YKnegp, "F0T": F0Tp},
        {"GT": wbuf[:, 0:2048], "OlocT": wbuf[:, 2048:4096],
         "YKneg": wbuf[:, 4096:6144], "F0T": wbuf[:, 6144:8192]},
    ]

    def pers(h):
        return PERS[(h // 2) % 2]

    def kd_sl(h, ch):
        return KD_[(h % 2) * 64:(h % 2) * 64 + 64,
                   (h // 2) * 2048 + ch * 64:(h // 2) * 2048 + (ch + 1) * 64]

    def qd_sl(h, ch):
        return QD_[(h % 2) * 64:(h % 2) * 64 + 64,
                   (h // 2) * 2048 + ch * 64:(h // 2) * 2048 + (ch + 1) * 64]

    def kt_sl(h, ch):
        return KT_[(ch % 2) * 64:(ch % 2) * 64 + 64,
                   (ch // 2) * 1024 + h * 64:(ch // 2) * 1024 + (h + 1) * 64]

    def x_sl(ch, j0, j1):
        return XB[(ch % 2) * 64:(ch % 2) * 64 + 64,
                  (ch // 2) * 128 + j0:(ch // 2) * 128 + j1]

    def pw_sl(buf, ch):
        return buf[(ch % 2) * 64:(ch % 2) * 64 + 64, (ch // 2) * 64:(ch // 2) * 64 + 64]

    maskUi_b = maskUi[:].unsqueeze(1).broadcast_to([128, 16, 64])
    maskUs_b = maskUs[:].unsqueeze(1).broadcast_to([128, 16, 64])
    maskLs_b = maskLs[:].unsqueeze(1).broadcast_to([128, 16, 64])

    def solve_steps(h):
        crep_av_h = [0]

        def s_crep():
            def _(g, h=h):
                g.wait_ge(sV.h, consts_ready)
                if crep_pe[0]:
                    g.wait_ge(sP.h, crep_pe[0])
                crow = hmr("C", slice(h, h + 1)).rearrange("p (c f) -> p c f", f=128)
                i = g.dma_start(out=hstg[64:65, :].rearrange("p (c f) -> p c f", f=64),
                                in_=crow[:, :, 0:64])
                sD.inc(i, 16)
                i = g.dma_start(out=hstg[65:66, :].rearrange("p (c f) -> p c f", f=64),
                                in_=crow[:, :, 64:128])
                sD.inc(i, 16)
            gps(_)
            stg_v = sD.n

            def _(t, stg_v=stg_v):
                t.wait_ge(sD.h, stg_v)
                t.wait_ge(sV.h, last_evac["RA"])
                if last_evacA["RA"]:
                    t.wait_ge(sAc.h, last_evacA["RA"])
                for cc_ in range(2):
                    i = t.matmul(out=RA[:, cc_ * 512:(cc_ + 1) * 512],
                                 lhsT=hv4B[64:66, :], rhs=hstg[64:66, cc_ * 512:(cc_ + 1) * 512],
                                 start=True, stop=True, tile_position=(64, 0))
                crep_pe[0] = sP.inc(i)
            ten(_)
            pv = sP.n

            def _(a, pv=pv):
                a.wait_ge(sP.h, pv)
                i = a.activation(out=crepB[:], in_=RA, func=AF.Copy)
                last_evacA["RA"] = sAc.inc(i)
            act(_)
            crep_av_h[0] = sAc.n
        yield s_crep

        def s_p1():
            def _(t, h=h):
                t.wait_ge(sV.h, max(consts_ready, last_evac["RA"], last_evac["RB"]))
                if last_evacA["RA"] or last_evacA["RB"]:
                    t.wait_ge(sAc.h, max(last_evacA["RA"], last_evacA["RB"]))
                for ch in range(NCH):
                    tp = ((h % 2) * 64, (ch % 2) * 64)
                    t.matmul(out=pw_sl(RA, ch), lhsT=kd_sl(h, ch), rhs=qd_sl(h, ch),
                             start=True, stop=True, tile_position=tp)
                    i = t.matmul(out=pw_sl(RB, ch), lhsT=kd_sl(h, ch), rhs=kd_sl(h, ch),
                                 start=True, stop=True, tile_position=tp)
                sP.inc(i)
            ten(_)
            pv = sP.n

            def _(v, pv=pv, h=h):
                v.wait_ge(sP.h, pv)
                if crep_av_h[0]:
                    v.wait_ge(sAc.h, crep_av_h[0])
                c_sm_b = smr("C").rearrange("p (t j) -> p t j", j=16)[:, :, h:h + 1].broadcast_to([128, 16, 64])
                RAr = RA.rearrange("p (t f) -> p t f", f=64)
                RBr = RB.rearrange("p (t f) -> p t f", f=64)
                v.tensor_tensor(out=PcB[:].rearrange("p (t f) -> p t f", f=64), in0=RAr, in1=maskUi_b, op=ALU.mult)
                v.tensor_tensor(out=A1B[:].rearrange("p (t f) -> p t f", f=64), in0=RBr, in1=maskUs_b, op=ALU.mult)
                v.tensor_tensor(out=A1B[:].rearrange("p (t f) -> p t f", f=64),
                                in0=A1B[:].rearrange("p (t f) -> p t f", f=64), in1=c_sm_b, op=ALU.mult)
                v.tensor_tensor(out=B1B[:].rearrange("p (t f) -> p t f", f=64), in0=RBr, in1=maskLs_b, op=ALU.mult)
                i = v.tensor_tensor(out=B1B[:], in0=B1B[:], in1=crepB[:], op=ALU.mult)
                last_evac["RA"] = last_evac["RB"] = sV.inc(i)
            vec(_)

            def _(v, h=h):
                dstv = XB[:].rearrange("p (t f) -> p t f", f=128)[:, :, 0:64]
                dstk = XB[:].rearrange("p (t f) -> p t f", f=128)[:, :, 64:128]
                srcv = VagT_[:].rearrange("p (t f) -> p t f", f=1024)[:, :, h * 64:(h + 1) * 64]
                srck = KT_[:].rearrange("p (t f) -> p t f", f=1024)[:, :, h * 64:(h + 1) * 64]
                v.tensor_copy(out=dstv, in_=srcv)
                i = v.tensor_copy(out=dstk, in_=srck)
                sV.inc(i)
            vec(_)
        yield s_p1

        state = {"x_v": None, "a_ready": None, "a_ready_a": 0,
                 "bufs": (A1B, B1B, A2B, B2B)}

        def mk_level(lv):
            def s_level():
                Acur, Bcur, Anxt, Bnxt = state["bufs"]
                x_v = state["x_v"] if state["x_v"] is not None else sV.n
                a_ready = state["a_ready"] if state["a_ready"] is not None else sV.n
                a_ready_a = state["a_ready_a"]

                def _(t, Acur=Acur, x_v=x_v, a_ready=a_ready, a_ready_a=a_ready_a):
                    t.wait_ge(sV.h, max(a_ready, x_v, last_evac["RA"], last_evac["RB"]))
                    wa = max(a_ready_a, last_evacA["RA"], last_evacA["RB"])
                    if wa:
                        t.wait_ge(sAc.h, wa)
                    for ch in range(NCH):
                        reg = RA if ch < 16 else RB
                        cc = ch % 16
                        out = reg[(ch % 2) * 64:(ch % 2) * 64 + 64, (cc // 2) * 128:(cc // 2) * 128 + 128]
                        i = t.matmul(out=out, lhsT=pw_sl(Acur, ch), rhs=x_sl(ch, 0, 128),
                                     start=True, stop=True, tile_position=((ch % 2) * 64, (ch % 2) * 64))
                    sP.inc(i)
                ten(_)
                pv = sP.n
                op = ALU.subtract if lv == 0 else ALU.add

                def _(v, pv=pv, op=op):
                    v.wait_ge(sP.h, pv)
                    v.tensor_tensor(out=XB[:, 0:1024], in0=XB[:, 0:1024], in1=RA, op=op)
                    i = v.tensor_tensor(out=XB[:, 1024:2048], in0=XB[:, 1024:2048], in1=RB, op=op)
                    state["x_v"] = last_evac["RA"] = last_evac["RB"] = sV.inc(i)
                vec(_)

                if lv < 5:
                    def _(t, Acur=Acur, Bcur=Bcur, a_ready_a=a_ready_a):
                        t.wait_ge(sV.h, state["x_v"])
                        if a_ready_a:
                            t.wait_ge(sAc.h, a_ready_a)
                        for ch in range(NCH):
                            tp = ((ch % 2) * 64, (ch % 2) * 64)
                            t.matmul(out=pw_sl(RA, ch), lhsT=pw_sl(Acur, ch), rhs=pw_sl(Bcur, ch),
                                     start=True, stop=True, tile_position=tp)
                            i = t.matmul(out=pw_sl(RB, ch), lhsT=pw_sl(Bcur, ch), rhs=pw_sl(Acur, ch),
                                         start=True, stop=True, tile_position=tp)
                        sP.inc(i)
                    ten(_)
                    pv2 = sP.n

                    def _(a, pv2=pv2, Anxt=Anxt, Bnxt=Bnxt):
                        a.wait_ge(sP.h, pv2)
                        a.activation(out=Bnxt[:], in_=RA, func=AF.Copy)
                        i = a.activation(out=Anxt[:], in_=RB, func=AF.Copy)
                        state["a_ready_a"] = last_evacA["RA"] = last_evacA["RB"] = sAc.inc(i)
                    act(_)
                    state["bufs"] = (Anxt, Bnxt, Acur, Bcur)
            return s_level

        for lv in range(6):
            yield mk_level(lv)

        def s_p4():
            if p4_parts < 1:
                return
            def _(v, h=h):
                v.wait_ge(sV.h, state["x_v"])
                c_b = smr("C").rearrange("p (t j) -> p t j", j=16)[:, :, h:h + 1].broadcast_to([128, 16, 128])
                i = v.tensor_tensor(out=XB[:].rearrange("p (t f) -> p t f", f=128),
                                    in0=XB[:].rearrange("p (t f) -> p t f", f=128), in1=c_b, op=ALU.mult)
                sV.inc(i)
            vec(_)
            xc_v = sV.n
            if p4_parts < 2:
                return

            def _(t, xc_v=xc_v, h=h):
                t.wait_ge(sV.h, max(xc_v, last_evac["RA"], last_evac["RB"]))
                if last_evacA["RA"] or last_evacA["RB"]:
                    t.wait_ge(sAc.h, max(last_evacA["RA"], last_evacA["RB"]))
                for ch in range(NCH):
                    tp = ((ch % 2) * 64, (ch % 2) * 64)
                    t.matmul(out=pw_sl(RA, ch), lhsT=x_sl(ch, 0, 64), rhs=pw_sl(PcB, ch),
                             start=True, stop=True, tile_position=tp)
                    i = t.matmul(out=pw_sl(RB, ch), lhsT=x_sl(ch, 64, 128), rhs=pw_sl(PcB, ch),
                                 start=True, stop=True, tile_position=tp)
                sP.inc(i)
            ten(_)
            pv = sP.n

            psTF = psT[:].bitcast(F32)

            def _(t, h=h):
                t.wait_ge(sV.h, free["psT"])
                if freeA["psT"]:
                    t.wait_ge(sAc.h, freeA["psT"])
                for ch in range(NCH):
                    tp = ((ch % 2) * 64, (ch % 2) * 64)
                    i = t.matmul(out=pw_sl(psTF, ch), lhsT=x_sl(ch, 64, 128), rhs=kt_sl(h, ch),
                                 start=True, stop=True, tile_position=tp)
                sP.inc(i)
            ten(_)
            pvYK = sP.n
            if p4_parts < 3:
                return

            def _(a, pvYK=pvYK):
                a.wait_ge(sP.h, pvYK)
                i = a.activation(out=hstg[:], in_=psTF, func=AF.Copy, scale=-1.0)
                sAc.inc(i)
            act(_)
            yk_a = sAc.n

            def _(g, yk_a=yk_a, h=h):
                g.wait_ge(sAc.h, yk_a)
                hh = (h % 2) * 64
                for par in range(2):
                    src_y = hstg[par * 64:par * 64 + 64, :].rearrange("p (c f) -> p c f", f=64)
                    dst_y = pers(h)["YKneg"][hh:hh + 64, :].rearrange("p (c f) -> p c f", f=64)[:, par::2, :]
                    i = g.dma_start(out=dst_y, in_=src_y)
                    sD.inc(i, 16)
            gps(_)
            rpy_v = sD.n

            def _(t, yk_a=yk_a, h=h):
                t.wait_ge(sAc.h, yk_a)
                for ch in range(NCH):
                    tp = ((ch % 2) * 64, (ch % 2) * 64)
                    i = t.matmul(out=pw_sl(psTF, ch), lhsT=kt_sl(h, ch), rhs=x_sl(ch, 0, 64),
                                 start=True, stop=True, tile_position=tp)
                sP.inc(i)
            ten(_)
            pvF0 = sP.n

            def _(a, pvF0=pvF0, rpy_v=rpy_v):
                a.wait_ge(sP.h, pvF0)
                a.wait_ge(sD.h, rpy_v)
                i = a.activation(out=hstg[:], in_=psTF, func=AF.Copy)
                freeA["psT"] = sAc.inc(i)
            act(_)
            f0_a = sAc.n

            def _(g, f0_a=f0_a, h=h):
                g.wait_ge(sAc.h, f0_a)
                hh = (h % 2) * 64
                for par in range(2):
                    src_f = hstg[par * 64:par * 64 + 64, :].rearrange("p (c f) -> p c f", f=64)
                    dst_f = pers(h)["F0T"][hh:hh + 64, :].rearrange("p (c f) -> p c f", f=64)[:, par::2, :]
                    i = g.dma_start(out=dst_f, in_=src_f)
                    sD.inc(i, 16)
            gps(_)
            rpf_v = sD.n

            def _(a, pvF0=pvF0):
                a.wait_ge(sP.h, pvF0)
                a.activation(out=XB[:, 0:1024], in_=RA, func=AF.Copy)
                i = a.activation(out=XB[:, 1024:2048], in_=RB, func=AF.Copy)
                last_evacA["RA"] = last_evacA["RB"] = sAc.inc(i)
            act(_)
            oz_a = sAc.n

            def _(g, oz_a=oz_a, h=h):
                g.wait_ge(sAc.h, oz_a)
                hh = (h % 2) * 64
                for par in range(2):
                    src_o = XB[par * 64:par * 64 + 64, 0:1024].rearrange("p (c f) -> p c f", f=64)
                    dst_o = pers(h)["OlocT"][hh:hh + 64, :].rearrange("p (c f) -> p c f", f=64)[:, par::2, :]
                    i = g.dma_start(out=dst_o, in_=src_o)
                    sD.inc(i, 16)
                    src_z = XB[par * 64:par * 64 + 64, 1024:2048].rearrange("p (c f) -> p c f", f=64)
                    dst_z = pers(h)["GT"][hh:hh + 64, :].rearrange("p (c f) -> p c f", f=64)[:, par::2, :]
                    i = g.dma_start(out=dst_z, in_=src_z)
                    sD.inc(i, 16)
            gps(_)
            rp1_v = sD.n

            def _(v, rp1_v=rp1_v, h=h):
                v.wait_ge(sD.h, rp1_v)
                hh = (h % 2) * 64
                qsl = QD_[hh:hh + 64, (h // 2) * 2048:(h // 2) * 2048 + 2048]
                i = v.tensor_tensor(out=pers(h)["GT"][hh:hh + 64, :], in0=qsl,
                                    in1=pers(h)["GT"][hh:hh + 64, :], op=ALU.subtract)
                sV.inc(i)
            vec(_)
            gt_v = sV.n
            if p4_parts < 4:
                return

            def _(v, rp1_v=rp1_v, h=h):
                v.wait_ge(sD.h, rp1_v)
                i = v.nop()
                last_evac["RA"] = max(last_evac["RA"], sV.inc(i))
                head_state[h] = sV.n
            vec(_)
        yield s_p4

    def chain_step(h, ch):
        hh = (h % 2) * 64
        hp = h // 2
        P = pers(h)
        mt = MTall[hh:hh + 64, hp * 64:(hp + 1) * 64]
        slot_o = ROUT[hh:hh + 64, (ch % 8) * 64:(ch % 8) * 64 + 64]
        slot_u = RUPD[hh:hh + 64, (ch % 2) * 64:(ch % 2) * 64 + 64]

        def _(t, h=h, ch=ch, P=P):
            if ch == 0:
                t.wait_ge(sV.h, max(head_state[h], mt0_v))
                if h >= 2:
                    t.wait_ge(sV.h, chain_out_evac[(h - 2, 3)])
                    ev, val = chain_upd_v[(h - 2, NCH - 1)]
                    t.wait_ge(ev.h, val)
            else:
                ev, val = chain_upd_v[(h, ch - 1)]
                t.wait_ge(ev.h, val)
            if ch >= 8:
                t.wait_ge(sV.h, chain_out_evac[(h, ch // 8 - 1)])
            t.matmul(out=slot_o, lhsT=mt, rhs=P["GT"][hh:hh + 64, ch * 64:(ch + 1) * 64],
                     start=True, stop=True, tile_position=(hh, hh))
            t.matmul(out=slot_u, lhsT=P["YKneg"][hh:hh + 64, ch * 64:(ch + 1) * 64],
                     rhs=mt, start=True, stop=False, tile_position=(hh, hh))
            t.matmul(out=slot_u, lhsT=identB[hh:hh + 64, hh:hh + 64],
                     rhs=mt, start=False, stop=False, tile_position=(hh, hh))
            i = t.matmul(out=slot_u, lhsT=identB[hh:hh + 64, hh:hh + 64],
                         rhs=P["F0T"][hh:hh + 64, ch * 64:(ch + 1) * 64],
                         start=False, stop=True, tile_position=(hh, hh))
            sP.inc(i)
        ten(_)
        pv = sP.n
        gc = gCrep2[hh:hh + 64, hp * 32 + ch: hp * 32 + ch + 1]

        if h % 2 == 1:
            def _(a, pv=pv, gc=gc):
                a.wait_ge(sP.h, pv)
                i = a.activation(out=mt, in_=slot_u, func=AF.Copy, scale=gc)
                last_upd_a[0] = sAc.inc(i)
                chain_upd_v[(h, ch)] = (sAc, last_upd_a[0])
            act(_)
            if ch % 8 == 7:
                def _(v, h=h, ch=ch, pv=pv, P=P):
                    v.wait_ge(sP.h, pv)
                    base = ch - 7
                    dst = AR1[hh:hh + 64, hp * 2048 + base * 64: hp * 2048 + (base + 8) * 64]
                    i = v.tensor_tensor(out=dst, in0=ROUT[hh:hh + 64, :],
                                        in1=P["OlocT"][hh:hh + 64, base * 64:(base + 8) * 64],
                                        op=ALU.add)
                    chain_out_evac[(h, ch // 8)] = sV.inc(i)
                vec(_)
        else:
            def _(v, h=h, ch=ch, pv=pv, gc=gc, P=P):
                v.wait_ge(sP.h, pv)
                i = v.tensor_scalar(out=mt, in0=slot_u, scalar1=gc, scalar2=None, op0=ALU.mult)
                chain_upd_v[(h, ch)] = (sV, sV.inc(i))
                if ch % 8 == 7:
                    base = ch - 7
                    dst = AR1[hh:hh + 64, hp * 2048 + base * 64: hp * 2048 + (base + 8) * 64]
                    i = v.tensor_tensor(out=dst, in0=ROUT[hh:hh + 64, :],
                                        in1=P["OlocT"][hh:hh + 64, base * 64:(base + 8) * 64],
                                        op=ALU.add)
                    chain_out_evac[(h, ch // 8)] = sV.inc(i)
            vec(_)

    def _interleave(a, b):
        out = []
        for i in range(max(len(a), len(b))):
            if i < len(a):
                out.append(a[i])
            if i < len(b):
                out.append(b[i])
        return out

    for h in range(solve_heads):
        steps = list(solve_steps(h))[:solve_nsteps]
        tasks = []
        if do_chain:
            a = [(h - 2, c) for c in range(16, NCH)] if h >= 2 else []
            b = [(h - 1, c) for c in range(0, 16)] if h >= 1 else []
            tasks = _interleave(a, b)
        per = (len(tasks) + len(steps) - 1) // len(steps) if tasks else 0
        ti = 0
        for s in steps:
            s()
            for _k in range(min(per, len(tasks) - ti)):
                chain_step(*tasks[ti])
                ti += 1
        while ti < len(tasks):
            chain_step(*tasks[ti])
            ti += 1
    if do_chain:
        H_ = solve_heads
        a = [(H_ - 2, c) for c in range(16, NCH)] if H_ >= 2 else []
        b = [(H_ - 1, c) for c in range(0, 16)] if H_ >= 1 else []
        for t_ in _interleave(a, b) + [(H_ - 1, c) for c in range(16, NCH)]:
            chain_step(*t_)
    attn_mix_v = sV.n
    tap("GTp", GTp[:], sV, attn_mix_v)
    tap("OlocTp", OlocTp[:], sV, attn_mix_v)
    tap("YKnegp", YKnegp[:], sV, attn_mix_v)
    tap("F0Tp", F0Tp[:], sV, attn_mix_v)
    if stop_phase <= 7.5:
        stub_out()
        st_.close()
        return nc, tapnames
    tap("outG", AR1[:], sV, attn_mix_v)
    tap("MTall", MTall[:], sV, attn_mix_v)
    tap("gCrep2end", gCrep2[:], sV, attn_mix_v)

    if stop_phase <= 8:
        stub_out()
        st_.close()
        return nc, tapnames

    sq_pe = [0]
    ro_pv = 0
    for dt in range(NDT):
        def _(v, dt=dt, attn_mix_v=attn_mix_v):
            v.wait_ge(sV.h, attn_mix_v)
            if sq_pe[0]:
                v.wait_ge(sP.h, sq_pe[0])
            i = v.tensor_tensor(out=sqB[:], in0=AR1[:, dt * 2048:(dt + 1) * 2048],
                                in1=AR1[:, dt * 2048:(dt + 1) * 2048], op=ALU.mult)
            sV.inc(i)
        vec(_)
        sqv = sV.n

        def _(t, dt=dt, sqv=sqv):
            t.wait_ge(sV.h, max(sqv, ones_v))
            if dt == 0:
                t.wait_ge(sV.h, max(last_evac["RA"], last_evac["RB"]))
                if last_evacA["RA"] or last_evacA["RB"]:
                    t.wait_ge(sAc.h, max(last_evacA["RA"], last_evacA["RB"]))
            for sc in range(4):
                i = t.matmul(out=psA[0:1, sc * 512:(sc + 1) * 512], lhsT=onescol[:],
                             rhs=sqB[:, sc * 512:(sc + 1) * 512],
                             start=(dt == 0), stop=(dt == NDT - 1))
            sq_pe[0] = sP.inc(i)
        ten(_)
        ro_pv = sP.n

    def _(v, ro_pv=ro_pv):
        v.wait_ge(sP.h, ro_pv)
        i = v.tensor_scalar(out=psA[0:1, :], in0=psA[0:1, :], scalar1=1.0 / D, scalar2=EPS_RMS,
                            op0=ALU.mult, op1=ALU.add)
        sV.inc(i)
    vec(_)
    ro1_v = sV.n

    def _(a, ro1_v=ro1_v):
        a.wait_ge(sV.h, ro1_v)
        a.activation(out=psA[32:33, :], in_=psA[0:1, :], func=AF.Ln)
        i = a.activation(out=sqB[0:1, 0:2048], in_=psA[32:33, :], func=AF.Exp, scale=-0.5)
        sAc.inc(i)
    act(_)
    ro2_v = sAc.n

    def _(v, ro2_v=ro2_v):
        v.wait_ge(sAc.h, ro2_v)
        i = v.nop()
        free["psA"] = sV.inc(i)
    vec(_)
    ro_v = sV.n

    mixn_prev = [ro_v]
    for dt in range(NDT):
        def _(t, dt=dt):
            t.wait_ge(sV.h, mixn_prev[0])
            for sc in range(4):
                i = t.matmul(out=psA[:, sc * 512:(sc + 1) * 512], lhsT=onesONE[:],
                             rhs=sqB[0:1, sc * 512:(sc + 1) * 512], start=True, stop=True)
            sP.inc(i)
        ten(_)
        rof_pv = sP.n

        def _(v, dt=dt, rof_pv=rof_pv):
            v.wait_ge(sP.h, rof_pv)
            i = v.tensor_tensor(out=AR1[:, dt * 2048:(dt + 1) * 2048],
                                in0=AR1[:, dt * 2048:(dt + 1) * 2048], in1=psA[:], op=ALU.mult)
            mixn_prev[0] = free["psA"] = sV.inc(i)
        vec(_)
    mixn_v = sV.n
    tap("mixn", AR1[:], sV, mixn_v)

    if stop_phase <= 9:
        stub_out()
        st_.close()
        return nc, tapnames

    def _(s, mixn_v=mixn_v):
        s.wait_ge(sV.h, mixn_v)
        i = s.dma_start(out=AR0[:], in_=xnT_scr[:])
        sX.inc(i, 16)
    syn(_)
    xnT2_v = sX.n

    hid_ar = [AR2, AR3, AR4]
    hidmm_done = {}
    silu_done = {}
    evac_done = {}
    psTF2 = psT[:].bitcast(F32)
    hid_v = 0
    for ht in range(NHT):
        wgt_b = wgt if ht % 2 == 0 else wgt2
        wut_b = wutl if ht % 2 == 0 else wutl2

        def _(g, ht=ht, wgt_b=wgt_b, wut_b=wut_b, mixn_v=mixn_v):
            if ht < 2:
                g.wait_ge(sV.h, mixn_v)
            if ht >= 2:
                g.wait_ge(sP.h, hidmm_done.get(ht - 2, 0))
            i = g.dma_start(out=wgt_b, in_=w["wgT"][ht * 128:(ht + 1) * 128, :])
            sD.inc(i, 16)
            i = g.dma_start(out=wut_b, in_=w["wuT"][ht * 128:(ht + 1) * 128, :])
            sD.inc(i, 16)
        gps(_)
        wl_v = sD.n

        def _(t, wl_v=wl_v, ht=ht, wgt_b=wgt_b):
            t.wait_ge(sD.h, wl_v)
            if ht == 0:
                t.wait_ge(sV.h, free["psA"])
                t.wait_ge(sX.h, xnT2_v)
            else:
                t.wait_ge(sAc.h, silu_done[ht - 1])
            for sc in range(4):
                for kt in range(NDT):
                    i = t.matmul(out=psA[:, sc * 512:(sc + 1) * 512],
                                 lhsT=wgt_b[:, kt * 128:(kt + 1) * 128],
                                 rhs=AR0[:, kt * 2048 + sc * 512: kt * 2048 + sc * 512 + 512],
                                 start=(kt == 0), stop=(kt == NDT - 1))
            sP.inc(i)
        ten(_)
        g_pv = sP.n

        def _(a, g_pv=g_pv, ht=ht):
            a.wait_ge(sP.h, g_pv)
            if ht >= 1:
                a.wait_ge(sV.h, evac_done[ht - 1])
            i = a.activation(out=siluB[:], in_=psA[:], func=AF.Silu, scale=SCALE)
            silu_done[ht] = sAc.inc(i)
        act(_)

        def _(t, ht=ht, wut_b=wut_b, mixn_v=mixn_v):
            if ht == 0:
                t.wait_ge(sV.h, mixn_v)
                t.wait_ge(sAc.h, last_upd_a[0])
            else:
                t.wait_ge(sV.h, evac_done[ht - 1])
            for sc in range(4):
                reg = psB[:, (sc % 2) * 512:(sc % 2) * 512 + 512] if sc < 2 \
                    else psTF2[:, (sc % 2) * 512:(sc % 2) * 512 + 512]
                for kt in range(NDT):
                    i = t.matmul(out=reg,
                                 lhsT=wut_b[:, kt * 128:(kt + 1) * 128],
                                 rhs=AR0[:, kt * 2048 + sc * 512: kt * 2048 + sc * 512 + 512],
                                 start=(kt == 0), stop=(kt == NDT - 1))
            hidmm_done[ht] = sP.inc(i)
        ten(_)
        u_pv = sP.n

        def _(v, u_pv=u_pv, ht=ht):
            v.wait_ge(sP.h, u_pv)
            v.wait_ge(sAc.h, silu_done[ht])
            dst = hid_ar[ht // 8][:, (ht % 8) * 2048:((ht % 8) + 1) * 2048]
            v.tensor_tensor(out=dst[:, 0:1024], in0=siluB[:, 0:1024], in1=psB[:], op=ALU.mult)
            i = v.tensor_tensor(out=dst[:, 1024:2048], in0=siluB[:, 1024:2048],
                                in1=psTF2[:], op=ALU.mult)
            evac_done[ht] = free["psA"] = sV.inc(i)
        vec(_)
        hid_v = sV.n

    tap("xnT2", AR0[:], sX, xnT2_v)
    tap("hidden0", AR2[:], sV, hid_v)
    tap("hidden1", AR3[:], sV, hid_v)
    tap("hidden2", AR4[:], sV, hid_v)

    if stop_phase <= 10:
        stub_out()
        st_.close()
        return nc, tapnames

    def _(g, hid_v=hid_v):
        g.wait_ge(sV.h, hid_v)
        i = g.dma_start(out=wbuf[:].rearrange("p (t c) -> p t c", c=D),
                        in_=w["Wo"][:].rearrange("(t p) c -> p t c", p=128))
        sD.inc(i, 16)
    gps(_)
    wo_v = sD.n

    def _(v, wo_v=wo_v):
        v.wait_ge(sD.h, wo_v)
        in1 = onwP[:].unsqueeze(2).broadcast_to([128, NDT, D])
        i = v.tensor_tensor(out=wbuf[:].rearrange("p (t c) -> p t c", c=D),
                            in0=wbuf[:].rearrange("p (t c) -> p t c", c=D), in1=in1, op=ALU.mult)
        sV.inc(i)
    vec(_)
    wofold_v = sV.n

    y_evac = {}
    out_dma = {}
    ymm_last = {}
    for dmac in range(2):
        def _(g, dmac=dmac, hid_v=hid_v):
            g.wait_ge(sV.h, hid_v)
            if dmac == 1:
                g.wait_ge(sP.h, ymm_last[0])
            i = g.dma_start(out=wdTm, in_=w["wdT"][dmac * 128:(dmac + 1) * 128, :])
            sD.inc(i, 16)
        gps(_)
        wdl_v = sD.n

        def _(a, wdl_v=wdl_v):
            a.wait_ge(sD.h, wdl_v)
            i = a.activation(out=wdTm, in_=wdTm, func=AF.Copy, scale=SCALE * SCALE)
            sAc.inc(i)
        act(_)
        wdT_a = sAc.n

        for sti in range(NST):
            slot = sti % 2
            yb = ybuf if slot == 0 else ybuf2
            xb = xfin if slot == 0 else xfin2

            def _(s, sti=sti, dmac=dmac, xb=xb, hid_v=hid_v):
                if dmac == 0 and sti < 2:
                    s.wait_ge(sV.h, hid_v)
                key = (dmac, sti - 2) if sti >= 2 else ((dmac - 1, NST - 2 + sti) if dmac else None)
                if key in y_evac:
                    s.wait_ge(sV.h, y_evac[key])
                i = s.dma_start(out=xb, in_=x_h[sti * 128:(sti + 1) * 128,
                                                dmac * 512:(dmac + 1) * 512])
                sX.inc(i, 16)
            syn(_)
            x_v = sX.n

            def _(t, sti=sti, dmac=dmac, slot=slot, wdT_a=wdT_a):
                t.wait_ge(sAc.h, wdT_a)
                if dmac == 0 and sti == 0:
                    t.wait_ge(sV.h, max(wofold_v, hid_v, mixn_v))
                key = (dmac, sti - 2) if sti >= 2 else ((dmac - 1, NST - 2 + sti) if dmac else None)
                if key in y_evac:
                    t.wait_ge(sV.h, y_evac[key])
                reg = psB[:, slot * 512:(slot + 1) * 512]
                for kt in range(NDT):
                    t.matmul(out=reg,
                             lhsT=AR1[:, kt * 2048 + sti * 128: kt * 2048 + (sti + 1) * 128],
                             rhs=wbuf[:, kt * 1024 + dmac * 512: kt * 1024 + (dmac + 1) * 512],
                             start=(kt == 0), stop=False)
                for b in range(NHT):
                    i = t.matmul(out=reg,
                                 lhsT=hid_ar[b // 8][:, (b % 8) * 2048 + sti * 128:
                                                     (b % 8) * 2048 + (sti + 1) * 128],
                                 rhs=wdTm[:, b * 512:(b + 1) * 512],
                                 start=False, stop=(b == NHT - 1))
                ymm_last[dmac] = sP.inc(i)
            ten(_)
            ymm_pv = sP.n

            def _(v, ymm_pv=ymm_pv, x_v=x_v, slot=slot, yb=yb, xb=xb, dmac=dmac, sti=sti):
                v.wait_ge(sP.h, ymm_pv)
                v.wait_ge(sX.h, x_v)
                key = (dmac, sti - 2) if sti >= 2 else ((dmac - 1, NST - 2 + sti) if dmac else None)
                if key in out_dma:
                    v.wait_ge(sO.h, out_dma[key])
                reg = psB[:, slot * 512:(slot + 1) * 512]
                i = v.tensor_tensor(out=yb, in0=reg, in1=xb, op=ALU.add)
                y_evac[(dmac, sti)] = sV.inc(i)
            vec(_)
            ye_v = sV.n

            def _(s, sti=sti, dmac=dmac, ye_v=ye_v, yb=yb):
                s.wait_ge(sV.h, ye_v)
                i = s.dma_start(out=y_h[sti * 128:(sti + 1) * 128, dmac * 512:(dmac + 1) * 512],
                                in_=yb)
                out_dma[(dmac, sti)] = sO.inc(i, 16)
            syn(_)

    total_out = sO.n

    def _(g, total_out=total_out):
        g.wait_ge(sO.h, total_out)
    gps(_)

    st_.close()
    return nc, tapnames


_cache = {}


def _get_nc(tapkey=()):
    if tapkey not in _cache:
        _cache[tapkey] = _build([(t, list(s), d) for t, s, d in tapkey])
    return _cache[tapkey]


def _host_inputs(inputs, b):
    f = lambda k: np.ascontiguousarray(np.asarray(inputs[k], dtype=np.float32))
    Wrq = np.asarray(inputs["L0_attn_Wrq"], np.float32)
    Wrk = np.asarray(inputs["L0_attn_Wrk"], np.float32)
    Wrq_p = np.zeros((D, 1024), np.float32)
    Wrk_p = np.zeros((D, 1024), np.float32)
    for h in range(H):
        Wrq_p[:, h * 64:h * 64 + RR] = Wrq[:, h * RR:(h + 1) * RR]
        Wrk_p[:, h * 64:h * 64 + RR] = Wrk[:, h * RR:(h + 1) * RR]
    Wfl = np.concatenate([np.asarray(inputs["L0_attn_Wf"], np.float32),
                          np.asarray(inputs["L0_attn_Wl"], np.float32)], axis=1)
    bfl = np.concatenate([np.asarray(inputs["L0_attn_bf"], np.float32),
                          np.asarray(inputs["L0_attn_bl"], np.float32)])[None, :]

    def cdiag_of(name):
        cw = np.asarray(inputs[name], np.float32)
        out = np.zeros((128, NDT, 4, 32), np.float32)
        for d in range(D):
            p, dt = d % 128, d // 128
            out[p, dt, :, p % 32] = cw[d, :]
        return np.ascontiguousarray(out.reshape(128, NDT * 4 * 32))

    onw = np.ascontiguousarray(np.asarray(inputs["L0_attn_on_w"], np.float32).reshape(NDT, 128).T)
    ident = np.eye(128, dtype=np.float32)
    t = np.arange(64)
    maskUi = np.ascontiguousarray(np.tile((t[:, None] <= t[None, :]).astype(np.float32), (2, 1)))
    maskUs = np.ascontiguousarray(np.tile((t[:, None] < t[None, :]).astype(np.float32), (2, 1)))
    maskLs = np.ascontiguousarray(np.tile((t[:, None] > t[None, :]).astype(np.float32), (2, 1)))
    hv4 = np.zeros((128, 128), np.float32)
    for bb_ in range(0, 128, 32):
        hv4[bb_, 0:64] = 1.0
        hv4[bb_ + 1, 64:128] = 1.0
    hv4[64, :] = 0.0
    hv4[65, :] = 0.0
    hv4[64, 0:64] = 1.0
    hv4[65, 64:128] = 1.0
    bonesAll = np.zeros((128, 8, 32), np.float32)
    for dt_ in range(8):
        bonesAll[0:64, dt_, 2 * dt_] = 1.0
        bonesAll[64:128, dt_, 2 * dt_ + 1] = 1.0
    bonesAll = np.ascontiguousarray(bonesAll.reshape(128, 256))
    p = np.arange(128)
    hsl = np.arange(16)
    selh = ((p[None, :] >= 64).astype(np.int32) == (hsl[:, None] % 2)).astype(np.float32)
    hseldt = (np.arange(8)[None, :] == (hsl[:, None] // 2)).astype(np.float32)

    return {
        "x": np.ascontiguousarray(np.asarray(inputs["L0_attn_x"], np.float32)[b]),
        "Wq": f("L0_attn_Wq"), "Wk": f("L0_attn_Wk"), "Wv": f("L0_attn_Wv"),
        "Wo": f("L0_attn_Wo"),
        "Wrq": Wrq_p, "Wrk": Wrk_p, "Wfl": np.ascontiguousarray(Wfl),
        "bfl_f": np.ascontiguousarray(bfl.reshape(32, 1)[0:16]),
        "bfl_l": np.ascontiguousarray(bfl.reshape(32, 1)[16:32]),
        "cdq": cdiag_of("L0_attn_conv_q"), "cdk": cdiag_of("L0_attn_conv_k"),
        "cdv": cdiag_of("L0_attn_conv_v"),
        "gs": np.ascontiguousarray(np.asarray(inputs["L0_attn_graph_scale"], np.float32).reshape(16, 1)),
        "qnw": np.ascontiguousarray(np.asarray(inputs["L0_attn_qn_w"], np.float32).reshape(64, 1)),
        "knw": np.ascontiguousarray(np.asarray(inputs["L0_attn_kn_w"], np.float32).reshape(64, 1)),
        "onw": onw,
        "wgT": np.ascontiguousarray(
            np.asarray(inputs["L0_ffn_w_gate"], np.int32)
            .reshape(24, 128, 8, 128).transpose(0, 3, 2, 1).reshape(3072, 1024)),
        "wuT": np.ascontiguousarray(
            np.asarray(inputs["L0_ffn_w_up"], np.int32)
            .reshape(24, 128, 8, 128).transpose(0, 3, 2, 1).reshape(3072, 1024)),
        "wdT": np.ascontiguousarray(
            np.asarray(inputs["L0_ffn_w_down"], np.int32)
            .reshape(2, 512, 24, 128).transpose(0, 3, 2, 1).reshape(256, 12288)),
        "ident": ident, "maskUi": maskUi, "maskUs": maskUs, "maskLs": maskLs,
        "hv4": hv4, "bonesAll": bonesAll,
        "onesone": np.ones((1, 128), np.float32),
        "selh": np.ascontiguousarray(selh), "hsel": np.ascontiguousarray(hseldt),
    }


def _run(inputs, taps=()):
    tapkey = tuple((t, tuple(s), d) for t, s, d in taps)
    nc, tapnames = _get_nc(tapkey)
    in_maps = [_host_inputs(inputs, b) for b in range(8)]
    res = run_bass_kernel_spmd(nc, in_maps, core_ids=list(range(8)))
    y = np.stack([r["y"] for r in res.results], axis=0)
    tapd = [{t: r["tap_" + t] for t in tapnames} for r in res.results]
    return y, tapd


def kernel(**inputs):
    y, _ = _run(inputs)
    return y


# version 09
# speedup vs baseline: 1.3137x; 1.0972x over previous
import contextlib
import numpy as np
import concourse.bass as bass
import concourse.mybir as mybir
from concourse.bass_utils import run_bass_kernel_spmd

F32 = mybir.dt.float32
BF16 = mybir.dt.bfloat16
I32 = mybir.dt.int32
AF = mybir.ActivationFunctionType
ALU = mybir.AluOpType
AXX = mybir.AxisListType.X

S, D, H, DH, RR, C = 2048, 1024, 16, 64, 48, 64
NST, NDT, NCH, NCP = 16, 8, 32, 16
HID, NHT = 3072, 24
EPS_LN, EPS_RMS = 1e-5, 1e-6
SCALE = 1.0 / 127.0


class SemT:
    def __init__(self, h):
        self.h = h
        self.n = 0

    def inc(self, instr, k=1):
        instr.then_inc(self.h, k)
        self.n += k
        return self.n


def _build(taps=(), stop_phase=99, solve_heads=H, do_chain=True, solve_nsteps=99, p4_parts=99):
    nc = bass.Bass("TRN2", target_bir_lowering=False)
    taps = list(taps)

    x_h = nc.dram_tensor("x", [S, D], F32, kind="ExternalInput").ap()
    w = {}
    for nm, shp, dt in [
        ("Wq", [D, D], F32), ("Wk", [D, D], F32), ("Wv", [D, D], F32), ("Wo", [D, D], F32),
        ("Wrq", [D, D], F32), ("Wrk", [D, D], F32),
        ("Wfl", [D, 32], F32), ("bfl_f", [16, 1], F32), ("bfl_l", [16, 1], F32),
        ("cdq", [128, NDT * 4 * 32], F32), ("cdk", [128, NDT * 4 * 32], F32),
        ("cdv", [128, NDT * 4 * 32], F32),
        ("gs", [16, 1], F32), ("qnw", [64, 1], F32), ("knw", [64, 1], F32),
        ("onw", [128, NDT], F32),
        ("wgT", [HID, D], I32), ("wuT", [HID, D], I32), ("wdT", [256, 12288], I32),
        ("ident", [128, 128], F32),
        ("maskUi", [128, 64], F32), ("maskUs", [128, 64], F32), ("maskLs", [128, 64], F32),
        ("hv4", [128, 128], F32), ("bonesAll", [128, 8 * 32], F32),
        ("onesone", [1, 128], F32),
        ("selh", [16, 128], F32), ("hsel", [16, 8], F32),
    ]:
        w[nm] = nc.dram_tensor(nm, shp, dt, kind="ExternalInput").ap()
    y_h = nc.dram_tensor("y", [S, D], F32, kind="ExternalOutput").ap()
    xnT_scr = nc.dram_tensor("xnT_scr", [128, NDT * S], BF16, kind="Internal").ap()
    tap_h = {t: nc.dram_tensor("tap_" + t, shp, dt, kind="ExternalOutput").ap()
             for t, shp, dt in taps}
    tapnames = [t for t, _, _ in taps]

    st_ = contextlib.ExitStack()
    SB = lambda n, s, d: st_.enter_context(nc.sbuf_tensor(n, s, d))

    AR0 = SB("AR0", [128, NDT * S], BF16)
    AR1 = SB("AR1", [128, NDT * S], BF16)
    AR2 = SB("AR2", [128, NDT * S], BF16)
    AR3 = SB("AR3", [128, NST * D], BF16)
    AR4 = SB("AR4", [128, NST * D], BF16)
    scrB = SB("scrB", [128, 2048], BF16)
    ksumB = SB("ksumB", [128, 2048], BF16)
    sqB = SB("sqB", [128, 2048], BF16)
    preB = sqB
    crepRow = sqB

    wbuf = SB("wbuf", [128, NDT * D], BF16)
    wflT = SB("wflT", [128, NDT * 32], BF16)
    bflP = SB("bflP", [48, 1], F32)
    bfP = bflP[0:16, :]
    blP = bflP[32:48, :]
    onesONE = SB("onesONE", [1, 128], BF16)
    onescol = SB("onescol", [128, 1], BF16)
    cdiag = SB("cdiag", [128, 2 * 128], BF16)
    identB = SB("identB", [128, 128], BF16)
    identF = SB("identF", [128, 128], F32)
    maskUi = SB("maskUi_s", [128, 64], F32)
    maskUs = SB("maskUs_s", [128, 64], F32)
    maskLs = SB("maskLs_s", [128, 64], F32)
    hv4B = SB("hv4B", [128, 128], BF16)
    hv4F = SB("hv4F", [128, 128], F32)
    bonesAll = SB("bonesAll_s", [128, 8 * 32], BF16)
    selhB = SB("selhB", [16, 128], BF16)
    hselB = SB("hselB", [16, 8], F32)
    hselL = SB("hselL", [16, 8], BF16)
    qnwB = SB("qnwB", [128, 1], F32)
    knwB = SB("knwB", [128, 1], F32)
    lamB = SB("lamB", [16, 1], F32)
    lamrep = SB("lamrep", [128, NDT], F32)
    onwP = SB("onwP", [128, NDT], F32)

    hm = SB("hm", [128, 2048], F32)
    hstg = SB("hstg", [128, 1024], BF16)
    sm = SB("sm", [128, 3 * 256], BF16)
    gCrep2 = SB("gCrep2", [128, 8 * 32], F32)
    stat1 = SB("stat1", [128, 1], F32)
    stat2 = SB("stat2", [128, 1], F32)
    stat3 = SB("stat3", [128, 1], F32)
    stat4 = SB("stat4", [128, 1], F32)
    MTall = SB("MTall", [128, 8 * 64], BF16)

    scrV = scrB[:].bitcast(F32)
    ksumV = ksumB[:].bitcast(F32)
    sqV = sqB[:].bitcast(F32)
    xin = ksumV
    xin2 = scrV
    xnf = ksumV
    scrF = scrV
    ksumF = ksumV
    wgt = ksumB[:, 0:1024]
    wutl = ksumB[:, 1024:2048]
    wgt2 = scrB[:, 0:1024]
    wutl2 = scrB[:, 1024:2048]
    siluB = preB
    wdTm = AR0[:, 0:NHT * 512]
    wdrow = AR0[:, NHT * 512:NHT * 512 + 3072]
    ybuf = sqV[:, 0:512]
    ybuf2 = sqV[:, 512:1024]
    xfin = ksumV[:, 0:512]
    xfin2 = ksumV[:, 512:1024]
    woB = wbuf
    PcB = AR0[:, 0:1024]
    A1B = AR0[:, 1024:2048]
    A2B = AR0[:, 2048:3072]
    B1B = AR0[:, 3072:4096]
    B2B = AR0[:, 4096:5120]
    crepB = AR0[:, 5120:6144]
    XB = AR0[:, 6144:8192]
    GTp = AR0[:, 8192:10240]
    OlocTp = AR0[:, 10240:12288]
    YKnegp = AR0[:, 12288:14336]
    F0Tp = AR0[:, 14336:16384]

    psA = st_.enter_context(nc.psum_tensor("psA", [128, 2048], F32))
    psB = st_.enter_context(nc.psum_tensor("psB", [128, 1024], F32))
    psT = st_.enter_context(nc.psum_tensor("psT", [128, 2048], BF16))
    RA = psA[:, 0:1024]
    RB = psA[:, 1024:2048]
    RUPD = psB[:, 0:512]
    ROUT = psB[:, 512:1024]

    sD = SemT(st_.enter_context(nc.semaphore("sD")))
    sP = SemT(st_.enter_context(nc.semaphore("sP")))
    sV = SemT(st_.enter_context(nc.semaphore("sV")))
    sAc = SemT(st_.enter_context(nc.semaphore("sAc")))
    sO = SemT(st_.enter_context(nc.semaphore("sO")))
    sX = SemT(st_.enter_context(nc.semaphore("sX")))
    blk = st_.enter_context(nc.Block())
    gps, syn, vec, act, ten = blk.gpsimd, blk.sync, blk.vector, blk.scalar, blk.tensor

    def stub_out():
        def _(v):
            v.wait_ge(sP.h, sP.n)
            v.wait_ge(sAc.h, sAc.n)
            v.wait_ge(sD.h, sD.n)
            v.wait_ge(sX.h, sX.n)
            i = v.memset(sqB[:], 0.0)
            sV.inc(i)
        vec(_)
        sv = sV.n

        def _(s, sv=sv):
            s.wait_ge(sV.h, sv)
            n = 0
            for sti in range(NST):
                for dm in range(2):
                    i = s.dma_start(out=y_h[sti * 128:(sti + 1) * 128, dm * 512:(dm + 1) * 512],
                                    in_=sqB[:].bitcast(F32)[:, 0:512])
                    sO.inc(i, 16)
        syn(_)
        tot = sO.n

        def _(g, tot=tot):
            g.wait_ge(sO.h, tot)
        gps(_)

    free = {"psA": 0, "psT": 0}

    HMI = {"C": 0, "S1": 1, "S2": 2, "S3": 3,
           "LNA": 1, "AG": 1, "G": 1, "N2K": 1, "INVN": 1,
           "L": 2, "B": 2,
           "A": 3, "GC": 3, "MSQ": 3, "AQG": 3, "MSK": 3}

    def hmr(k, sub=slice(0, 16)):
        i = HMI[k]
        base = i * 32
        return hm[base + sub.start: base + sub.stop, :]

    fstage_pe_init = [0]

    SMI = {"C": 0, "INVN": 1, "AG": 2}

    def smr(k):
        return sm[:, SMI[k] * 256:(SMI[k] + 1) * 256]

    def tap(name, src_ap, wait_sem, wait_val):
        if name not in tap_h:
            return
        def _(g, name=name, src_ap=src_ap, wait_sem=wait_sem, wait_val=wait_val):
            g.wait_ge(wait_sem.h, wait_val)
            i = g.dma_start(out=tap_h[name][:], in_=src_ap)
            sO.inc(i, 16)
        gps(_)

    def _(g):
        for dst, src in [(identB, "ident"), (identF, "ident"), (hv4B, "hv4"),
                         (hv4F, "hv4"), (bonesAll, "bonesAll"),
                         (onesONE, "onesone"),
                         (maskUi, "maskUi"), (maskUs, "maskUs"), (maskLs, "maskLs"),
                         (selhB, "selh"), (hselB, "hsel"), (lamB, "gs")]:
            i = g.dma_start(out=dst[:], in_=w[src][:])
            sD.inc(i, 16)
        i = g.dma_start(out=wflT[:].rearrange("p (t c) -> p t c", c=32),
                        in_=w["Wfl"][:].rearrange("(t p) c -> p t c", p=128))
        sD.inc(i, 16)
        i = g.dma_start(out=bflP[0:16, :], in_=w["bfl_f"][:])
        sD.inc(i, 16)
        i = g.dma_start(out=bflP[32:48, :], in_=w["bfl_l"][:])
        sD.inc(i, 16)
        for dst_, src_ in [(qnwB[0:64, :], "qnw"), (qnwB[64:128, :], "qnw"),
                           (knwB[0:64, :], "knw"), (knwB[64:128, :], "knw")]:
            i = g.dma_start(out=dst_, in_=w[src_][:])
            sD.inc(i, 16)
        i = g.dma_start(out=onwP[:], in_=w["onw"][:])
        sD.inc(i, 16)
    gps(_)
    consts_v = sD.n

    def _(v):
        i = v.memset(onescol[:], 1.0)
        sV.inc(i)
    vec(_)
    ones_v = sV.n

    xnT = AR0
    ln_done_v = [0] * NST
    tr_evac_v = [0] * NST
    sq_read_v = [0] * NST
    for sti in range(NST):
        xb = xin if sti % 2 == 0 else xin2

        def _(s, sti=sti, xb=xb):
            if sti >= 2:
                s.wait_ge(sV.h, ln_done_v[sti - 2])
            i = s.dma_start(out=xb, in_=x_h[sti * 128:(sti + 1) * 128, :])
            sX.inc(i, 16)
        syn(_)
        dv = sX.n

        def _(a, dv=dv, xb=xb, sti=sti):
            a.wait_ge(sX.h, dv)
            if sti >= 1:
                a.wait_ge(sV.h, sq_read_v[sti - 1])
            i = a.activation(out=psB[:], in_=xb, func=AF.Square)
            sAc.inc(i)
        act(_)
        av = sAc.n

        def _(v, dv=dv, av=av, xb=xb, sti=sti):
            v.wait_ge(sX.h, dv)
            v.reduce_sum(out=stat1[:], in_=xb, axis=AXX)
            v.wait_ge(sAc.h, av)
            i = v.reduce_sum(out=stat2[:], in_=psB[:], axis=AXX)
            sq_read_v[sti] = sV.inc(i)
            v.tensor_scalar(out=stat1[:], in0=stat1[:], scalar1=1.0 / D, scalar2=None, op0=ALU.mult)
            v.tensor_scalar(out=stat2[:], in0=stat2[:], scalar1=1.0 / D, scalar2=None, op0=ALU.mult)
            v.tensor_tensor(out=stat3[:], in0=stat1[:], in1=stat1[:], op=ALU.mult)
            v.tensor_tensor(out=stat2[:], in0=stat2[:], in1=stat3[:], op=ALU.subtract)
            i = v.tensor_scalar(out=stat2[:], in0=stat2[:], scalar1=EPS_LN, scalar2=None, op0=ALU.add)
            sV.inc(i)
        vec(_)
        vv = sV.n

        def _(a, vv=vv):
            a.wait_ge(sV.h, vv)
            a.activation(out=stat3[:], in_=stat2[:], func=AF.Ln)
            i = a.activation(out=stat4[:], in_=stat3[:], func=AF.Exp, scale=-0.5)
            sAc.inc(i)
        act(_)
        av2 = sAc.n

        def _(v, av2=av2, xb=xb, sti=sti):
            v.wait_ge(sAc.h, av2)
            i = v.tensor_scalar(out=preB[:, 0:1024], in0=xb, scalar1=stat1[:, 0:1],
                                scalar2=stat4[:, 0:1], op0=ALU.subtract, op1=ALU.mult)
            ln_done_v[sti] = sV.inc(i)
        vec(_)
        xn_v = sV.n

        def _(t, xn_v=xn_v, sti=sti):
            t.wait_ge(sV.h, xn_v)
            t.wait_ge(sD.h, consts_v)
            if sti >= 1:
                t.wait_ge(sV.h, tr_evac_v[sti - 1])
            for dtb in range(NDT):
                i = t.transpose(out=psT[:, dtb * 128:(dtb + 1) * 128],
                                in_=preB[:, dtb * 128:(dtb + 1) * 128], identity=identB[:])
            sP.inc(i)
        ten(_)
        pv = sP.n

        def _(v, pv=pv, sti=sti):
            v.wait_ge(sP.h, pv)
            dst = xnT[:].rearrange("p (t s) -> p t s", t=NDT)[:, :, sti * 128:(sti + 1) * 128]
            i = v.tensor_copy(out=dst, in_=psT[:, 0:1024].rearrange("p (t s) -> p t s", s=128))
            tr_evac_v[sti] = sV.inc(i)
        vec(_)
    xnT_v = sV.n
    free["psA"] = xnT_v
    free["psT"] = xnT_v
    tap("xnT", xnT[:], sV, xnT_v)

    if stop_phase <= 1:
        stub_out()
        st_.close()
        return nc, tapnames

    def _(a):
        a.wait_ge(sD.h, consts_v)
        i = a.activation(out=lamB[:], in_=lamB[:], func=AF.Abs)
        sAc.inc(i)
    act(_)
    lam_av = sAc.n

    def _(v, lam_av=lam_av):
        v.wait_ge(sAc.h, lam_av)
        i = v.tensor_scalar(out=hselL[:], in0=hselB[:], scalar1=lamB[:, 0:1], scalar2=None, op0=ALU.mult)
        sV.inc(i)
    vec(_)
    lam_v = sV.n

    fl_evac = [0, 0, 0, 0]
    for sh in range(4):
        def _(t, sh=sh):
            t.wait_ge(sV.h, xnT_v)
            if sh >= 2:
                t.wait_ge(sAc.h, fl_evac[sh - 2])
            base = (sh % 2) * 512
            for kt in range(NDT):
                t.matmul(out=psB[0:16, base:base + 512],
                         lhsT=wflT[:, kt * 32:kt * 32 + 16],
                         rhs=xnT[:, kt * 2048 + sh * 512: kt * 2048 + sh * 512 + 512],
                         start=(kt == 0), stop=(kt == NDT - 1))
                i = t.matmul(out=psB[32:48, base:base + 512],
                             lhsT=wflT[:, kt * 32 + 16:(kt + 1) * 32],
                             rhs=xnT[:, kt * 2048 + sh * 512: kt * 2048 + sh * 512 + 512],
                             start=(kt == 0), stop=(kt == NDT - 1))
            sP.inc(i)
        ten(_)
        pv = sP.n

        def _(a, sh=sh, pv=pv):
            a.wait_ge(sP.h, pv)
            base = (sh % 2) * 512
            a.activation(out=hmr("A")[:, sh * 512:(sh + 1) * 512], in_=psB[0:16, base:base + 512],
                         func=AF.Sigmoid, bias=bfP)
            i = a.activation(out=hmr("B")[:, sh * 512:(sh + 1) * 512], in_=psB[32:48, base:base + 512],
                             func=AF.Sigmoid, bias=blP)
            fl_evac[sh] = sAc.inc(i)
        act(_)
    fl_done_a = fl_evac[3]

    def _(a):
        a.wait_ge(sAc.h, fl_done_a)
        i = a.activation(out=hmr("LNA"), in_=hmr("A"), func=AF.Ln)
        sAc.inc(i)
    act(_)
    lna_av = sAc.n

    def _(a):
        a.wait_ge(sAc.h, lna_av)
        a.wait_ge(sV.h, free["psA"])
        i = a.activation(out=psA[0:16, :], in_=hmr("B"), func=AF.Ln)
        sAc.inc(i)
    act(_)
    lnb_av = sAc.n

    def _(v, lnb_av=lnb_av):
        v.wait_ge(sAc.h, lnb_av)
        i = v.tensor_tensor(out=psA[0:16, :], in0=psA[0:16, :], in1=hmr("LNA"), op=ALU.subtract)
        sV.inc(i)
    vec(_)
    cdiff_v = sV.n

    def _(a, cdiff_v=cdiff_v):
        a.wait_ge(sV.h, cdiff_v)
        i = a.activation(out=hmr("C"), in_=psA[0:16, :], func=AF.Exp)
        sAc.inc(i)
    act(_)
    c_av = sAc.n

    def _(v, c_av=c_av):
        v.wait_ge(sAc.h, c_av)
        v.tensor_copy(out=hmr("S2"), in_=hmr("LNA"))
        cur, other = "S2", "S3"
        for k in [1, 2, 4, 8, 16, 32]:
            c_ = hmr(cur).rearrange("p (c f) -> p c f", f=64)
            o_ = hmr(other).rearrange("p (c f) -> p c f", f=64)
            v.tensor_copy(out=o_[:, :, 0:k], in_=c_[:, :, 0:k])
            v.tensor_tensor(out=o_[:, :, k:64], in0=c_[:, :, k:64], in1=c_[:, :, 0:64 - k], op=ALU.add)
            cur, other = other, cur
        v.tensor_copy(out=psA[32:48, :], in_=hmr("L"))
        i = v.tensor_tensor(out=hmr("S1"), in0=hmr("LNA"), in1=psA[32:48, :], op=ALU.subtract)
        sV.inc(i)
    vec(_)
    hm_vv = sV.n

    def _(a):
        a.wait_ge(sV.h, hm_vv)
        i = a.activation(out=hmr("S1"), in_=hmr("S1"), func=AF.Exp)
        sAc.inc(i)
    act(_)
    ag_av = sAc.n

    def _(v, ag_av=ag_av):
        v.wait_ge(sAc.h, ag_av)
        Lr = hmr("L").rearrange("p (c f) -> p c f", f=64)
        i = v.tensor_copy(out=hmr("S3").rearrange("p (c f) -> p c f", f=64),
                          in_=Lr[:, :, 63:64].broadcast_to([16, 32, 64]))
        sV.inc(i)
    vec(_)
    gcs_vv = sV.n

    def _(a, gcs_vv=gcs_vv):
        a.wait_ge(sV.h, gcs_vv)
        i = a.activation(out=hmr("GC"), in_=hmr("GC"), func=AF.Exp)
        sAc.inc(i)
    act(_)
    gc_av = sAc.n

    def _(t):
        t.wait_ge(sAc.h, gc_av)
        t.wait_ge(sV.h, max(free["psA"], lam_v))
        for chp in range(16):
            t.transpose(out=psA[:, chp * 16:(chp + 1) * 16],
                        in_=hmr("C")[:, chp * 128:(chp + 1) * 128], identity=identF[0:16, 0:16])
        for chp in range(16):
            t.transpose(out=psA[:, 256 + chp * 16: 256 + (chp + 1) * 16],
                        in_=hmr("AG")[:, chp * 128:(chp + 1) * 128], identity=identF[32:48, 32:48])
        i = t.matmul(out=psA[:, 768:776], lhsT=selhB[:], rhs=hselL[:], start=True, stop=True)
        sP.inc(i)
    ten(_)
    smt_pv = sP.n

    def _(g, smt_pv=smt_pv):
        g.wait_ge(sAc.h, gc_av)
        with nc.allow_non_contiguous_dma(reason="tiny gC sample gather"):
            for hp in range(8):
                src_ = hmr("GC", slice(2 * hp, 2 * hp + 2)).rearrange("p (c f) -> p c f", f=64)[:, :, 63:64].squeeze(2)
                i = g.dma_start(out=hstg[0:2, hp * 32:(hp + 1) * 32], in_=src_)
                sD.inc(i, 16)
    gps(_)
    gcs2_v = sD.n

    def _(t, gcs2_v=gcs2_v, smt_pv=smt_pv):
        t.wait_ge(sD.h, gcs2_v)
        t.wait_ge(sP.h, smt_pv)
        for hp in range(8):
            i = t.matmul(out=psA[:, 512 + hp * 32: 512 + (hp + 1) * 32], lhsT=hv4B[0:2, :],
                         rhs=hstg[0:2, hp * 32:(hp + 1) * 32], start=True, stop=True)
        fstage_pe_init[0] = sP.inc(i)
    ten(_)
    sm_pv = sP.n

    def _(v):
        v.wait_ge(sP.h, sm_pv)
        v.tensor_copy(out=smr("C"), in_=psA[:, 0:256])
        v.tensor_copy(out=smr("AG"), in_=psA[:, 256:512])
        v.tensor_copy(out=gCrep2[:], in_=psA[:, 512:768])
        i = v.tensor_copy(out=lamrep[:], in_=psA[:, 768:776])
        free["psA"] = sV.inc(i)
    vec(_)
    sm_vv = sV.n
    tap("hm", hm[:], sV, sm_vv)
    tap("sm", sm[:], sV, sm_vv)
    tap("gCrep2", gCrep2[:], sV, sm_vv)

    wload_v = {}

    def load_weight(name, wait_pe=0):
        def _(g, name=name, wait_pe=wait_pe):
            if wait_pe:
                g.wait_ge(sP.h, wait_pe)
            i = g.dma_start(out=wbuf[:].rearrange("p (t c) -> p t c", c=D),
                            in_=w[name][:].rearrange("(t p) c -> p t c", p=128))
            sD.inc(i, 16)
        gps(_)
        wload_v[name] = sD.n

    cd_used = {}

    def load_cdiag_dt(name, dt):
        def _(g, name=name, dt=dt):
            if (name, dt - 2) in cd_used:
                g.wait_ge(sP.h, cd_used[(name, dt - 2)])
            i = g.dma_start(out=cdiag[:, (dt % 2) * 128:(dt % 2) * 128 + 128],
                            in_=w[name][:, dt * 128:(dt + 1) * 128])
            sD.inc(i, 16)
        gps(_)
        wload_v[(name, dt)] = sD.n

    conv_evac_v = [0, 0]

    def proj_dt(dt, wname, out_sb, conv_name=None):
        def _(t, dt=dt, wname=wname):
            t.wait_ge(sD.h, wload_v[wname])
            t.wait_ge(sV.h, free["psA"])
            for sc in range(4):
                for kt in range(NDT):
                    i = t.matmul(out=psA[:, sc * 512:(sc + 1) * 512],
                                 lhsT=wbuf[:, kt * 1024 + dt * 128: kt * 1024 + (dt + 1) * 128],
                                 rhs=xnT[:, kt * 2048 + sc * 512: kt * 2048 + sc * 512 + 512],
                                 start=(kt == 0), stop=(kt == NDT - 1))
            sP.inc(i)
        ten(_)
        mm_pv = sP.n

        if conv_name is None:
            def _(v, dt=dt, mm_pv=mm_pv):
                v.wait_ge(sP.h, mm_pv)
                v.wait_ge(sV.h, sm_vv)
                i = v.tensor_scalar(out=out_sb[:, dt * 2048:(dt + 1) * 2048],
                                    in0=psA[:], scalar1=lamrep[:, dt:dt + 1], scalar2=None, op0=ALU.mult)
                free["psA"] = sV.inc(i)
            vec(_)
            return sV.n

        def _(v, mm_pv=mm_pv):
            v.wait_ge(sP.h, mm_pv)
            i = v.tensor_copy(out=preB[:], in_=psA[:])
            free["psA"] = sV.inc(i)
        vec(_)
        pre_vv = sV.n

        for half in range(2):
            def _(t, half=half, dt=dt, pre_vv=pre_vv):
                t.wait_ge(sV.h, pre_vv)
                t.wait_ge(sD.h, wload_v[(conv_name, dt)])
                t.wait_ge(sAc.h, fl_done_a)
                if conv_evac_v[1] or conv_evac_v[0]:
                    t.wait_ge(sV.h, max(conv_evac_v))
                for sc in range(2):
                    ofs = half * 1024 + sc * 512
                    for sub in range(4):
                        for j in [3, 2, 1, 0]:
                            delta = 3 - j
                            lo = max(0, delta - ofs)
                            i = t.matmul(
                                out=psB[32 * sub:32 * sub + 32, sc * 512 + lo:(sc + 1) * 512],
                                lhsT=cdiag[32 * sub:32 * sub + 32,
                                           (dt % 2) * 128 + j * 32: (dt % 2) * 128 + j * 32 + 32],
                                rhs=preB[32 * sub:32 * sub + 32,
                                         ofs + lo - delta: ofs + 512 - delta],
                                start=(j == 3), stop=(j == 0),
                                tile_position=(32 * sub, 32 * sub))
                cd_used[(conv_name, dt)] = sP.inc(i)
            ten(_)
            h_pv = sP.n

            def _(v, half=half, dt=dt, h_pv=h_pv):
                v.wait_ge(sP.h, h_pv)
                i = v.tensor_copy(out=out_sb[:, dt * 2048 + half * 1024: dt * 2048 + (half + 1) * 1024],
                                  in_=psB[:])
                conv_evac_v[half] = sV.inc(i)
            vec(_)
        return conv_evac_v[1]

    def stats_all(src_sb, slot, ready_v):
        sq_pv = [0]
        for dt in range(NDT):
            def _(v, dt=dt, ready_v=ready_v):
                v.wait_ge(sV.h, ready_v)
                if sq_pv[0]:
                    v.wait_ge(sP.h, sq_pv[0])
                elif dt == 0:
                    v.wait_ge(sV.h, free["psA"])
                i = v.tensor_tensor(out=sqB[:], in0=src_sb[:, dt * 2048:(dt + 1) * 2048],
                                    in1=src_sb[:, dt * 2048:(dt + 1) * 2048], op=ALU.mult)
                sV.inc(i)
            vec(_)
            sq_vv = sV.n

            def _(t, sq_vv=sq_vv, dt=dt):
                t.wait_ge(sV.h, sq_vv)
                if dt == 0:
                    t.wait_ge(sV.h, free["psA"])
                for sc in range(4):
                    i = t.matmul(out=psA[0:32, sc * 512:(sc + 1) * 512],
                                 lhsT=bonesAll[:, dt * 32:(dt + 1) * 32],
                                 rhs=sqB[:, sc * 512:(sc + 1) * 512],
                                 start=(dt == 0), stop=(dt == NDT - 1))
                sq_pv[0] = sP.inc(i)
            ten(_)
        st_pv = sP.n

        def _(v, slot=slot, st_pv=st_pv):
            v.wait_ge(sP.h, st_pv)
            i = v.tensor_copy(out=hmr(slot), in_=psA[0:16, :])
            free["psA"] = sV.inc(i)
        vec(_)
        return sV.n

    def rsqrt_pipe(src_slot, dst_slot, scale, eps, v_ready):
        def _(v, v_ready=v_ready):
            v.wait_ge(sV.h, max(v_ready, free["psA"]))
            i = v.tensor_scalar(out=psA[0:16, :], in0=hmr(src_slot), scalar1=scale, scalar2=eps,
                                op0=ALU.mult, op1=ALU.add)
            sV.inc(i)
        vec(_)
        t1v = sV.n

        def _(a, t1v=t1v):
            a.wait_ge(sV.h, t1v)
            a.activation(out=psA[32:48, :], in_=psA[0:16, :], func=AF.Ln)
            i = a.activation(out=hmr(dst_slot), in_=psA[32:48, :], func=AF.Exp, scale=-0.5)
            sAc.inc(i)
        act(_)
        ra = sAc.n

        def _(v, ra=ra):
            v.wait_ge(sAc.h, ra)
            i = v.nop()
            free["psA"] = sV.inc(i)
        vec(_)
        return sV.n

    fstage_pe = fstage_pe_init

    def field_mm(hm_slot, dt, ready_v):
        def _(g, dt=dt, ready_v=ready_v, hm_slot=hm_slot):
            g.wait_ge(sV.h, ready_v)
            if fstage_pe[0]:
                g.wait_ge(sP.h, fstage_pe[0])
            i = g.dma_start(out=hstg[0:2, :], in_=hmr(hm_slot, slice(2 * dt, 2 * dt + 2))[:, 0:1024])
            sD.inc(i, 16)
            i = g.dma_start(out=hstg[32:34, :], in_=hmr(hm_slot, slice(2 * dt, 2 * dt + 2))[:, 1024:2048])
            sD.inc(i, 16)
        gps(_)
        stg_v = sD.n

        def _(t, stg_v=stg_v):
            t.wait_ge(sD.h, stg_v)
            t.wait_ge(sV.h, free["psA"])
            for sc in range(4):
                b = (sc // 2) * 32
                rhs = hstg[b:b + 2, (sc % 2) * 512:(sc % 2) * 512 + 512]
                i = t.matmul(out=psA[:, sc * 512:(sc + 1) * 512],
                             lhsT=hv4B[b:b + 2, :], rhs=rhs, start=True, stop=True,
                             tile_position=(b, 0))
            fstage_pe[0] = sP.inc(i)
        ten(_)
        return sP.n

    if stop_phase <= 2:
        stub_out()
        st_.close()
        return nc, tapnames

    load_weight("Wrq")
    rq_v = 0
    for dt in range(NDT):
        rq_v = proj_dt(dt, "Wrq", AR4, None)

    if stop_phase <= 3:
        stub_out()
        st_.close()
        return nc, tapnames

    load_weight("Wq", wait_pe=sP.n)
    qcv = 0
    for dt in range(NDT):
        load_cdiag_dt("cdq", dt)
        qcv = proj_dt(dt, "Wq", AR2, "cdq")
    q_stat_v = stats_all(AR2, "MSQ", qcv)

    aq_v = rsqrt_pipe("MSQ", "MSQ", 1.0 / DH, EPS_RMS, q_stat_v)

    def _(a, aq_v=aq_v):
        a.wait_ge(sV.h, aq_v)
        i = a.activation(out=hmr("G"), in_=hmr("L"), func=AF.Exp)
        sAc.inc(i)
    act(_)
    g_av = sAc.n

    def _(v, g_av=g_av):
        v.wait_ge(sAc.h, g_av)
        v.wait_ge(sV.h, free["psA"])
        v.tensor_copy(out=psA[32:48, :], in_=hmr("G"))
        i = v.tensor_tensor(out=hmr("AQG"), in0=hmr("MSQ"), in1=psA[32:48, :], op=ALU.mult)
        free["psA"] = sV.inc(i)
    vec(_)
    aqg_v = sV.n

    for dt in range(NDT):
        f1 = field_mm("AQG", dt, aqg_v)

        def _(v, dt=dt, f1=f1):
            v.wait_ge(sP.h, f1)
            v.tensor_scalar(out=scrF[:, 0:1024], in0=AR2[:, dt * 2048:dt * 2048 + 1024],
                            scalar1=qnwB[:, 0:1], scalar2=None, op0=ALU.mult)
            v.tensor_scalar(out=xnf, in0=AR2[:, dt * 2048 + 1024:(dt + 1) * 2048],
                            scalar1=qnwB[:, 0:1], scalar2=None, op0=ALU.mult)
            v.tensor_tensor(out=scrF[:, 0:1024], in0=scrF[:, 0:1024], in1=psA[:, 0:1024], op=ALU.mult)
            i = v.tensor_tensor(out=xnf, in0=xnf, in1=psA[:, 1024:2048], op=ALU.mult)
            free["psA"] = sV.inc(i)
        vec(_)
        f2 = field_mm("G", dt, aqg_v)

        def _(v, dt=dt, f2=f2, rq_v=rq_v):
            v.wait_ge(sP.h, f2)
            v.wait_ge(sV.h, rq_v)
            v.tensor_tensor(out=sqB[:], in0=AR4[:, dt * 2048:(dt + 1) * 2048],
                            in1=psA[:], op=ALU.mult)
            v.tensor_tensor(out=AR1[:, dt * 2048:dt * 2048 + 1024],
                            in0=scrF[:, 0:1024], in1=sqB[:].bitcast(F32)[:, 0:1024] if False else sqB[:, 0:1024],
                            op=ALU.add)
            i = v.tensor_tensor(out=AR1[:, dt * 2048 + 1024:(dt + 1) * 2048],
                                in0=xnf, in1=sqB[:, 1024:2048], op=ALU.add)
            free["psA"] = sV.inc(i)
        vec(_)
    QD_v = sV.n
    tap("QD", AR1[:], sV, QD_v)

    if stop_phase <= 4:
        stub_out()
        st_.close()
        return nc, tapnames

    load_weight("Wv", wait_pe=sP.n)
    for dt in range(NDT):
        load_cdiag_dt("cdv", dt)
        cv = proj_dt(dt, "Wv", AR2, "cdv")

        def _(t, dt=dt, cv=cv):
            t.wait_ge(sV.h, cv)
            if free["psT"]:
                t.wait_ge(sV.h, free["psT"])
            for sb in range(16):
                i = t.transpose(out=psT[:, sb * 128:(sb + 1) * 128],
                                in_=AR2[:, dt * 2048 + sb * 128: dt * 2048 + (sb + 1) * 128],
                                identity=identB[:])
            sP.inc(i)
        ten(_)
        tr_pv = sP.n

        def _(v, dt=dt, tr_pv=tr_pv):
            v.wait_ge(sP.h, tr_pv)
            v.wait_ge(sV.h, sm_vv)
            dst = AR3[:].rearrange("p (t c) -> p t c", c=1024)[:, :, dt * 128:(dt + 1) * 128] \
                .rearrange("p t (h f) -> p t h f", f=64)
            src = psT[:].rearrange("p (t c) -> p t c", c=128).rearrange("p t (h f) -> p t h f", f=64)
            in1 = smr("AG").rearrange("p (t h) -> p t h", h=16)[:, :, 2 * dt:2 * dt + 2] \
                .unsqueeze(3).broadcast_to([128, 16, 2, 64])
            i = v.tensor_tensor(out=dst, in0=src, in1=in1, op=ALU.mult)
            free["psT"] = sV.inc(i)
        vec(_)
    VagT_v = sV.n
    tap("VagT", AR3[:], sV, VagT_v)

    if stop_phase <= 5:
        stub_out()
        st_.close()
        return nc, tapnames

    load_weight("Wrk", wait_pe=sP.n)
    rk_v = 0
    for dt in range(NDT):
        rk_v = proj_dt(dt, "Wrk", AR4, None)

    if stop_phase <= 6:
        stub_out()
        st_.close()
        return nc, tapnames

    load_weight("Wk", wait_pe=sP.n)
    kcv = 0
    for dt in range(NDT):
        load_cdiag_dt("cdk", dt)
        kcv = proj_dt(dt, "Wk", AR2, "cdk")
    k_stat_v = stats_all(AR2, "MSK", kcv)

    def _(s):
        s.wait_ge(sP.h, sP.n)
        i = s.dma_start(out=xnT_scr[:], in_=AR0[:])
        sX.inc(i, 16)
    syn(_)
    xnT_dump_v = sX.n

    ak_v = rsqrt_pipe("MSK", "MSK", 1.0 / DH, EPS_RMS, k_stat_v)
    akb_v = ak_v

    ksum_done = 0
    for dt in range(NDT):
        f1 = field_mm("MSK", dt, akb_v)

        def _(v, dt=dt, f1=f1, rk_v=rk_v):
            v.wait_ge(sP.h, f1)
            v.wait_ge(sV.h, rk_v)
            if dt == 0:
                v.wait_ge(sX.h, xnT_dump_v)
            for hf in range(2):
                o = dt * 2048 + hf * 1024
                v.tensor_scalar(out=ksumF[:], in0=AR2[:, o:o + 1024],
                                scalar1=knwB[:, 0:1], scalar2=None, op0=ALU.mult)
                v.tensor_tensor(out=ksumF[:], in0=ksumF[:],
                                in1=psA[:, hf * 1024:(hf + 1) * 1024], op=ALU.mult)
                v.tensor_tensor(out=ksumF[:], in0=ksumF[:], in1=AR4[:, o:o + 1024], op=ALU.add)
                i = v.tensor_copy(out=AR0[:, o:o + 1024], in_=ksumF[:])
            free["psA"] = sV.inc(i)
        vec(_)
        ksum_done = sV.n
    ksum_v = ksum_done
    n2_v = stats_all(AR0, "N2K", ksum_done)

    invn_v = rsqrt_pipe("N2K", "N2K", 1.0, 1e-24, n2_v)
    invnb_v = invn_v

    def _(t, invn_v=invn_v):
        t.wait_ge(sV.h, max(invn_v, free["psA"]))
        for chp in range(16):
            i = t.transpose(out=psA[:, chp * 16:(chp + 1) * 16],
                            in_=hmr("N2K")[:, chp * 128:(chp + 1) * 128], identity=identF[32:48, 32:48])
        sP.inc(i)
    ten(_)
    ivs_pv = sP.n

    def _(v, ivs_pv=ivs_pv):
        v.wait_ge(sP.h, ivs_pv)
        i = v.tensor_copy(out=smr("INVN"), in_=psA[:, 0:256])
        free["psA"] = sV.inc(i)
    vec(_)
    invnsm_v = sV.n

    for dt in range(NDT):
        f1 = field_mm("N2K", dt, invnb_v)

        def _(v, dt=dt, f1=f1):
            v.wait_ge(sP.h, f1)
            i = v.tensor_tensor(out=AR2[:, dt * 2048:(dt + 1) * 2048],
                                in0=AR0[:, dt * 2048:(dt + 1) * 2048], in1=psA[:], op=ALU.mult)
            free["psA"] = sV.inc(i)
        vec(_)

        def _(t, dt=dt):
            t.wait_ge(sV.h, max(invnsm_v, ksum_v, free["psT"]))
            for sb in range(16):
                i = t.transpose(out=psT[:, sb * 128:(sb + 1) * 128],
                                in_=AR0[:, dt * 2048 + sb * 128: dt * 2048 + (sb + 1) * 128],
                                identity=identB[:])
            sP.inc(i)
        ten(_)
        kt_pv = sP.n

        def _(v, dt=dt, kt_pv=kt_pv):
            v.wait_ge(sP.h, kt_pv)
            dst = AR4[:].rearrange("p (t c) -> p t c", c=1024)[:, :, dt * 128:(dt + 1) * 128] \
                .rearrange("p t (h f) -> p t h f", f=64)
            src = psT[:].rearrange("p (t c) -> p t c", c=128).rearrange("p t (h f) -> p t h f", f=64)
            in1 = smr("INVN").rearrange("p (t h) -> p t h", h=16)[:, :, 2 * dt:2 * dt + 2] \
                .unsqueeze(3).broadcast_to([128, 16, 2, 64])
            i = v.tensor_tensor(out=dst, in0=src, in1=in1, op=ALU.mult)
            free["psT"] = sV.inc(i)
        vec(_)
    KD_v = KT_v = sV.n
    tap("KD", AR2[:], sV, KD_v)
    tap("KT", AR4[:], sV, KT_v)

    if stop_phase <= 7:
        stub_out()
        st_.close()
        return nc, tapnames

    KD_, QD_, KT_, VagT_ = AR2, AR1, AR4, AR3
    consts_ready = max(KD_v, KT_v, QD_v, VagT_v, sm_vv, invnsm_v, hm_vv)

    def _(v):
        i = v.memset(MTall[:], 0.0)
        sV.inc(i)
    vec(_)
    mt0_v = sV.n

    last_evac = {"RA": free["psA"], "RB": free["psA"]}
    last_evacA = {"RA": 0, "RB": 0}
    freeA = {"psT": 0}
    crep_pe = [0]
    head_state = {}
    chain_upd_v = {}
    chain_out_evac = {}
    last_upd_a = [0]

    PERS = [
        {"GT": GTp, "OlocT": OlocTp, "YKneg": YKnegp, "F0T": F0Tp},
        {"GT": wbuf[:, 0:2048], "OlocT": wbuf[:, 2048:4096],
         "YKneg": wbuf[:, 4096:6144], "F0T": wbuf[:, 6144:8192]},
    ]

    def pers(h):
        return PERS[(h // 2) % 2]

    def kd_sl(h, ch):
        return KD_[(h % 2) * 64:(h % 2) * 64 + 64,
                   (h // 2) * 2048 + ch * 64:(h // 2) * 2048 + (ch + 1) * 64]

    def qd_sl(h, ch):
        return QD_[(h % 2) * 64:(h % 2) * 64 + 64,
                   (h // 2) * 2048 + ch * 64:(h // 2) * 2048 + (ch + 1) * 64]

    def kt_sl(h, ch):
        return KT_[(ch % 2) * 64:(ch % 2) * 64 + 64,
                   (ch // 2) * 1024 + h * 64:(ch // 2) * 1024 + (h + 1) * 64]

    def x_sl(ch, j0, j1):
        return XB[(ch % 2) * 64:(ch % 2) * 64 + 64,
                  (ch // 2) * 128 + j0:(ch // 2) * 128 + j1]

    def pw_sl(buf, ch):
        return buf[(ch % 2) * 64:(ch % 2) * 64 + 64, (ch // 2) * 64:(ch // 2) * 64 + 64]

    maskUi_b = maskUi[:].unsqueeze(1).broadcast_to([128, 16, 64])
    maskUs_b = maskUs[:].unsqueeze(1).broadcast_to([128, 16, 64])
    maskLs_b = maskLs[:].unsqueeze(1).broadcast_to([128, 16, 64])

    def solve_steps(h):
        crep_av_h = [0]

        def s_crep():
            def _(s, h=h):
                s.wait_ge(sV.h, consts_ready)
                if crep_pe[0]:
                    s.wait_ge(sP.h, crep_pe[0])
                crow = hmr("C", slice(h, h + 1)).rearrange("p (c f) -> p c f", f=128)
                i = s.dma_start(out=sqV[64:65, :].rearrange("p (c f) -> p c f", f=64),
                                in_=crow[:, :, 0:64])
                sX.inc(i, 16)
                i = s.dma_start(out=sqV[65:66, :].rearrange("p (c f) -> p c f", f=64),
                                in_=crow[:, :, 64:128])
                sX.inc(i, 16)
            syn(_)
            stg_v = sX.n

            def _(t, stg_v=stg_v):
                t.wait_ge(sX.h, stg_v)
                t.wait_ge(sV.h, last_evac["RA"])
                if last_evacA["RA"]:
                    t.wait_ge(sAc.h, last_evacA["RA"])
                for cc_ in range(2):
                    i = t.matmul(out=RA[:, cc_ * 512:(cc_ + 1) * 512],
                                 lhsT=hv4F[64:66, :], rhs=sqV[64:66, cc_ * 512:(cc_ + 1) * 512],
                                 start=True, stop=True, tile_position=(64, 0))
                crep_pe[0] = sP.inc(i)
            ten(_)
            pv = sP.n

            def _(a, pv=pv):
                a.wait_ge(sP.h, pv)
                i = a.activation(out=crepB[:], in_=RA, func=AF.Copy)
                last_evacA["RA"] = sAc.inc(i)
            act(_)
            crep_av_h[0] = sAc.n
        yield s_crep

        p1_pv = [0]

        def s_p1_mm():
            def _(t, h=h):
                t.wait_ge(sV.h, max(consts_ready, last_evac["RA"], last_evac["RB"]))
                if last_evacA["RA"] or last_evacA["RB"]:
                    t.wait_ge(sAc.h, max(last_evacA["RA"], last_evacA["RB"]))
                for ch in range(NCH):
                    tp = ((h % 2) * 64, (ch % 2) * 64)
                    t.matmul(out=pw_sl(RA, ch), lhsT=kd_sl(h, ch), rhs=qd_sl(h, ch),
                             start=True, stop=True, tile_position=tp)
                    i = t.matmul(out=pw_sl(RB, ch), lhsT=kd_sl(h, ch), rhs=kd_sl(h, ch),
                                 start=True, stop=True, tile_position=tp)
                p1_pv[0] = sP.inc(i)
            ten(_)
        yield s_p1_mm

        def s_p1_mask():
            def _(v, h=h):
                v.wait_ge(sP.h, p1_pv[0])
                if crep_av_h[0]:
                    v.wait_ge(sAc.h, crep_av_h[0])
                c_sm_b = smr("C").rearrange("p (t j) -> p t j", j=16)[:, :, h:h + 1].broadcast_to([128, 16, 64])
                RAr = RA.rearrange("p (t f) -> p t f", f=64)
                RBr = RB.rearrange("p (t f) -> p t f", f=64)
                v.tensor_tensor(out=PcB[:].rearrange("p (t f) -> p t f", f=64), in0=RAr, in1=maskUi_b, op=ALU.mult)
                v.tensor_tensor(out=A1B[:].rearrange("p (t f) -> p t f", f=64), in0=RBr, in1=maskUs_b, op=ALU.mult)
                v.tensor_tensor(out=A1B[:].rearrange("p (t f) -> p t f", f=64),
                                in0=A1B[:].rearrange("p (t f) -> p t f", f=64), in1=c_sm_b, op=ALU.mult)
                v.tensor_tensor(out=B1B[:].rearrange("p (t f) -> p t f", f=64), in0=RBr, in1=maskLs_b, op=ALU.mult)
                i = v.tensor_tensor(out=B1B[:], in0=B1B[:], in1=crepB[:], op=ALU.mult)
                last_evac["RA"] = last_evac["RB"] = sV.inc(i)
            vec(_)

            def _(v, h=h):
                dstv = XB[:].rearrange("p (t f) -> p t f", f=128)[:, :, 0:64]
                dstk = XB[:].rearrange("p (t f) -> p t f", f=128)[:, :, 64:128]
                srcv = VagT_[:].rearrange("p (t f) -> p t f", f=1024)[:, :, h * 64:(h + 1) * 64]
                srck = KT_[:].rearrange("p (t f) -> p t f", f=1024)[:, :, h * 64:(h + 1) * 64]
                v.tensor_copy(out=dstv, in_=srcv)
                i = v.tensor_copy(out=dstk, in_=srck)
                sV.inc(i)
            vec(_)
        yield s_p1_mask

        state = {"x_v": None, "a_ready": None, "a_ready_a": 0,
                 "bufs": (A1B, B1B, A2B, B2B)}

        def mk_level_ax(lv):
            def s_level_ax():
                Acur, Bcur, Anxt, Bnxt = state["bufs"]
                x_v = state["x_v"] if state["x_v"] is not None else sV.n
                a_ready = state["a_ready"] if state["a_ready"] is not None else sV.n
                a_ready_a = state["a_ready_a"]

                def _(t, Acur=Acur, x_v=x_v, a_ready=a_ready, a_ready_a=a_ready_a):
                    t.wait_ge(sV.h, max(a_ready, x_v, last_evac["RA"], last_evac["RB"]))
                    wa = max(a_ready_a, last_evacA["RA"], last_evacA["RB"])
                    if wa:
                        t.wait_ge(sAc.h, wa)
                    for ch in range(NCH):
                        reg = RA if ch < 16 else RB
                        cc = ch % 16
                        out = reg[(ch % 2) * 64:(ch % 2) * 64 + 64, (cc // 2) * 128:(cc // 2) * 128 + 128]
                        i = t.matmul(out=out, lhsT=pw_sl(Acur, ch), rhs=x_sl(ch, 0, 128),
                                     start=True, stop=True, tile_position=((ch % 2) * 64, (ch % 2) * 64))
                    sP.inc(i)
                ten(_)
                pv = sP.n
                op = ALU.subtract if lv == 0 else ALU.add

                def _(v, pv=pv, op=op):
                    v.wait_ge(sP.h, pv)
                    v.tensor_tensor(out=XB[:, 0:1024], in0=XB[:, 0:1024], in1=RA, op=op)
                    i = v.tensor_tensor(out=XB[:, 1024:2048], in0=XB[:, 1024:2048], in1=RB, op=op)
                    state["x_v"] = last_evac["RA"] = last_evac["RB"] = sV.inc(i)
                vec(_)
            return s_level_ax

        def mk_level_sq(lv):
            def s_level_sq():
                Acur, Bcur, Anxt, Bnxt = state["bufs"]
                a_ready_a = state["a_ready_a"]

                def _(t, Acur=Acur, Bcur=Bcur, a_ready_a=a_ready_a):
                    t.wait_ge(sV.h, state["x_v"])
                    if a_ready_a:
                        t.wait_ge(sAc.h, a_ready_a)
                    for ch in range(NCH):
                        tp = ((ch % 2) * 64, (ch % 2) * 64)
                        t.matmul(out=pw_sl(RA, ch), lhsT=pw_sl(Acur, ch), rhs=pw_sl(Bcur, ch),
                                 start=True, stop=True, tile_position=tp)
                        i = t.matmul(out=pw_sl(RB, ch), lhsT=pw_sl(Bcur, ch), rhs=pw_sl(Acur, ch),
                                     start=True, stop=True, tile_position=tp)
                    sP.inc(i)
                ten(_)
                pv2 = sP.n

                def _(a, pv2=pv2, Bnxt=Bnxt):
                    a.wait_ge(sP.h, pv2)
                    i = a.activation(out=Bnxt[:], in_=RA, func=AF.Copy)
                    state["a_ready_a"] = last_evacA["RA"] = sAc.inc(i)
                act(_)

                def _(v, pv2=pv2, Anxt=Anxt):
                    v.wait_ge(sP.h, pv2)
                    i = v.tensor_copy(out=Anxt[:], in_=RB)
                    state["a_ready"] = last_evac["RB"] = sV.inc(i)
                vec(_)
                state["bufs"] = (Anxt, Bnxt, Acur, Bcur)
            return s_level_sq

        for lv in range(6):
            yield mk_level_ax(lv)
            if lv < 5:
                yield mk_level_sq(lv)

        p4h = {}

        def s_p4a():
            if p4_parts < 1:
                return
            def _(v, h=h):
                v.wait_ge(sV.h, state["x_v"])
                c_b = smr("C").rearrange("p (t j) -> p t j", j=16)[:, :, h:h + 1].broadcast_to([128, 16, 128])
                i = v.tensor_tensor(out=XB[:].rearrange("p (t f) -> p t f", f=128),
                                    in0=XB[:].rearrange("p (t f) -> p t f", f=128), in1=c_b, op=ALU.mult)
                sV.inc(i)
            vec(_)
            xc_v = sV.n
            if p4_parts < 2:
                return

            def _(t, xc_v=xc_v, h=h):
                t.wait_ge(sV.h, max(xc_v, last_evac["RA"], last_evac["RB"]))
                if last_evacA["RA"] or last_evacA["RB"]:
                    t.wait_ge(sAc.h, max(last_evacA["RA"], last_evacA["RB"]))
                for ch in range(NCH):
                    tp = ((ch % 2) * 64, (ch % 2) * 64)
                    t.matmul(out=pw_sl(RA, ch), lhsT=x_sl(ch, 0, 64), rhs=pw_sl(PcB, ch),
                             start=True, stop=True, tile_position=tp)
                    i = t.matmul(out=pw_sl(RB, ch), lhsT=x_sl(ch, 64, 128), rhs=pw_sl(PcB, ch),
                                 start=True, stop=True, tile_position=tp)
                p4h["pv"] = sP.inc(i)
            ten(_)

            psTF = psT[:].bitcast(F32)
            p4h["psTF"] = psTF

            def _(t, h=h):
                t.wait_ge(sV.h, free["psT"])
                if freeA["psT"]:
                    t.wait_ge(sAc.h, freeA["psT"])
                for ch in range(NCH):
                    tp = ((ch % 2) * 64, (ch % 2) * 64)
                    i = t.matmul(out=pw_sl(psTF, ch), lhsT=x_sl(ch, 64, 128), rhs=kt_sl(h, ch),
                                 start=True, stop=True, tile_position=tp)
                p4h["pvYK"] = sP.inc(i)
            ten(_)
        yield s_p4a

        def s_p4():
            if p4_parts < 3:
                return
            pv = p4h["pv"]
            pvYK = p4h["pvYK"]
            psTF = p4h["psTF"]

            def _(a, pvYK=pvYK):
                a.wait_ge(sP.h, pvYK)
                i = a.activation(out=hstg[:], in_=psTF, func=AF.Copy, scale=-1.0)
                sAc.inc(i)
            act(_)
            yk_a = sAc.n

            def _(g, yk_a=yk_a, h=h):
                g.wait_ge(sAc.h, yk_a)
                hh = (h % 2) * 64
                for par in range(2):
                    src_y = hstg[par * 64:par * 64 + 64, :].rearrange("p (c f) -> p c f", f=64)
                    dst_y = pers(h)["YKneg"][hh:hh + 64, :].rearrange("p (c f) -> p c f", f=64)[:, par::2, :]
                    i = g.dma_start(out=dst_y, in_=src_y)
                    sD.inc(i, 16)
            gps(_)
            rpy_v = sD.n

            def _(t, yk_a=yk_a, h=h):
                t.wait_ge(sAc.h, yk_a)
                for ch in range(NCH):
                    tp = ((ch % 2) * 64, (ch % 2) * 64)
                    i = t.matmul(out=pw_sl(psTF, ch), lhsT=kt_sl(h, ch), rhs=x_sl(ch, 0, 64),
                                 start=True, stop=True, tile_position=tp)
                sP.inc(i)
            ten(_)
            pvF0 = sP.n

            def _(a, pvF0=pvF0, rpy_v=rpy_v):
                a.wait_ge(sP.h, pvF0)
                a.wait_ge(sD.h, rpy_v)
                i = a.activation(out=hstg[:], in_=psTF, func=AF.Copy)
                freeA["psT"] = sAc.inc(i)
            act(_)
            f0_a = sAc.n

            def _(g, f0_a=f0_a, h=h):
                g.wait_ge(sAc.h, f0_a)
                hh = (h % 2) * 64
                for par in range(2):
                    src_f = hstg[par * 64:par * 64 + 64, :].rearrange("p (c f) -> p c f", f=64)
                    dst_f = pers(h)["F0T"][hh:hh + 64, :].rearrange("p (c f) -> p c f", f=64)[:, par::2, :]
                    i = g.dma_start(out=dst_f, in_=src_f)
                    sD.inc(i, 16)
            gps(_)
            rpf_v = sD.n

            def _(a, pvF0=pvF0):
                a.wait_ge(sP.h, pvF0)
                a.activation(out=XB[:, 0:1024], in_=RA, func=AF.Copy)
                i = a.activation(out=XB[:, 1024:2048], in_=RB, func=AF.Copy)
                last_evacA["RA"] = last_evacA["RB"] = sAc.inc(i)
            act(_)
            oz_a = sAc.n

            def _(g, oz_a=oz_a, h=h):
                g.wait_ge(sAc.h, oz_a)
                hh = (h % 2) * 64
                for par in range(2):
                    src_o = XB[par * 64:par * 64 + 64, 0:1024].rearrange("p (c f) -> p c f", f=64)
                    dst_o = pers(h)["OlocT"][hh:hh + 64, :].rearrange("p (c f) -> p c f", f=64)[:, par::2, :]
                    i = g.dma_start(out=dst_o, in_=src_o)
                    sD.inc(i, 16)
                    src_z = XB[par * 64:par * 64 + 64, 1024:2048].rearrange("p (c f) -> p c f", f=64)
                    dst_z = pers(h)["GT"][hh:hh + 64, :].rearrange("p (c f) -> p c f", f=64)[:, par::2, :]
                    i = g.dma_start(out=dst_z, in_=src_z)
                    sD.inc(i, 16)
            gps(_)
            rp1_v = sD.n

            def _(v, rp1_v=rp1_v, h=h):
                v.wait_ge(sD.h, rp1_v)
                hh = (h % 2) * 64
                qsl = QD_[hh:hh + 64, (h // 2) * 2048:(h // 2) * 2048 + 2048]
                i = v.tensor_tensor(out=pers(h)["GT"][hh:hh + 64, :], in0=qsl,
                                    in1=pers(h)["GT"][hh:hh + 64, :], op=ALU.subtract)
                sV.inc(i)
            vec(_)
            gt_v = sV.n
            if p4_parts < 4:
                return

            def _(v, rp1_v=rp1_v, h=h):
                v.wait_ge(sD.h, rp1_v)
                i = v.nop()
                last_evac["RA"] = max(last_evac["RA"], sV.inc(i))
                head_state[h] = sV.n
            vec(_)
        yield s_p4

    def chain_step(h, ch):
        hh = (h % 2) * 64
        hp = h // 2
        P = pers(h)
        mt = MTall[hh:hh + 64, hp * 64:(hp + 1) * 64]
        slot_o = ROUT[hh:hh + 64, (ch % 8) * 64:(ch % 8) * 64 + 64]
        slot_u = RUPD[hh:hh + 64, (ch % 2) * 64:(ch % 2) * 64 + 64]

        def _(t, h=h, ch=ch, P=P):
            if ch == 0:
                t.wait_ge(sV.h, max(head_state[h], mt0_v))
                if h >= 2:
                    t.wait_ge(sV.h, chain_out_evac[(h - 2, 3)])
                    ev, val = chain_upd_v[(h - 2, NCH - 1)]
                    t.wait_ge(ev.h, val)
            else:
                ev, val = chain_upd_v[(h, ch - 1)]
                t.wait_ge(ev.h, val)
            if ch >= 8:
                t.wait_ge(sV.h, chain_out_evac[(h, ch // 8 - 1)])
            t.matmul(out=slot_o, lhsT=mt, rhs=P["GT"][hh:hh + 64, ch * 64:(ch + 1) * 64],
                     start=True, stop=True, tile_position=(hh, hh))
            t.matmul(out=slot_u, lhsT=P["YKneg"][hh:hh + 64, ch * 64:(ch + 1) * 64],
                     rhs=mt, start=True, stop=False, tile_position=(hh, hh))
            t.matmul(out=slot_u, lhsT=identB[hh:hh + 64, hh:hh + 64],
                     rhs=mt, start=False, stop=False, tile_position=(hh, hh))
            i = t.matmul(out=slot_u, lhsT=identB[hh:hh + 64, hh:hh + 64],
                         rhs=P["F0T"][hh:hh + 64, ch * 64:(ch + 1) * 64],
                         start=False, stop=True, tile_position=(hh, hh))
            sP.inc(i)
        ten(_)
        pv = sP.n
        gc = gCrep2[hh:hh + 64, hp * 32 + ch: hp * 32 + ch + 1]

        if False:
            def _(a, pv=pv, gc=gc):
                a.wait_ge(sP.h, pv)
                i = a.activation(out=mt, in_=slot_u, func=AF.Copy, scale=gc)
                last_upd_a[0] = sAc.inc(i)
                chain_upd_v[(h, ch)] = (sAc, last_upd_a[0])
            act(_)
            if ch % 8 == 7:
                def _(v, h=h, ch=ch, pv=pv, P=P):
                    v.wait_ge(sP.h, pv)
                    base = ch - 7
                    dst = AR1[hh:hh + 64, hp * 2048 + base * 64: hp * 2048 + (base + 8) * 64]
                    i = v.tensor_tensor(out=dst, in0=ROUT[hh:hh + 64, :],
                                        in1=P["OlocT"][hh:hh + 64, base * 64:(base + 8) * 64],
                                        op=ALU.add)
                    chain_out_evac[(h, ch // 8)] = sV.inc(i)
                vec(_)
        else:
            def _(v, h=h, ch=ch, pv=pv, gc=gc, P=P):
                v.wait_ge(sP.h, pv)
                i = v.tensor_scalar(out=mt, in0=slot_u, scalar1=gc, scalar2=None, op0=ALU.mult)
                chain_upd_v[(h, ch)] = (sV, sV.inc(i))
                if ch % 8 == 7:
                    base = ch - 7
                    dst = AR1[hh:hh + 64, hp * 2048 + base * 64: hp * 2048 + (base + 8) * 64]
                    i = v.tensor_tensor(out=dst, in0=ROUT[hh:hh + 64, :],
                                        in1=P["OlocT"][hh:hh + 64, base * 64:(base + 8) * 64],
                                        op=ALU.add)
                    chain_out_evac[(h, ch // 8)] = sV.inc(i)
            vec(_)

    def _interleave(a, b):
        out = []
        for i in range(max(len(a), len(b))):
            if i < len(a):
                out.append(a[i])
            if i < len(b):
                out.append(b[i])
        return out

    for h in range(solve_heads):
        steps = list(solve_steps(h))[:solve_nsteps]
        tasks = []
        if do_chain:
            a = [(h - 2, c) for c in range(16, NCH)] if h >= 2 else []
            b = [(h - 1, c) for c in range(0, 16)] if h >= 1 else []
            tasks = _interleave(a, b)
        per = (len(tasks) + len(steps) - 1) // len(steps) if tasks else 0
        ti = 0
        for s in steps:
            s()
            for _k in range(min(per, len(tasks) - ti)):
                chain_step(*tasks[ti])
                ti += 1
        while ti < len(tasks):
            chain_step(*tasks[ti])
            ti += 1
    if do_chain:
        H_ = solve_heads
        a = [(H_ - 2, c) for c in range(16, NCH)] if H_ >= 2 else []
        b = [(H_ - 1, c) for c in range(0, 16)] if H_ >= 1 else []
        for t_ in _interleave(a, b) + [(H_ - 1, c) for c in range(16, NCH)]:
            chain_step(*t_)
    attn_mix_v = sV.n
    tap("GTp", GTp[:], sV, attn_mix_v)
    tap("OlocTp", OlocTp[:], sV, attn_mix_v)
    tap("YKnegp", YKnegp[:], sV, attn_mix_v)
    tap("F0Tp", F0Tp[:], sV, attn_mix_v)
    if stop_phase <= 7.5:
        stub_out()
        st_.close()
        return nc, tapnames
    tap("outG", AR1[:], sV, attn_mix_v)
    tap("MTall", MTall[:], sV, attn_mix_v)
    tap("gCrep2end", gCrep2[:], sV, attn_mix_v)

    if stop_phase <= 8:
        stub_out()
        st_.close()
        return nc, tapnames

    sq_pe = [0]
    ro_pv = 0
    for dt in range(NDT):
        def _(v, dt=dt, attn_mix_v=attn_mix_v):
            v.wait_ge(sV.h, attn_mix_v)
            if sq_pe[0]:
                v.wait_ge(sP.h, sq_pe[0])
            i = v.tensor_tensor(out=sqB[:], in0=AR1[:, dt * 2048:(dt + 1) * 2048],
                                in1=AR1[:, dt * 2048:(dt + 1) * 2048], op=ALU.mult)
            sV.inc(i)
        vec(_)
        sqv = sV.n

        def _(t, dt=dt, sqv=sqv):
            t.wait_ge(sV.h, max(sqv, ones_v))
            if dt == 0:
                t.wait_ge(sV.h, max(last_evac["RA"], last_evac["RB"]))
                if last_evacA["RA"] or last_evacA["RB"]:
                    t.wait_ge(sAc.h, max(last_evacA["RA"], last_evacA["RB"]))
            for sc in range(4):
                i = t.matmul(out=psA[0:1, sc * 512:(sc + 1) * 512], lhsT=onescol[:],
                             rhs=sqB[:, sc * 512:(sc + 1) * 512],
                             start=(dt == 0), stop=(dt == NDT - 1))
            sq_pe[0] = sP.inc(i)
        ten(_)
        ro_pv = sP.n

    def _(v, ro_pv=ro_pv):
        v.wait_ge(sP.h, ro_pv)
        i = v.tensor_scalar(out=psA[0:1, :], in0=psA[0:1, :], scalar1=1.0 / D, scalar2=EPS_RMS,
                            op0=ALU.mult, op1=ALU.add)
        sV.inc(i)
    vec(_)
    ro1_v = sV.n

    def _(a, ro1_v=ro1_v):
        a.wait_ge(sV.h, ro1_v)
        a.activation(out=psA[32:33, :], in_=psA[0:1, :], func=AF.Ln)
        i = a.activation(out=sqB[0:1, 0:2048], in_=psA[32:33, :], func=AF.Exp, scale=-0.5)
        sAc.inc(i)
    act(_)
    ro2_v = sAc.n

    def _(v, ro2_v=ro2_v):
        v.wait_ge(sAc.h, ro2_v)
        i = v.nop()
        free["psA"] = sV.inc(i)
    vec(_)
    ro_v = sV.n

    mixn_prev = [ro_v]
    for dt in range(NDT):
        def _(t, dt=dt):
            t.wait_ge(sV.h, mixn_prev[0])
            for sc in range(4):
                i = t.matmul(out=psA[:, sc * 512:(sc + 1) * 512], lhsT=onesONE[:],
                             rhs=sqB[0:1, sc * 512:(sc + 1) * 512], start=True, stop=True)
            sP.inc(i)
        ten(_)
        rof_pv = sP.n

        def _(v, dt=dt, rof_pv=rof_pv):
            v.wait_ge(sP.h, rof_pv)
            i = v.tensor_tensor(out=AR1[:, dt * 2048:(dt + 1) * 2048],
                                in0=AR1[:, dt * 2048:(dt + 1) * 2048], in1=psA[:], op=ALU.mult)
            mixn_prev[0] = free["psA"] = sV.inc(i)
        vec(_)
    mixn_v = sV.n
    tap("mixn", AR1[:], sV, mixn_v)

    if stop_phase <= 9:
        stub_out()
        st_.close()
        return nc, tapnames

    def _(s, mixn_v=mixn_v):
        s.wait_ge(sV.h, mixn_v)
        i = s.dma_start(out=AR0[:], in_=xnT_scr[:])
        sX.inc(i, 16)
    syn(_)
    xnT2_v = sX.n

    hid_ar = [AR2, AR3, AR4]
    hidmm_done = {}
    silu_done = {}
    evac_done = {}
    psTF2 = psT[:].bitcast(F32)
    hid_v = 0
    for ht in range(NHT):
        wgt_b = wgt if ht % 2 == 0 else wgt2
        wut_b = wutl if ht % 2 == 0 else wutl2

        def _(g, ht=ht, wgt_b=wgt_b, wut_b=wut_b, mixn_v=mixn_v):
            if ht < 2:
                g.wait_ge(sV.h, mixn_v)
            if ht >= 2:
                g.wait_ge(sP.h, hidmm_done.get(ht - 2, 0))
            i = g.dma_start(out=wgt_b, in_=w["wgT"][ht * 128:(ht + 1) * 128, :])
            sD.inc(i, 16)
            i = g.dma_start(out=wut_b, in_=w["wuT"][ht * 128:(ht + 1) * 128, :])
            sD.inc(i, 16)
        gps(_)
        wl_v = sD.n

        def _(t, wl_v=wl_v, ht=ht, wgt_b=wgt_b):
            t.wait_ge(sD.h, wl_v)
            if ht == 0:
                t.wait_ge(sV.h, free["psA"])
                t.wait_ge(sX.h, xnT2_v)
            else:
                t.wait_ge(sAc.h, silu_done[ht - 1])
            for sc in range(4):
                for kt in range(NDT):
                    i = t.matmul(out=psA[:, sc * 512:(sc + 1) * 512],
                                 lhsT=wgt_b[:, kt * 128:(kt + 1) * 128],
                                 rhs=AR0[:, kt * 2048 + sc * 512: kt * 2048 + sc * 512 + 512],
                                 start=(kt == 0), stop=(kt == NDT - 1))
            sP.inc(i)
        ten(_)
        g_pv = sP.n

        def _(a, g_pv=g_pv, ht=ht):
            a.wait_ge(sP.h, g_pv)
            if ht >= 1:
                a.wait_ge(sV.h, evac_done[ht - 1])
            i = a.activation(out=siluB[:], in_=psA[:], func=AF.Silu, scale=SCALE)
            silu_done[ht] = sAc.inc(i)
        act(_)

        def _(t, ht=ht, wut_b=wut_b, mixn_v=mixn_v):
            if ht == 0:
                t.wait_ge(sV.h, mixn_v)
                t.wait_ge(sAc.h, last_upd_a[0])
            else:
                t.wait_ge(sV.h, evac_done[ht - 1])
            for sc in range(4):
                reg = psB[:, (sc % 2) * 512:(sc % 2) * 512 + 512] if sc < 2 \
                    else psTF2[:, (sc % 2) * 512:(sc % 2) * 512 + 512]
                for kt in range(NDT):
                    i = t.matmul(out=reg,
                                 lhsT=wut_b[:, kt * 128:(kt + 1) * 128],
                                 rhs=AR0[:, kt * 2048 + sc * 512: kt * 2048 + sc * 512 + 512],
                                 start=(kt == 0), stop=(kt == NDT - 1))
            hidmm_done[ht] = sP.inc(i)
        ten(_)
        u_pv = sP.n

        def _(v, u_pv=u_pv, ht=ht):
            v.wait_ge(sP.h, u_pv)
            v.wait_ge(sAc.h, silu_done[ht])
            dst = hid_ar[ht // 8][:, (ht % 8) * 2048:((ht % 8) + 1) * 2048]
            v.tensor_tensor(out=dst[:, 0:1024], in0=siluB[:, 0:1024], in1=psB[:], op=ALU.mult)
            i = v.tensor_tensor(out=dst[:, 1024:2048], in0=siluB[:, 1024:2048],
                                in1=psTF2[:], op=ALU.mult)
            evac_done[ht] = free["psA"] = sV.inc(i)
        vec(_)
        hid_v = sV.n

    tap("xnT2", AR0[:], sX, xnT2_v)
    tap("hidden0", AR2[:], sV, hid_v)
    tap("hidden1", AR3[:], sV, hid_v)
    tap("hidden2", AR4[:], sV, hid_v)

    if stop_phase <= 10:
        stub_out()
        st_.close()
        return nc, tapnames

    def _(g, hid_v=hid_v):
        g.wait_ge(sV.h, hid_v)
        i = g.dma_start(out=wbuf[:].rearrange("p (t c) -> p t c", c=D),
                        in_=w["Wo"][:].rearrange("(t p) c -> p t c", p=128))
        sD.inc(i, 16)
    gps(_)
    wo_v = sD.n

    def _(v, wo_v=wo_v):
        v.wait_ge(sD.h, wo_v)
        in1 = onwP[:].unsqueeze(2).broadcast_to([128, NDT, D])
        i = v.tensor_tensor(out=wbuf[:].rearrange("p (t c) -> p t c", c=D),
                            in0=wbuf[:].rearrange("p (t c) -> p t c", c=D), in1=in1, op=ALU.mult)
        sV.inc(i)
    vec(_)
    wofold_v = sV.n

    y_evac = {}
    out_dma = {}
    ymm_last = {}
    for dmac in range(2):
        def _(g, dmac=dmac, hid_v=hid_v):
            g.wait_ge(sV.h, hid_v)
            if dmac == 1:
                g.wait_ge(sP.h, ymm_last[0])
            i = g.dma_start(out=wdTm, in_=w["wdT"][dmac * 128:(dmac + 1) * 128, :])
            sD.inc(i, 16)
        gps(_)
        wdl_v = sD.n

        def _(a, wdl_v=wdl_v):
            a.wait_ge(sD.h, wdl_v)
            i = a.activation(out=wdTm, in_=wdTm, func=AF.Copy, scale=SCALE * SCALE)
            sAc.inc(i)
        act(_)
        wdT_a = sAc.n

        for sti in range(NST):
            slot = sti % 2
            yb = ybuf if slot == 0 else ybuf2
            xb = xfin if slot == 0 else xfin2

            def _(s, sti=sti, dmac=dmac, xb=xb, hid_v=hid_v):
                if dmac == 0 and sti < 2:
                    s.wait_ge(sV.h, hid_v)
                key = (dmac, sti - 2) if sti >= 2 else ((dmac - 1, NST - 2 + sti) if dmac else None)
                if key in y_evac:
                    s.wait_ge(sV.h, y_evac[key])
                i = s.dma_start(out=xb, in_=x_h[sti * 128:(sti + 1) * 128,
                                                dmac * 512:(dmac + 1) * 512])
                sX.inc(i, 16)
            syn(_)
            x_v = sX.n

            def _(t, sti=sti, dmac=dmac, slot=slot, wdT_a=wdT_a):
                t.wait_ge(sAc.h, wdT_a)
                if dmac == 0 and sti == 0:
                    t.wait_ge(sV.h, max(wofold_v, hid_v, mixn_v))
                key = (dmac, sti - 2) if sti >= 2 else ((dmac - 1, NST - 2 + sti) if dmac else None)
                if key in y_evac:
                    t.wait_ge(sV.h, y_evac[key])
                reg = psB[:, slot * 512:(slot + 1) * 512]
                for kt in range(NDT):
                    t.matmul(out=reg,
                             lhsT=AR1[:, kt * 2048 + sti * 128: kt * 2048 + (sti + 1) * 128],
                             rhs=wbuf[:, kt * 1024 + dmac * 512: kt * 1024 + (dmac + 1) * 512],
                             start=(kt == 0), stop=False)
                for b in range(NHT):
                    i = t.matmul(out=reg,
                                 lhsT=hid_ar[b // 8][:, (b % 8) * 2048 + sti * 128:
                                                     (b % 8) * 2048 + (sti + 1) * 128],
                                 rhs=wdTm[:, b * 512:(b + 1) * 512],
                                 start=False, stop=(b == NHT - 1))
                ymm_last[dmac] = sP.inc(i)
            ten(_)
            ymm_pv = sP.n

            def _(v, ymm_pv=ymm_pv, x_v=x_v, slot=slot, yb=yb, xb=xb, dmac=dmac, sti=sti):
                v.wait_ge(sP.h, ymm_pv)
                v.wait_ge(sX.h, x_v)
                key = (dmac, sti - 2) if sti >= 2 else ((dmac - 1, NST - 2 + sti) if dmac else None)
                if key in out_dma:
                    v.wait_ge(sO.h, out_dma[key])
                reg = psB[:, slot * 512:(slot + 1) * 512]
                i = v.tensor_tensor(out=yb, in0=reg, in1=xb, op=ALU.add)
                y_evac[(dmac, sti)] = sV.inc(i)
            vec(_)
            ye_v = sV.n

            def _(s, sti=sti, dmac=dmac, ye_v=ye_v, yb=yb):
                s.wait_ge(sV.h, ye_v)
                i = s.dma_start(out=y_h[sti * 128:(sti + 1) * 128, dmac * 512:(dmac + 1) * 512],
                                in_=yb)
                out_dma[(dmac, sti)] = sO.inc(i, 16)
            syn(_)

    total_out = sO.n

    def _(g, total_out=total_out):
        g.wait_ge(sO.h, total_out)
    gps(_)

    st_.close()
    return nc, tapnames


_cache = {}


def _get_nc(tapkey=()):
    if tapkey not in _cache:
        _cache[tapkey] = _build([(t, list(s), d) for t, s, d in tapkey])
    return _cache[tapkey]


def _host_inputs(inputs, b):
    f = lambda k: np.ascontiguousarray(np.asarray(inputs[k], dtype=np.float32))
    Wrq = np.asarray(inputs["L0_attn_Wrq"], np.float32)
    Wrk = np.asarray(inputs["L0_attn_Wrk"], np.float32)
    Wrq_p = np.zeros((D, 1024), np.float32)
    Wrk_p = np.zeros((D, 1024), np.float32)
    for h in range(H):
        Wrq_p[:, h * 64:h * 64 + RR] = Wrq[:, h * RR:(h + 1) * RR]
        Wrk_p[:, h * 64:h * 64 + RR] = Wrk[:, h * RR:(h + 1) * RR]
    Wfl = np.concatenate([np.asarray(inputs["L0_attn_Wf"], np.float32),
                          np.asarray(inputs["L0_attn_Wl"], np.float32)], axis=1)
    bfl = np.concatenate([np.asarray(inputs["L0_attn_bf"], np.float32),
                          np.asarray(inputs["L0_attn_bl"], np.float32)])[None, :]

    def cdiag_of(name):
        cw = np.asarray(inputs[name], np.float32)
        out = np.zeros((128, NDT, 4, 32), np.float32)
        for d in range(D):
            p, dt = d % 128, d // 128
            out[p, dt, :, p % 32] = cw[d, :]
        return np.ascontiguousarray(out.reshape(128, NDT * 4 * 32))

    onw = np.ascontiguousarray(np.asarray(inputs["L0_attn_on_w"], np.float32).reshape(NDT, 128).T)
    ident = np.eye(128, dtype=np.float32)
    t = np.arange(64)
    maskUi = np.ascontiguousarray(np.tile((t[:, None] <= t[None, :]).astype(np.float32), (2, 1)))
    maskUs = np.ascontiguousarray(np.tile((t[:, None] < t[None, :]).astype(np.float32), (2, 1)))
    maskLs = np.ascontiguousarray(np.tile((t[:, None] > t[None, :]).astype(np.float32), (2, 1)))
    hv4 = np.zeros((128, 128), np.float32)
    for bb_ in range(0, 128, 32):
        hv4[bb_, 0:64] = 1.0
        hv4[bb_ + 1, 64:128] = 1.0
    hv4[64, :] = 0.0
    hv4[65, :] = 0.0
    hv4[64, 0:64] = 1.0
    hv4[65, 64:128] = 1.0
    bonesAll = np.zeros((128, 8, 32), np.float32)
    for dt_ in range(8):
        bonesAll[0:64, dt_, 2 * dt_] = 1.0
        bonesAll[64:128, dt_, 2 * dt_ + 1] = 1.0
    bonesAll = np.ascontiguousarray(bonesAll.reshape(128, 256))
    p = np.arange(128)
    hsl = np.arange(16)
    selh = ((p[None, :] >= 64).astype(np.int32) == (hsl[:, None] % 2)).astype(np.float32)
    hseldt = (np.arange(8)[None, :] == (hsl[:, None] // 2)).astype(np.float32)

    return {
        "x": np.ascontiguousarray(np.asarray(inputs["L0_attn_x"], np.float32)[b]),
        "Wq": f("L0_attn_Wq"), "Wk": f("L0_attn_Wk"), "Wv": f("L0_attn_Wv"),
        "Wo": f("L0_attn_Wo"),
        "Wrq": Wrq_p, "Wrk": Wrk_p, "Wfl": np.ascontiguousarray(Wfl),
        "bfl_f": np.ascontiguousarray(bfl.reshape(32, 1)[0:16]),
        "bfl_l": np.ascontiguousarray(bfl.reshape(32, 1)[16:32]),
        "cdq": cdiag_of("L0_attn_conv_q"), "cdk": cdiag_of("L0_attn_conv_k"),
        "cdv": cdiag_of("L0_attn_conv_v"),
        "gs": np.ascontiguousarray(np.asarray(inputs["L0_attn_graph_scale"], np.float32).reshape(16, 1)),
        "qnw": np.ascontiguousarray(np.asarray(inputs["L0_attn_qn_w"], np.float32).reshape(64, 1)),
        "knw": np.ascontiguousarray(np.asarray(inputs["L0_attn_kn_w"], np.float32).reshape(64, 1)),
        "onw": onw,
        "wgT": np.ascontiguousarray(
            np.asarray(inputs["L0_ffn_w_gate"], np.int32)
            .reshape(24, 128, 8, 128).transpose(0, 3, 2, 1).reshape(3072, 1024)),
        "wuT": np.ascontiguousarray(
            np.asarray(inputs["L0_ffn_w_up"], np.int32)
            .reshape(24, 128, 8, 128).transpose(0, 3, 2, 1).reshape(3072, 1024)),
        "wdT": np.ascontiguousarray(
            np.asarray(inputs["L0_ffn_w_down"], np.int32)
            .reshape(2, 512, 24, 128).transpose(0, 3, 2, 1).reshape(256, 12288)),
        "ident": ident, "maskUi": maskUi, "maskUs": maskUs, "maskLs": maskLs,
        "hv4": hv4, "bonesAll": bonesAll,
        "onesone": np.ones((1, 128), np.float32),
        "selh": np.ascontiguousarray(selh), "hsel": np.ascontiguousarray(hseldt),
    }


def _run(inputs, taps=()):
    tapkey = tuple((t, tuple(s), d) for t, s, d in taps)
    nc, tapnames = _get_nc(tapkey)
    in_maps = [_host_inputs(inputs, b) for b in range(8)]
    res = run_bass_kernel_spmd(nc, in_maps, core_ids=list(range(8)))
    y = np.stack([r["y"] for r in res.results], axis=0)
    tapd = [{t: r["tap_" + t] for t in tapnames} for r in res.results]
    return y, tapd


def kernel(**inputs):
    y, _ = _run(inputs)
    return y


# version 10
# speedup vs baseline: 1.4089x; 1.0725x over previous
import contextlib
import numpy as np
import concourse.bass as bass
import concourse.mybir as mybir
from concourse.bass_utils import run_bass_kernel_spmd

F32 = mybir.dt.float32
BF16 = mybir.dt.bfloat16
I32 = mybir.dt.int32
AF = mybir.ActivationFunctionType
ALU = mybir.AluOpType
AXX = mybir.AxisListType.X

S, D, H, DH, RR, C = 2048, 1024, 16, 64, 48, 64
NST, NDT, NCH, NCP = 16, 8, 32, 16
HID, NHT = 3072, 24
EPS_LN, EPS_RMS = 1e-5, 1e-6
SCALE = 1.0 / 127.0


class SemT:
    def __init__(self, h):
        self.h = h
        self.n = 0

    def inc(self, instr, k=1):
        instr.then_inc(self.h, k)
        self.n += k
        return self.n


def _build(taps=(), stop_phase=99, solve_heads=H, do_chain=True, solve_nsteps=99, p4_parts=99):
    nc = bass.Bass("TRN2", target_bir_lowering=False)
    taps = list(taps)

    x_h = nc.dram_tensor("x", [S, D], F32, kind="ExternalInput").ap()
    w = {}
    for nm, shp, dt in [
        ("Wq", [D, D], F32), ("Wk", [D, D], F32), ("Wv", [D, D], F32), ("Wo", [D, D], F32),
        ("Wrq", [D, D], F32), ("Wrk", [D, D], F32),
        ("Wfl", [D, 32], F32), ("bfl_f", [16, 1], F32), ("bfl_l", [16, 1], F32),
        ("cdq", [128, NDT * 4 * 32], F32), ("cdk", [128, NDT * 4 * 32], F32),
        ("cdv", [128, NDT * 4 * 32], F32),
        ("gs", [16, 1], F32), ("qnw", [64, 1], F32), ("knw", [64, 1], F32),
        ("onw", [128, NDT], F32),
        ("wgT", [HID, D], I32), ("wuT", [HID, D], I32), ("wdT", [256, 12288], I32),
        ("ident", [128, 128], F32),
        ("maskUi", [128, 64], F32), ("maskUs", [128, 64], F32), ("maskLs", [128, 64], F32),
        ("hv4", [128, 128], F32), ("bonesAll", [128, 8 * 32], F32),
        ("onesone", [1, 128], F32),
        ("selh", [16, 128], F32), ("hsel", [16, 8], F32),
    ]:
        w[nm] = nc.dram_tensor(nm, shp, dt, kind="ExternalInput").ap()
    y_h = nc.dram_tensor("y", [S, D], F32, kind="ExternalOutput").ap()
    xnT_scr = nc.dram_tensor("xnT_scr", [128, NDT * S], BF16, kind="Internal").ap()
    tap_h = {t: nc.dram_tensor("tap_" + t, shp, dt, kind="ExternalOutput").ap()
             for t, shp, dt in taps}
    tapnames = [t for t, _, _ in taps]

    st_ = contextlib.ExitStack()
    SB = lambda n, s, d: st_.enter_context(nc.sbuf_tensor(n, s, d))

    AR0 = SB("AR0", [128, NDT * S], BF16)
    AR1 = SB("AR1", [128, NDT * S], BF16)
    AR2 = SB("AR2", [128, NDT * S], BF16)
    AR3 = SB("AR3", [128, NST * D], BF16)
    AR4 = SB("AR4", [128, NST * D], BF16)
    scrB = SB("scrB", [128, 2048], BF16)
    ksumB = SB("ksumB", [128, 2048], BF16)
    sqB = SB("sqB", [128, 2048], BF16)
    preB = sqB
    crepRow = sqB

    wbuf = SB("wbuf", [128, NDT * D], BF16)
    wflT = SB("wflT", [128, NDT * 32], BF16)
    bflP = SB("bflP", [48, 1], F32)
    bfP = bflP[0:16, :]
    blP = bflP[32:48, :]
    onesONE = SB("onesONE", [1, 128], BF16)
    onescol = SB("onescol", [128, 1], BF16)
    cdiag = SB("cdiag", [128, 2 * 128], BF16)
    identB = SB("identB", [128, 128], BF16)
    identF = SB("identF", [128, 128], F32)
    maskUi = SB("maskUi_s", [128, 64], F32)
    maskUs = SB("maskUs_s", [128, 64], F32)
    maskLs = SB("maskLs_s", [128, 64], F32)
    hv4B = SB("hv4B", [128, 128], BF16)
    hv4F = SB("hv4F", [128, 128], F32)
    bonesAll = SB("bonesAll_s", [128, 8 * 32], BF16)
    selhB = SB("selhB", [16, 128], BF16)
    hselB = SB("hselB", [16, 8], F32)
    hselL = SB("hselL", [16, 8], BF16)
    qnwB = SB("qnwB", [128, 1], F32)
    knwB = SB("knwB", [128, 1], F32)
    lamB = SB("lamB", [16, 1], F32)
    lamrep = SB("lamrep", [128, NDT], F32)
    onwP = SB("onwP", [128, NDT], F32)

    hm = SB("hm", [128, 2048], F32)
    hstg = SB("hstg", [128, 1024], BF16)
    sm = SB("sm", [128, 3 * 256], BF16)
    gCrep2 = SB("gCrep2", [128, 8 * 32], F32)
    stat1 = SB("stat1", [128, 1], F32)
    stat2 = SB("stat2", [128, 1], F32)
    stat3 = SB("stat3", [128, 1], F32)
    stat4 = SB("stat4", [128, 1], F32)
    MTall = SB("MTall", [128, 8 * 64], BF16)

    scrV = scrB[:].bitcast(F32)
    ksumV = ksumB[:].bitcast(F32)
    sqV = sqB[:].bitcast(F32)
    xin = ksumV
    xin2 = scrV
    xnf = ksumV
    scrF = scrV
    ksumF = ksumV
    wgt = ksumB[:, 0:1024]
    wutl = ksumB[:, 1024:2048]
    wgt2 = scrB[:, 0:1024]
    wutl2 = scrB[:, 1024:2048]
    siluB = preB
    wdTm = AR0[:, 0:NHT * 512]
    wdrow = AR0[:, NHT * 512:NHT * 512 + 3072]
    ybuf = sqV[:, 0:512]
    ybuf2 = sqV[:, 512:1024]
    xfin = ksumV[:, 0:512]
    xfin2 = ksumV[:, 512:1024]
    woB = wbuf
    PcB = AR0[:, 0:1024]
    A1B = AR0[:, 1024:2048]
    A2B = AR0[:, 2048:3072]
    B1B = AR0[:, 3072:4096]
    B2B = AR0[:, 4096:5120]
    crepB = AR0[:, 5120:6144]
    XB = AR0[:, 6144:8192]
    GTp = AR0[:, 8192:10240]
    OlocTp = AR0[:, 10240:12288]
    YKnegp = AR0[:, 12288:14336]
    F0Tp = AR0[:, 14336:16384]

    psA = st_.enter_context(nc.psum_tensor("psA", [128, 2048], F32))
    psB = st_.enter_context(nc.psum_tensor("psB", [128, 1024], F32))
    psT = st_.enter_context(nc.psum_tensor("psT", [128, 2048], BF16))
    RA = psA[:, 0:1024]
    RB = psA[:, 1024:2048]
    RUPD = psB[:, 0:512]
    ROUT = psB[:, 512:1024]

    sD = SemT(st_.enter_context(nc.semaphore("sD")))
    sP = SemT(st_.enter_context(nc.semaphore("sP")))
    sV = SemT(st_.enter_context(nc.semaphore("sV")))
    sAc = SemT(st_.enter_context(nc.semaphore("sAc")))
    sO = SemT(st_.enter_context(nc.semaphore("sO")))
    sX = SemT(st_.enter_context(nc.semaphore("sX")))
    blk = st_.enter_context(nc.Block())
    gps, syn, vec, act, ten = blk.gpsimd, blk.sync, blk.vector, blk.scalar, blk.tensor

    def stub_out():
        def _(v):
            v.wait_ge(sP.h, sP.n)
            v.wait_ge(sAc.h, sAc.n)
            v.wait_ge(sD.h, sD.n)
            v.wait_ge(sX.h, sX.n)
            i = v.memset(sqB[:], 0.0)
            sV.inc(i)
        vec(_)
        sv = sV.n

        def _(s, sv=sv):
            s.wait_ge(sV.h, sv)
            n = 0
            for sti in range(NST):
                for dm in range(2):
                    i = s.dma_start(out=y_h[sti * 128:(sti + 1) * 128, dm * 512:(dm + 1) * 512],
                                    in_=sqB[:].bitcast(F32)[:, 0:512])
                    sO.inc(i, 16)
        syn(_)
        tot = sO.n

        def _(g, tot=tot):
            g.wait_ge(sO.h, tot)
        gps(_)

    free = {"psA": 0, "psT": 0}

    HMI = {"C": 0, "S1": 1, "S2": 2, "S3": 3,
           "LNA": 1, "AG": 1, "G": 1, "N2K": 1, "INVN": 1,
           "L": 2, "B": 2,
           "A": 3, "GC": 3, "MSQ": 3, "AQG": 3, "MSK": 3}

    def hmr(k, sub=slice(0, 16)):
        i = HMI[k]
        base = i * 32
        return hm[base + sub.start: base + sub.stop, :]

    fstage_pe_init = [0]

    SMI = {"C": 0, "INVN": 1, "AG": 2}

    def smr(k):
        return sm[:, SMI[k] * 256:(SMI[k] + 1) * 256]

    def tap(name, src_ap, wait_sem, wait_val):
        if name not in tap_h:
            return
        def _(g, name=name, src_ap=src_ap, wait_sem=wait_sem, wait_val=wait_val):
            g.wait_ge(wait_sem.h, wait_val)
            i = g.dma_start(out=tap_h[name][:], in_=src_ap)
            sO.inc(i, 16)
        gps(_)

    def _(g):
        for dst, src in [(identB, "ident"), (identF, "ident"), (hv4B, "hv4"),
                         (hv4F, "hv4"), (bonesAll, "bonesAll"),
                         (onesONE, "onesone"),
                         (maskUi, "maskUi"), (maskUs, "maskUs"), (maskLs, "maskLs"),
                         (selhB, "selh"), (hselB, "hsel"), (lamB, "gs")]:
            i = g.dma_start(out=dst[:], in_=w[src][:])
            sD.inc(i, 16)
        i = g.dma_start(out=wflT[:].rearrange("p (t c) -> p t c", c=32),
                        in_=w["Wfl"][:].rearrange("(t p) c -> p t c", p=128))
        sD.inc(i, 16)
        i = g.dma_start(out=bflP[0:16, :], in_=w["bfl_f"][:])
        sD.inc(i, 16)
        i = g.dma_start(out=bflP[32:48, :], in_=w["bfl_l"][:])
        sD.inc(i, 16)
        for dst_, src_ in [(qnwB[0:64, :], "qnw"), (qnwB[64:128, :], "qnw"),
                           (knwB[0:64, :], "knw"), (knwB[64:128, :], "knw")]:
            i = g.dma_start(out=dst_, in_=w[src_][:])
            sD.inc(i, 16)
        i = g.dma_start(out=onwP[:], in_=w["onw"][:])
        sD.inc(i, 16)
    gps(_)
    consts_v = sD.n

    def _(v):
        i = v.memset(onescol[:], 1.0)
        sV.inc(i)
    vec(_)
    ones_v = sV.n

    xnT = AR0
    ln_done_v = [0] * NST
    tr_evac_v = [0] * NST
    sq_read_v = [0] * NST
    for sti in range(NST):
        xb = xin if sti % 2 == 0 else xin2

        def _(s, sti=sti, xb=xb):
            if sti >= 2:
                s.wait_ge(sV.h, ln_done_v[sti - 2])
            i = s.dma_start(out=xb, in_=x_h[sti * 128:(sti + 1) * 128, :])
            sX.inc(i, 16)
        syn(_)
        dv = sX.n

        def _(a, dv=dv, xb=xb, sti=sti):
            a.wait_ge(sX.h, dv)
            if sti >= 1:
                a.wait_ge(sV.h, sq_read_v[sti - 1])
            i = a.activation(out=psB[:], in_=xb, func=AF.Square)
            sAc.inc(i)
        act(_)
        av = sAc.n

        def _(v, dv=dv, av=av, xb=xb, sti=sti):
            v.wait_ge(sX.h, dv)
            v.reduce_sum(out=stat1[:], in_=xb, axis=AXX)
            v.wait_ge(sAc.h, av)
            i = v.reduce_sum(out=stat2[:], in_=psB[:], axis=AXX)
            sq_read_v[sti] = sV.inc(i)
            v.tensor_scalar(out=stat1[:], in0=stat1[:], scalar1=1.0 / D, scalar2=None, op0=ALU.mult)
            v.tensor_scalar(out=stat2[:], in0=stat2[:], scalar1=1.0 / D, scalar2=None, op0=ALU.mult)
            v.tensor_tensor(out=stat3[:], in0=stat1[:], in1=stat1[:], op=ALU.mult)
            v.tensor_tensor(out=stat2[:], in0=stat2[:], in1=stat3[:], op=ALU.subtract)
            i = v.tensor_scalar(out=stat2[:], in0=stat2[:], scalar1=EPS_LN, scalar2=None, op0=ALU.add)
            sV.inc(i)
        vec(_)
        vv = sV.n

        def _(a, vv=vv):
            a.wait_ge(sV.h, vv)
            a.activation(out=stat3[:], in_=stat2[:], func=AF.Ln)
            i = a.activation(out=stat4[:], in_=stat3[:], func=AF.Exp, scale=-0.5)
            sAc.inc(i)
        act(_)
        av2 = sAc.n

        def _(v, av2=av2, xb=xb, sti=sti):
            v.wait_ge(sAc.h, av2)
            i = v.tensor_scalar(out=preB[:, 0:1024], in0=xb, scalar1=stat1[:, 0:1],
                                scalar2=stat4[:, 0:1], op0=ALU.subtract, op1=ALU.mult)
            ln_done_v[sti] = sV.inc(i)
        vec(_)
        xn_v = sV.n

        def _(t, xn_v=xn_v, sti=sti):
            t.wait_ge(sV.h, xn_v)
            t.wait_ge(sD.h, consts_v)
            if sti >= 1:
                t.wait_ge(sV.h, tr_evac_v[sti - 1])
            for dtb in range(NDT):
                i = t.transpose(out=psT[:, dtb * 128:(dtb + 1) * 128],
                                in_=preB[:, dtb * 128:(dtb + 1) * 128], identity=identB[:])
            sP.inc(i)
        ten(_)
        pv = sP.n

        def _(v, pv=pv, sti=sti):
            v.wait_ge(sP.h, pv)
            dst = xnT[:].rearrange("p (t s) -> p t s", t=NDT)[:, :, sti * 128:(sti + 1) * 128]
            i = v.tensor_copy(out=dst, in_=psT[:, 0:1024].rearrange("p (t s) -> p t s", s=128))
            tr_evac_v[sti] = sV.inc(i)
        vec(_)
    xnT_v = sV.n
    free["psA"] = xnT_v
    free["psT"] = xnT_v
    tap("xnT", xnT[:], sV, xnT_v)

    if stop_phase <= 1:
        stub_out()
        st_.close()
        return nc, tapnames

    def _(a):
        a.wait_ge(sD.h, consts_v)
        i = a.activation(out=lamB[:], in_=lamB[:], func=AF.Abs)
        sAc.inc(i)
    act(_)
    lam_av = sAc.n

    def _(v, lam_av=lam_av):
        v.wait_ge(sAc.h, lam_av)
        i = v.tensor_scalar(out=hselL[:], in0=hselB[:], scalar1=lamB[:, 0:1], scalar2=None, op0=ALU.mult)
        sV.inc(i)
    vec(_)
    lam_v = sV.n

    fl_evac = [0, 0, 0, 0]
    for sh in range(4):
        def _(t, sh=sh):
            t.wait_ge(sV.h, xnT_v)
            if sh >= 2:
                t.wait_ge(sAc.h, fl_evac[sh - 2])
            base = (sh % 2) * 512
            for kt in range(NDT):
                t.matmul(out=psB[0:16, base:base + 512],
                         lhsT=wflT[:, kt * 32:kt * 32 + 16],
                         rhs=xnT[:, kt * 2048 + sh * 512: kt * 2048 + sh * 512 + 512],
                         start=(kt == 0), stop=(kt == NDT - 1))
                i = t.matmul(out=psB[32:48, base:base + 512],
                             lhsT=wflT[:, kt * 32 + 16:(kt + 1) * 32],
                             rhs=xnT[:, kt * 2048 + sh * 512: kt * 2048 + sh * 512 + 512],
                             start=(kt == 0), stop=(kt == NDT - 1))
            sP.inc(i)
        ten(_)
        pv = sP.n

        def _(a, sh=sh, pv=pv):
            a.wait_ge(sP.h, pv)
            base = (sh % 2) * 512
            a.activation(out=hmr("A")[:, sh * 512:(sh + 1) * 512], in_=psB[0:16, base:base + 512],
                         func=AF.Sigmoid, bias=bfP)
            i = a.activation(out=hmr("B")[:, sh * 512:(sh + 1) * 512], in_=psB[32:48, base:base + 512],
                             func=AF.Sigmoid, bias=blP)
            fl_evac[sh] = sAc.inc(i)
        act(_)
    fl_done_a = fl_evac[3]

    def _(a):
        a.wait_ge(sAc.h, fl_done_a)
        i = a.activation(out=hmr("LNA"), in_=hmr("A"), func=AF.Ln)
        sAc.inc(i)
    act(_)
    lna_av = sAc.n

    def _(a):
        a.wait_ge(sAc.h, lna_av)
        a.wait_ge(sV.h, free["psA"])
        i = a.activation(out=psA[0:16, :], in_=hmr("B"), func=AF.Ln)
        sAc.inc(i)
    act(_)
    lnb_av = sAc.n

    def _(v, lnb_av=lnb_av):
        v.wait_ge(sAc.h, lnb_av)
        i = v.tensor_tensor(out=psA[0:16, :], in0=psA[0:16, :], in1=hmr("LNA"), op=ALU.subtract)
        sV.inc(i)
    vec(_)
    cdiff_v = sV.n

    def _(a, cdiff_v=cdiff_v):
        a.wait_ge(sV.h, cdiff_v)
        i = a.activation(out=hmr("C"), in_=psA[0:16, :], func=AF.Exp)
        sAc.inc(i)
    act(_)
    c_av = sAc.n

    def _(v, c_av=c_av):
        v.wait_ge(sAc.h, c_av)
        v.tensor_copy(out=hmr("S2"), in_=hmr("LNA"))
        cur, other = "S2", "S3"
        for k in [1, 2, 4, 8, 16, 32]:
            c_ = hmr(cur).rearrange("p (c f) -> p c f", f=64)
            o_ = hmr(other).rearrange("p (c f) -> p c f", f=64)
            v.tensor_copy(out=o_[:, :, 0:k], in_=c_[:, :, 0:k])
            v.tensor_tensor(out=o_[:, :, k:64], in0=c_[:, :, k:64], in1=c_[:, :, 0:64 - k], op=ALU.add)
            cur, other = other, cur
        v.tensor_copy(out=psA[32:48, :], in_=hmr("L"))
        i = v.tensor_tensor(out=hmr("S1"), in0=hmr("LNA"), in1=psA[32:48, :], op=ALU.subtract)
        sV.inc(i)
    vec(_)
    hm_vv = sV.n

    def _(a):
        a.wait_ge(sV.h, hm_vv)
        i = a.activation(out=hmr("S1"), in_=hmr("S1"), func=AF.Exp)
        sAc.inc(i)
    act(_)
    ag_av = sAc.n

    def _(v, ag_av=ag_av):
        v.wait_ge(sAc.h, ag_av)
        Lr = hmr("L").rearrange("p (c f) -> p c f", f=64)
        i = v.tensor_copy(out=hmr("S3").rearrange("p (c f) -> p c f", f=64),
                          in_=Lr[:, :, 63:64].broadcast_to([16, 32, 64]))
        sV.inc(i)
    vec(_)
    gcs_vv = sV.n

    def _(a, gcs_vv=gcs_vv):
        a.wait_ge(sV.h, gcs_vv)
        i = a.activation(out=hmr("GC"), in_=hmr("GC"), func=AF.Exp)
        sAc.inc(i)
    act(_)
    gc_av = sAc.n

    def _(t):
        t.wait_ge(sAc.h, gc_av)
        t.wait_ge(sV.h, max(free["psA"], lam_v))
        for chp in range(16):
            t.transpose(out=psA[:, chp * 16:(chp + 1) * 16],
                        in_=hmr("C")[:, chp * 128:(chp + 1) * 128], identity=identF[0:16, 0:16])
        for chp in range(16):
            t.transpose(out=psA[:, 256 + chp * 16: 256 + (chp + 1) * 16],
                        in_=hmr("AG")[:, chp * 128:(chp + 1) * 128], identity=identF[32:48, 32:48])
        i = t.matmul(out=psA[:, 768:776], lhsT=selhB[:], rhs=hselL[:], start=True, stop=True)
        sP.inc(i)
    ten(_)
    smt_pv = sP.n

    def _(g, smt_pv=smt_pv):
        g.wait_ge(sAc.h, gc_av)
        with nc.allow_non_contiguous_dma(reason="tiny gC sample gather"):
            for hp in range(8):
                src_ = hmr("GC", slice(2 * hp, 2 * hp + 2)).rearrange("p (c f) -> p c f", f=64)[:, :, 63:64].squeeze(2)
                i = g.dma_start(out=hstg[0:2, hp * 32:(hp + 1) * 32], in_=src_)
                sD.inc(i, 16)
    gps(_)
    gcs2_v = sD.n

    def _(t, gcs2_v=gcs2_v, smt_pv=smt_pv):
        t.wait_ge(sD.h, gcs2_v)
        t.wait_ge(sP.h, smt_pv)
        for hp in range(8):
            i = t.matmul(out=psA[:, 512 + hp * 32: 512 + (hp + 1) * 32], lhsT=hv4B[0:2, :],
                         rhs=hstg[0:2, hp * 32:(hp + 1) * 32], start=True, stop=True)
        fstage_pe_init[0] = sP.inc(i)
    ten(_)
    sm_pv = sP.n

    def _(v):
        v.wait_ge(sP.h, sm_pv)
        v.tensor_copy(out=smr("C"), in_=psA[:, 0:256])
        v.tensor_copy(out=smr("AG"), in_=psA[:, 256:512])
        v.tensor_copy(out=gCrep2[:], in_=psA[:, 512:768])
        i = v.tensor_copy(out=lamrep[:], in_=psA[:, 768:776])
        free["psA"] = sV.inc(i)
    vec(_)
    sm_vv = sV.n
    tap("hm", hm[:], sV, sm_vv)
    tap("sm", sm[:], sV, sm_vv)
    tap("gCrep2", gCrep2[:], sV, sm_vv)

    wload_v = {}

    def load_weight(name, wait_pe=0):
        def _(g, name=name, wait_pe=wait_pe):
            if wait_pe:
                g.wait_ge(sP.h, wait_pe)
            i = g.dma_start(out=wbuf[:].rearrange("p (t c) -> p t c", c=D),
                            in_=w[name][:].rearrange("(t p) c -> p t c", p=128))
            sD.inc(i, 16)
        gps(_)
        wload_v[name] = sD.n

    cd_used = {}

    def load_cdiag_dt(name, dt):
        def _(g, name=name, dt=dt):
            if (name, dt - 2) in cd_used:
                g.wait_ge(sP.h, cd_used[(name, dt - 2)])
            i = g.dma_start(out=cdiag[:, (dt % 2) * 128:(dt % 2) * 128 + 128],
                            in_=w[name][:, dt * 128:(dt + 1) * 128])
            sD.inc(i, 16)
        gps(_)
        wload_v[(name, dt)] = sD.n

    conv_evac_v = [0, 0]

    def proj_dt(dt, wname, out_sb, conv_name=None):
        def _(t, dt=dt, wname=wname):
            t.wait_ge(sD.h, wload_v[wname])
            t.wait_ge(sV.h, free["psA"])
            for sc in range(4):
                for kt in range(NDT):
                    i = t.matmul(out=psA[:, sc * 512:(sc + 1) * 512],
                                 lhsT=wbuf[:, kt * 1024 + dt * 128: kt * 1024 + (dt + 1) * 128],
                                 rhs=xnT[:, kt * 2048 + sc * 512: kt * 2048 + sc * 512 + 512],
                                 start=(kt == 0), stop=(kt == NDT - 1))
            sP.inc(i)
        ten(_)
        mm_pv = sP.n

        if conv_name is None:
            def _(v, dt=dt, mm_pv=mm_pv):
                v.wait_ge(sP.h, mm_pv)
                v.wait_ge(sV.h, sm_vv)
                i = v.tensor_scalar(out=out_sb[:, dt * 2048:(dt + 1) * 2048],
                                    in0=psA[:], scalar1=lamrep[:, dt:dt + 1], scalar2=None, op0=ALU.mult)
                free["psA"] = sV.inc(i)
            vec(_)
            return sV.n

        def _(v, mm_pv=mm_pv):
            v.wait_ge(sP.h, mm_pv)
            i = v.tensor_copy(out=preB[:], in_=psA[:])
            free["psA"] = sV.inc(i)
        vec(_)
        pre_vv = sV.n

        for half in range(2):
            def _(t, half=half, dt=dt, pre_vv=pre_vv):
                t.wait_ge(sV.h, pre_vv)
                t.wait_ge(sD.h, wload_v[(conv_name, dt)])
                t.wait_ge(sAc.h, fl_done_a)
                if conv_evac_v[1] or conv_evac_v[0]:
                    t.wait_ge(sV.h, max(conv_evac_v))
                for sc in range(2):
                    ofs = half * 1024 + sc * 512
                    for sub in range(4):
                        for j in [3, 2, 1, 0]:
                            delta = 3 - j
                            lo = max(0, delta - ofs)
                            i = t.matmul(
                                out=psB[32 * sub:32 * sub + 32, sc * 512 + lo:(sc + 1) * 512],
                                lhsT=cdiag[32 * sub:32 * sub + 32,
                                           (dt % 2) * 128 + j * 32: (dt % 2) * 128 + j * 32 + 32],
                                rhs=preB[32 * sub:32 * sub + 32,
                                         ofs + lo - delta: ofs + 512 - delta],
                                start=(j == 3), stop=(j == 0),
                                tile_position=(32 * sub, 32 * sub))
                cd_used[(conv_name, dt)] = sP.inc(i)
            ten(_)
            h_pv = sP.n

            def _(v, half=half, dt=dt, h_pv=h_pv):
                v.wait_ge(sP.h, h_pv)
                i = v.tensor_copy(out=out_sb[:, dt * 2048 + half * 1024: dt * 2048 + (half + 1) * 1024],
                                  in_=psB[:])
                conv_evac_v[half] = sV.inc(i)
            vec(_)
        return conv_evac_v[1]

    def stats_all(src_sb, slot, ready_v):
        sq_pv = [0]
        for dt in range(NDT):
            def _(v, dt=dt, ready_v=ready_v):
                v.wait_ge(sV.h, ready_v)
                if sq_pv[0]:
                    v.wait_ge(sP.h, sq_pv[0])
                elif dt == 0:
                    v.wait_ge(sV.h, free["psA"])
                i = v.tensor_tensor(out=sqB[:], in0=src_sb[:, dt * 2048:(dt + 1) * 2048],
                                    in1=src_sb[:, dt * 2048:(dt + 1) * 2048], op=ALU.mult)
                sV.inc(i)
            vec(_)
            sq_vv = sV.n

            def _(t, sq_vv=sq_vv, dt=dt):
                t.wait_ge(sV.h, sq_vv)
                if dt == 0:
                    t.wait_ge(sV.h, free["psA"])
                for sc in range(4):
                    i = t.matmul(out=psA[0:32, sc * 512:(sc + 1) * 512],
                                 lhsT=bonesAll[:, dt * 32:(dt + 1) * 32],
                                 rhs=sqB[:, sc * 512:(sc + 1) * 512],
                                 start=(dt == 0), stop=(dt == NDT - 1))
                sq_pv[0] = sP.inc(i)
            ten(_)
        st_pv = sP.n

        def _(v, slot=slot, st_pv=st_pv):
            v.wait_ge(sP.h, st_pv)
            i = v.tensor_copy(out=hmr(slot), in_=psA[0:16, :])
            free["psA"] = sV.inc(i)
        vec(_)
        return sV.n

    def rsqrt_pipe(src_slot, dst_slot, scale, eps, v_ready):
        def _(v, v_ready=v_ready):
            v.wait_ge(sV.h, max(v_ready, free["psA"]))
            i = v.tensor_scalar(out=psA[0:16, :], in0=hmr(src_slot), scalar1=scale, scalar2=eps,
                                op0=ALU.mult, op1=ALU.add)
            sV.inc(i)
        vec(_)
        t1v = sV.n

        def _(a, t1v=t1v):
            a.wait_ge(sV.h, t1v)
            a.activation(out=psA[32:48, :], in_=psA[0:16, :], func=AF.Ln)
            i = a.activation(out=hmr(dst_slot), in_=psA[32:48, :], func=AF.Exp, scale=-0.5)
            sAc.inc(i)
        act(_)
        ra = sAc.n

        def _(v, ra=ra):
            v.wait_ge(sAc.h, ra)
            i = v.nop()
            free["psA"] = sV.inc(i)
        vec(_)
        return sV.n

    fstage_pe = fstage_pe_init

    def field_mm(hm_slot, dt, ready_v):
        def _(g, dt=dt, ready_v=ready_v, hm_slot=hm_slot):
            g.wait_ge(sV.h, ready_v)
            if fstage_pe[0]:
                g.wait_ge(sP.h, fstage_pe[0])
            i = g.dma_start(out=hstg[0:2, :], in_=hmr(hm_slot, slice(2 * dt, 2 * dt + 2))[:, 0:1024])
            sD.inc(i, 16)
            i = g.dma_start(out=hstg[32:34, :], in_=hmr(hm_slot, slice(2 * dt, 2 * dt + 2))[:, 1024:2048])
            sD.inc(i, 16)
        gps(_)
        stg_v = sD.n

        def _(t, stg_v=stg_v):
            t.wait_ge(sD.h, stg_v)
            t.wait_ge(sV.h, free["psA"])
            for sc in range(4):
                b = (sc // 2) * 32
                rhs = hstg[b:b + 2, (sc % 2) * 512:(sc % 2) * 512 + 512]
                i = t.matmul(out=psA[:, sc * 512:(sc + 1) * 512],
                             lhsT=hv4B[b:b + 2, :], rhs=rhs, start=True, stop=True,
                             tile_position=(b, 0))
            fstage_pe[0] = sP.inc(i)
        ten(_)
        return sP.n

    if stop_phase <= 2:
        stub_out()
        st_.close()
        return nc, tapnames

    load_weight("Wrq")
    rq_v = 0
    for dt in range(NDT):
        rq_v = proj_dt(dt, "Wrq", AR4, None)

    if stop_phase <= 3:
        stub_out()
        st_.close()
        return nc, tapnames

    load_weight("Wq", wait_pe=sP.n)
    qcv = 0
    for dt in range(NDT):
        load_cdiag_dt("cdq", dt)
        qcv = proj_dt(dt, "Wq", AR2, "cdq")
    q_stat_v = stats_all(AR2, "MSQ", qcv)

    aq_v = rsqrt_pipe("MSQ", "MSQ", 1.0 / DH, EPS_RMS, q_stat_v)

    def _(a, aq_v=aq_v):
        a.wait_ge(sV.h, aq_v)
        i = a.activation(out=hmr("G"), in_=hmr("L"), func=AF.Exp)
        sAc.inc(i)
    act(_)
    g_av = sAc.n

    def _(v, g_av=g_av):
        v.wait_ge(sAc.h, g_av)
        v.wait_ge(sV.h, free["psA"])
        v.tensor_copy(out=psA[32:48, :], in_=hmr("G"))
        i = v.tensor_tensor(out=hmr("AQG"), in0=hmr("MSQ"), in1=psA[32:48, :], op=ALU.mult)
        free["psA"] = sV.inc(i)
    vec(_)
    aqg_v = sV.n

    for dt in range(NDT):
        f1 = field_mm("AQG", dt, aqg_v)

        def _(v, dt=dt, f1=f1):
            v.wait_ge(sP.h, f1)
            v.tensor_scalar(out=scrF[:, 0:1024], in0=AR2[:, dt * 2048:dt * 2048 + 1024],
                            scalar1=qnwB[:, 0:1], scalar2=None, op0=ALU.mult)
            v.tensor_scalar(out=xnf, in0=AR2[:, dt * 2048 + 1024:(dt + 1) * 2048],
                            scalar1=qnwB[:, 0:1], scalar2=None, op0=ALU.mult)
            v.tensor_tensor(out=scrF[:, 0:1024], in0=scrF[:, 0:1024], in1=psA[:, 0:1024], op=ALU.mult)
            i = v.tensor_tensor(out=xnf, in0=xnf, in1=psA[:, 1024:2048], op=ALU.mult)
            free["psA"] = sV.inc(i)
        vec(_)
        f2 = field_mm("G", dt, aqg_v)

        def _(v, dt=dt, f2=f2, rq_v=rq_v):
            v.wait_ge(sP.h, f2)
            v.wait_ge(sV.h, rq_v)
            v.tensor_tensor(out=sqB[:], in0=AR4[:, dt * 2048:(dt + 1) * 2048],
                            in1=psA[:], op=ALU.mult)
            v.tensor_tensor(out=AR1[:, dt * 2048:dt * 2048 + 1024],
                            in0=scrF[:, 0:1024], in1=sqB[:].bitcast(F32)[:, 0:1024] if False else sqB[:, 0:1024],
                            op=ALU.add)
            i = v.tensor_tensor(out=AR1[:, dt * 2048 + 1024:(dt + 1) * 2048],
                                in0=xnf, in1=sqB[:, 1024:2048], op=ALU.add)
            free["psA"] = sV.inc(i)
        vec(_)
    QD_v = sV.n
    tap("QD", AR1[:], sV, QD_v)

    if stop_phase <= 4:
        stub_out()
        st_.close()
        return nc, tapnames

    load_weight("Wv", wait_pe=sP.n)
    for dt in range(NDT):
        load_cdiag_dt("cdv", dt)
        cv = proj_dt(dt, "Wv", AR2, "cdv")

        def _(t, dt=dt, cv=cv):
            t.wait_ge(sV.h, cv)
            if free["psT"]:
                t.wait_ge(sV.h, free["psT"])
            for sb in range(16):
                i = t.transpose(out=psT[:, sb * 128:(sb + 1) * 128],
                                in_=AR2[:, dt * 2048 + sb * 128: dt * 2048 + (sb + 1) * 128],
                                identity=identB[:])
            sP.inc(i)
        ten(_)
        tr_pv = sP.n

        def _(v, dt=dt, tr_pv=tr_pv):
            v.wait_ge(sP.h, tr_pv)
            v.wait_ge(sV.h, sm_vv)
            dst = AR3[:].rearrange("p (t c) -> p t c", c=1024)[:, :, dt * 128:(dt + 1) * 128] \
                .rearrange("p t (h f) -> p t h f", f=64)
            src = psT[:].rearrange("p (t c) -> p t c", c=128).rearrange("p t (h f) -> p t h f", f=64)
            in1 = smr("AG").rearrange("p (t h) -> p t h", h=16)[:, :, 2 * dt:2 * dt + 2] \
                .unsqueeze(3).broadcast_to([128, 16, 2, 64])
            i = v.tensor_tensor(out=dst, in0=src, in1=in1, op=ALU.mult)
            free["psT"] = sV.inc(i)
        vec(_)
    VagT_v = sV.n
    tap("VagT", AR3[:], sV, VagT_v)

    if stop_phase <= 5:
        stub_out()
        st_.close()
        return nc, tapnames

    load_weight("Wrk", wait_pe=sP.n)
    rk_v = 0
    for dt in range(NDT):
        rk_v = proj_dt(dt, "Wrk", AR4, None)

    if stop_phase <= 6:
        stub_out()
        st_.close()
        return nc, tapnames

    load_weight("Wk", wait_pe=sP.n)
    kcv = 0
    for dt in range(NDT):
        load_cdiag_dt("cdk", dt)
        kcv = proj_dt(dt, "Wk", AR2, "cdk")
    k_stat_v = stats_all(AR2, "MSK", kcv)

    def _(s):
        s.wait_ge(sP.h, sP.n)
        i = s.dma_start(out=xnT_scr[:], in_=AR0[:])
        sX.inc(i, 16)
    syn(_)
    xnT_dump_v = sX.n

    ak_v = rsqrt_pipe("MSK", "MSK", 1.0 / DH, EPS_RMS, k_stat_v)
    akb_v = ak_v

    ksum_done = 0
    for dt in range(NDT):
        f1 = field_mm("MSK", dt, akb_v)

        def _(v, dt=dt, f1=f1, rk_v=rk_v):
            v.wait_ge(sP.h, f1)
            v.wait_ge(sV.h, rk_v)
            if dt == 0:
                v.wait_ge(sX.h, xnT_dump_v)
            for hf in range(2):
                o = dt * 2048 + hf * 1024
                v.tensor_scalar(out=ksumF[:], in0=AR2[:, o:o + 1024],
                                scalar1=knwB[:, 0:1], scalar2=None, op0=ALU.mult)
                v.tensor_tensor(out=ksumF[:], in0=ksumF[:],
                                in1=psA[:, hf * 1024:(hf + 1) * 1024], op=ALU.mult)
                v.tensor_tensor(out=ksumF[:], in0=ksumF[:], in1=AR4[:, o:o + 1024], op=ALU.add)
                i = v.tensor_copy(out=AR0[:, o:o + 1024], in_=ksumF[:])
            free["psA"] = sV.inc(i)
        vec(_)
        ksum_done = sV.n
    ksum_v = ksum_done
    n2_v = stats_all(AR0, "N2K", ksum_done)

    invn_v = rsqrt_pipe("N2K", "N2K", 1.0, 1e-24, n2_v)
    invnb_v = invn_v

    def _(t, invn_v=invn_v):
        t.wait_ge(sV.h, max(invn_v, free["psA"]))
        for chp in range(16):
            i = t.transpose(out=psA[:, chp * 16:(chp + 1) * 16],
                            in_=hmr("N2K")[:, chp * 128:(chp + 1) * 128], identity=identF[32:48, 32:48])
        sP.inc(i)
    ten(_)
    ivs_pv = sP.n

    def _(v, ivs_pv=ivs_pv):
        v.wait_ge(sP.h, ivs_pv)
        i = v.tensor_copy(out=smr("INVN"), in_=psA[:, 0:256])
        free["psA"] = sV.inc(i)
    vec(_)
    invnsm_v = sV.n

    for dt in range(NDT):
        f1 = field_mm("N2K", dt, invnb_v)

        def _(v, dt=dt, f1=f1):
            v.wait_ge(sP.h, f1)
            i = v.tensor_tensor(out=AR2[:, dt * 2048:(dt + 1) * 2048],
                                in0=AR0[:, dt * 2048:(dt + 1) * 2048], in1=psA[:], op=ALU.mult)
            free["psA"] = sV.inc(i)
        vec(_)

        def _(t, dt=dt):
            t.wait_ge(sV.h, max(invnsm_v, ksum_v, free["psT"]))
            for sb in range(16):
                i = t.transpose(out=psT[:, sb * 128:(sb + 1) * 128],
                                in_=AR0[:, dt * 2048 + sb * 128: dt * 2048 + (sb + 1) * 128],
                                identity=identB[:])
            sP.inc(i)
        ten(_)
        kt_pv = sP.n

        def _(v, dt=dt, kt_pv=kt_pv):
            v.wait_ge(sP.h, kt_pv)
            dst = AR4[:].rearrange("p (t c) -> p t c", c=1024)[:, :, dt * 128:(dt + 1) * 128] \
                .rearrange("p t (h f) -> p t h f", f=64)
            src = psT[:].rearrange("p (t c) -> p t c", c=128).rearrange("p t (h f) -> p t h f", f=64)
            in1 = smr("INVN").rearrange("p (t h) -> p t h", h=16)[:, :, 2 * dt:2 * dt + 2] \
                .unsqueeze(3).broadcast_to([128, 16, 2, 64])
            i = v.tensor_tensor(out=dst, in0=src, in1=in1, op=ALU.mult)
            free["psT"] = sV.inc(i)
        vec(_)
    KD_v = KT_v = sV.n
    tap("KD", AR2[:], sV, KD_v)
    tap("KT", AR4[:], sV, KT_v)

    if stop_phase <= 7:
        stub_out()
        st_.close()
        return nc, tapnames

    KD_, QD_, KT_, VagT_ = AR2, AR1, AR4, AR3
    consts_ready = max(KD_v, KT_v, QD_v, VagT_v, sm_vv, invnsm_v, hm_vv)

    def _(v):
        i = v.memset(MTall[:], 0.0)
        sV.inc(i)
    vec(_)
    mt0_v = sV.n

    last_evac = {"RA": free["psA"], "RB": free["psA"]}
    last_evacA = {"RA": 0, "RB": 0}
    freeA = {"psT": 0}
    crep_pe = [0]
    head_state = {}
    chain_upd_v = {}
    chain_out_evac = {}
    last_upd_a = [0]

    PERS = [
        {"GT": GTp, "OlocT": OlocTp, "YKneg": YKnegp, "F0T": F0Tp},
        {"GT": wbuf[:, 0:2048], "OlocT": wbuf[:, 2048:4096],
         "YKneg": wbuf[:, 4096:6144], "F0T": wbuf[:, 6144:8192]},
    ]

    def pers(h):
        return PERS[(h // 2) % 2]

    def kd_sl(h, ch):
        return KD_[(h % 2) * 64:(h % 2) * 64 + 64,
                   (h // 2) * 2048 + ch * 64:(h // 2) * 2048 + (ch + 1) * 64]

    def qd_sl(h, ch):
        return QD_[(h % 2) * 64:(h % 2) * 64 + 64,
                   (h // 2) * 2048 + ch * 64:(h // 2) * 2048 + (ch + 1) * 64]

    def kt_sl(h, ch):
        return KT_[(ch % 2) * 64:(ch % 2) * 64 + 64,
                   (ch // 2) * 1024 + h * 64:(ch // 2) * 1024 + (h + 1) * 64]

    def x_sl(ch, j0, j1):
        return XB[(ch % 2) * 64:(ch % 2) * 64 + 64,
                  (ch // 2) * 128 + j0:(ch // 2) * 128 + j1]

    def pw_sl(buf, ch):
        return buf[(ch % 2) * 64:(ch % 2) * 64 + 64, (ch // 2) * 64:(ch // 2) * 64 + 64]

    maskUi_b = maskUi[:].unsqueeze(1).broadcast_to([128, 16, 64])
    maskUs_b = maskUs[:].unsqueeze(1).broadcast_to([128, 16, 64])
    maskLs_b = maskLs[:].unsqueeze(1).broadcast_to([128, 16, 64])

    def solve_steps(h):
        crep_av_h = [0]

        def s_crep():
            def _(s, h=h):
                s.wait_ge(sV.h, consts_ready)
                if crep_pe[0]:
                    s.wait_ge(sP.h, crep_pe[0])
                crow = hmr("C", slice(h, h + 1)).rearrange("p (c f) -> p c f", f=128)
                i = s.dma_start(out=sqV[64:65, :].rearrange("p (c f) -> p c f", f=64),
                                in_=crow[:, :, 0:64])
                sX.inc(i, 16)
                i = s.dma_start(out=sqV[65:66, :].rearrange("p (c f) -> p c f", f=64),
                                in_=crow[:, :, 64:128])
                sX.inc(i, 16)
            syn(_)
            stg_v = sX.n

            def _(t, stg_v=stg_v):
                t.wait_ge(sX.h, stg_v)
                t.wait_ge(sV.h, last_evac["RA"])
                if last_evacA["RA"]:
                    t.wait_ge(sAc.h, last_evacA["RA"])
                for cc_ in range(2):
                    i = t.matmul(out=RA[:, cc_ * 512:(cc_ + 1) * 512],
                                 lhsT=hv4F[64:66, :], rhs=sqV[64:66, cc_ * 512:(cc_ + 1) * 512],
                                 start=True, stop=True, tile_position=(64, 0))
                crep_pe[0] = sP.inc(i)
            ten(_)
            pv = sP.n

            def _(a, pv=pv):
                a.wait_ge(sP.h, pv)
                i = a.activation(out=crepB[:], in_=RA, func=AF.Copy)
                last_evacA["RA"] = sAc.inc(i)
            act(_)
            crep_av_h[0] = sAc.n
        yield s_crep

        p1_pv = [0]

        def s_p1_mm():
            def _(t, h=h):
                t.wait_ge(sV.h, max(consts_ready, last_evac["RA"], last_evac["RB"]))
                if last_evacA["RA"] or last_evacA["RB"]:
                    t.wait_ge(sAc.h, max(last_evacA["RA"], last_evacA["RB"]))
                for ch in range(NCH):
                    tp = ((h % 2) * 64, (ch % 2) * 64)
                    t.matmul(out=pw_sl(RA, ch), lhsT=kd_sl(h, ch), rhs=qd_sl(h, ch),
                             start=True, stop=True, tile_position=tp)
                    i = t.matmul(out=pw_sl(RB, ch), lhsT=kd_sl(h, ch), rhs=kd_sl(h, ch),
                                 start=True, stop=True, tile_position=tp)
                p1_pv[0] = sP.inc(i)
            ten(_)
        yield s_p1_mm

        def s_p1_mask():
            def _(v, h=h):
                v.wait_ge(sP.h, p1_pv[0])
                if crep_av_h[0]:
                    v.wait_ge(sAc.h, crep_av_h[0])
                c_sm_b = smr("C").rearrange("p (t j) -> p t j", j=16)[:, :, h:h + 1].broadcast_to([128, 16, 64])
                RAr = RA.rearrange("p (t f) -> p t f", f=64)
                RBr = RB.rearrange("p (t f) -> p t f", f=64)
                v.tensor_tensor(out=PcB[:].rearrange("p (t f) -> p t f", f=64), in0=RAr, in1=maskUi_b, op=ALU.mult)
                v.tensor_tensor(out=A1B[:].rearrange("p (t f) -> p t f", f=64), in0=RBr, in1=maskUs_b, op=ALU.mult)
                v.tensor_tensor(out=A1B[:].rearrange("p (t f) -> p t f", f=64),
                                in0=A1B[:].rearrange("p (t f) -> p t f", f=64), in1=c_sm_b, op=ALU.mult)
                v.tensor_tensor(out=B1B[:].rearrange("p (t f) -> p t f", f=64), in0=RBr, in1=maskLs_b, op=ALU.mult)
                i = v.tensor_tensor(out=B1B[:], in0=B1B[:], in1=crepB[:], op=ALU.mult)
                last_evac["RA"] = last_evac["RB"] = sV.inc(i)
            vec(_)

            def _(v, h=h):
                dstv = XB[:].rearrange("p (t f) -> p t f", f=128)[:, :, 0:64]
                dstk = XB[:].rearrange("p (t f) -> p t f", f=128)[:, :, 64:128]
                srcv = VagT_[:].rearrange("p (t f) -> p t f", f=1024)[:, :, h * 64:(h + 1) * 64]
                srck = KT_[:].rearrange("p (t f) -> p t f", f=1024)[:, :, h * 64:(h + 1) * 64]
                v.tensor_copy(out=dstv, in_=srcv)
                i = v.tensor_copy(out=dstk, in_=srck)
                sV.inc(i)
            vec(_)
        yield s_p1_mask

        state = {"x_v": None, "a_ready": None, "a_ready_a": 0,
                 "bufs": (A1B, B1B, A2B, B2B)}

        psTL = psT[:].bitcast(F32)

        def mk_level_ax(lv):
            def s_level_ax():
                Acur, Bcur, Anxt, Bnxt = state["bufs"]
                x_v = state["x_v"] if state["x_v"] is not None else sV.n
                a_ready = state["a_ready"] if state["a_ready"] is not None else sV.n
                a_ready_a = state["a_ready_a"]

                def _(t, Acur=Acur, x_v=x_v, a_ready=a_ready, a_ready_a=a_ready_a):
                    t.wait_ge(sV.h, max(a_ready, x_v, last_evac["RA"], last_evac["RB"]))
                    wa = max(a_ready_a, last_evacA["RA"], last_evacA["RB"])
                    if wa:
                        t.wait_ge(sAc.h, wa)
                    for ch in range(NCH):
                        reg = RA if ch < 16 else RB
                        cc = ch % 16
                        out = reg[(ch % 2) * 64:(ch % 2) * 64 + 64, (cc // 2) * 128:(cc // 2) * 128 + 128]
                        i = t.matmul(out=out, lhsT=pw_sl(Acur, ch), rhs=x_sl(ch, 0, 128),
                                     start=True, stop=True, tile_position=((ch % 2) * 64, (ch % 2) * 64))
                    sP.inc(i)
                ten(_)
                pv = sP.n

                if lv < 5:
                    def _(t, Acur=Acur, Bcur=Bcur, lv=lv):
                        if lv == 0:
                            t.wait_ge(sV.h, free["psT"])
                            if freeA["psT"]:
                                t.wait_ge(sAc.h, freeA["psT"])
                        for ch in range(NCH):
                            tp = ((ch % 2) * 64, (ch % 2) * 64)
                            i = t.matmul(out=pw_sl(psTL, ch), lhsT=pw_sl(Acur, ch), rhs=pw_sl(Bcur, ch),
                                         start=True, stop=True, tile_position=tp)
                        sP.inc(i)
                    ten(_)
                    state["pv2"] = sP.n

                op = ALU.subtract if lv == 0 else ALU.add

                def _(v, pv=pv, op=op):
                    v.wait_ge(sP.h, pv)
                    i = v.tensor_tensor(out=XB[:, 0:1024], in0=XB[:, 0:1024], in1=RA, op=op)
                    state["xa_v"] = sV.inc(i)
                    i = v.tensor_tensor(out=XB[:, 1024:2048], in0=XB[:, 1024:2048], in1=RB, op=op)
                    state["x_v"] = last_evac["RB"] = sV.inc(i)
                vec(_)

                if lv < 5:
                    def _(t, Acur=Acur, Bcur=Bcur):
                        t.wait_ge(sV.h, state["xa_v"])
                        for ch in range(NCH):
                            tp = ((ch % 2) * 64, (ch % 2) * 64)
                            i = t.matmul(out=pw_sl(RA, ch), lhsT=pw_sl(Bcur, ch), rhs=pw_sl(Acur, ch),
                                         start=True, stop=True, tile_position=tp)
                        sP.inc(i)
                    ten(_)
                    state["pv3"] = sP.n
            return s_level_ax

        def mk_level_sq(lv):
            def s_level_evac():
                Acur, Bcur, Anxt, Bnxt = state["bufs"]
                pv2 = state["pv2"]
                pv3 = state["pv3"]

                def _(a, pv2=pv2, Bnxt=Bnxt):
                    a.wait_ge(sP.h, pv2)
                    i = a.activation(out=Bnxt[:], in_=psTL, func=AF.Copy)
                    state["a_ready_a"] = sAc.inc(i)
                act(_)

                def _(v, pv3=pv3, Anxt=Anxt):
                    v.wait_ge(sP.h, pv3)
                    i = v.tensor_copy(out=Anxt[:], in_=RA)
                    state["a_ready"] = last_evac["RA"] = sV.inc(i)
                vec(_)
                state["bufs"] = (Anxt, Bnxt, Acur, Bcur)
            return s_level_evac

        for lv in range(6):
            yield mk_level_ax(lv)
            if lv < 5:
                yield mk_level_sq(lv)

        p4h = {}

        def s_p4a():
            if p4_parts < 1:
                return
            def _(v, h=h):
                v.wait_ge(sV.h, state["x_v"])
                c_b = smr("C").rearrange("p (t j) -> p t j", j=16)[:, :, h:h + 1].broadcast_to([128, 16, 128])
                i = v.tensor_tensor(out=XB[:].rearrange("p (t f) -> p t f", f=128),
                                    in0=XB[:].rearrange("p (t f) -> p t f", f=128), in1=c_b, op=ALU.mult)
                sV.inc(i)
            vec(_)
            xc_v = sV.n
            if p4_parts < 2:
                return

            def _(t, xc_v=xc_v, h=h):
                t.wait_ge(sV.h, max(xc_v, last_evac["RA"], last_evac["RB"]))
                if last_evacA["RA"] or last_evacA["RB"]:
                    t.wait_ge(sAc.h, max(last_evacA["RA"], last_evacA["RB"]))
                for ch in range(NCH):
                    tp = ((ch % 2) * 64, (ch % 2) * 64)
                    t.matmul(out=pw_sl(RA, ch), lhsT=x_sl(ch, 0, 64), rhs=pw_sl(PcB, ch),
                             start=True, stop=True, tile_position=tp)
                    i = t.matmul(out=pw_sl(RB, ch), lhsT=x_sl(ch, 64, 128), rhs=pw_sl(PcB, ch),
                                 start=True, stop=True, tile_position=tp)
                p4h["pv"] = sP.inc(i)
            ten(_)

            psTF = psT[:].bitcast(F32)
            p4h["psTF"] = psTF

            ar = state["a_ready"] if state["a_ready"] is not None else 0
            ara = state["a_ready_a"]

            def _(t, h=h, ar=ar, ara=ara):
                t.wait_ge(sV.h, max(free["psT"], ar))
                wa = max(freeA["psT"], ara)
                if wa:
                    t.wait_ge(sAc.h, wa)
                for ch in range(NCH):
                    tp = ((ch % 2) * 64, (ch % 2) * 64)
                    i = t.matmul(out=pw_sl(psTF, ch), lhsT=x_sl(ch, 64, 128), rhs=kt_sl(h, ch),
                                 start=True, stop=True, tile_position=tp)
                p4h["pvYK"] = sP.inc(i)
            ten(_)
        yield s_p4a

        def s_p4():
            if p4_parts < 3:
                return
            pv = p4h["pv"]
            pvYK = p4h["pvYK"]
            psTF = p4h["psTF"]

            def _(a, pvYK=pvYK):
                a.wait_ge(sP.h, pvYK)
                i = a.activation(out=hstg[:], in_=psTF, func=AF.Copy, scale=-1.0)
                sAc.inc(i)
            act(_)
            yk_a = sAc.n

            def _(g, yk_a=yk_a, h=h):
                g.wait_ge(sAc.h, yk_a)
                hh = (h % 2) * 64
                for par in range(2):
                    src_y = hstg[par * 64:par * 64 + 64, :].rearrange("p (c f) -> p c f", f=64)
                    dst_y = pers(h)["YKneg"][hh:hh + 64, :].rearrange("p (c f) -> p c f", f=64)[:, par::2, :]
                    i = g.dma_start(out=dst_y, in_=src_y)
                    sD.inc(i, 16)
            gps(_)
            rpy_v = sD.n

            def _(t, yk_a=yk_a, h=h):
                t.wait_ge(sAc.h, yk_a)
                for ch in range(NCH):
                    tp = ((ch % 2) * 64, (ch % 2) * 64)
                    i = t.matmul(out=pw_sl(psTF, ch), lhsT=kt_sl(h, ch), rhs=x_sl(ch, 0, 64),
                                 start=True, stop=True, tile_position=tp)
                sP.inc(i)
            ten(_)
            pvF0 = sP.n

            def _(a, pvF0=pvF0, rpy_v=rpy_v):
                a.wait_ge(sP.h, pvF0)
                a.wait_ge(sD.h, rpy_v)
                i = a.activation(out=hstg[:], in_=psTF, func=AF.Copy)
                freeA["psT"] = sAc.inc(i)
            act(_)
            f0_a = sAc.n

            def _(g, f0_a=f0_a, h=h):
                g.wait_ge(sAc.h, f0_a)
                hh = (h % 2) * 64
                for par in range(2):
                    src_f = hstg[par * 64:par * 64 + 64, :].rearrange("p (c f) -> p c f", f=64)
                    dst_f = pers(h)["F0T"][hh:hh + 64, :].rearrange("p (c f) -> p c f", f=64)[:, par::2, :]
                    i = g.dma_start(out=dst_f, in_=src_f)
                    sD.inc(i, 16)
            gps(_)
            rpf_v = sD.n

            def _(a, pvF0=pvF0):
                a.wait_ge(sP.h, pvF0)
                a.activation(out=XB[:, 0:1024], in_=RA, func=AF.Copy)
                i = a.activation(out=XB[:, 1024:2048], in_=RB, func=AF.Copy)
                last_evacA["RA"] = last_evacA["RB"] = sAc.inc(i)
            act(_)
            oz_a = sAc.n

            def _(g, oz_a=oz_a, h=h):
                g.wait_ge(sAc.h, oz_a)
                hh = (h % 2) * 64
                for par in range(2):
                    src_z = XB[par * 64:par * 64 + 64, 1024:2048].rearrange("p (c f) -> p c f", f=64)
                    dst_z = pers(h)["GT"][hh:hh + 64, :].rearrange("p (c f) -> p c f", f=64)[:, par::2, :]
                    i = g.dma_start(out=dst_z, in_=src_z)
                    sD.inc(i, 16)
            gps(_)
            rpz_v = sD.n

            def _(g, h=h):
                hh = (h % 2) * 64
                for par in range(2):
                    src_o = XB[par * 64:par * 64 + 64, 0:1024].rearrange("p (c f) -> p c f", f=64)
                    dst_o = pers(h)["OlocT"][hh:hh + 64, :].rearrange("p (c f) -> p c f", f=64)[:, par::2, :]
                    i = g.dma_start(out=dst_o, in_=src_o)
                    sD.inc(i, 16)
            gps(_)
            rp1_v = sD.n

            def _(v, rpz_v=rpz_v, h=h):
                v.wait_ge(sD.h, rpz_v)
                hh = (h % 2) * 64
                qsl = QD_[hh:hh + 64, (h // 2) * 2048:(h // 2) * 2048 + 2048]
                i = v.tensor_tensor(out=pers(h)["GT"][hh:hh + 64, :], in0=qsl,
                                    in1=pers(h)["GT"][hh:hh + 64, :], op=ALU.subtract)
                sV.inc(i)
            vec(_)
            gt_v = sV.n
            if p4_parts < 4:
                return

            def _(v, rp1_v=rp1_v, h=h):
                v.wait_ge(sD.h, rp1_v)
                i = v.nop()
                head_state[h] = sV.inc(i)
            vec(_)
        yield s_p4

    def chain_step(h, ch):
        hh = (h % 2) * 64
        hp = h // 2
        P = pers(h)
        mt = MTall[hh:hh + 64, hp * 64:(hp + 1) * 64]
        slot_o = ROUT[hh:hh + 64, (ch % 8) * 64:(ch % 8) * 64 + 64]
        slot_u = RUPD[hh:hh + 64, (ch % 2) * 64:(ch % 2) * 64 + 64]

        def _(t, h=h, ch=ch, P=P):
            if ch == 0:
                t.wait_ge(sV.h, max(head_state[h], mt0_v))
                if h >= 2:
                    t.wait_ge(sV.h, chain_out_evac[(h - 2, 3)])
                    ev, val = chain_upd_v[(h - 2, NCH - 1)]
                    t.wait_ge(ev.h, val)
            else:
                ev, val = chain_upd_v[(h, ch - 1)]
                t.wait_ge(ev.h, val)
            if ch >= 8:
                t.wait_ge(sV.h, chain_out_evac[(h, ch // 8 - 1)])
            t.matmul(out=slot_o, lhsT=mt, rhs=P["GT"][hh:hh + 64, ch * 64:(ch + 1) * 64],
                     start=True, stop=True, tile_position=(hh, hh))
            t.matmul(out=slot_u, lhsT=P["YKneg"][hh:hh + 64, ch * 64:(ch + 1) * 64],
                     rhs=mt, start=True, stop=False, tile_position=(hh, hh))
            t.matmul(out=slot_u, lhsT=identB[hh:hh + 64, hh:hh + 64],
                     rhs=mt, start=False, stop=False, tile_position=(hh, hh))
            i = t.matmul(out=slot_u, lhsT=identB[hh:hh + 64, hh:hh + 64],
                         rhs=P["F0T"][hh:hh + 64, ch * 64:(ch + 1) * 64],
                         start=False, stop=True, tile_position=(hh, hh))
            sP.inc(i)
        ten(_)
        pv = sP.n
        gc = gCrep2[hh:hh + 64, hp * 32 + ch: hp * 32 + ch + 1]

        if False:
            def _(a, pv=pv, gc=gc):
                a.wait_ge(sP.h, pv)
                i = a.activation(out=mt, in_=slot_u, func=AF.Copy, scale=gc)
                last_upd_a[0] = sAc.inc(i)
                chain_upd_v[(h, ch)] = (sAc, last_upd_a[0])
            act(_)
            if ch % 8 == 7:
                def _(v, h=h, ch=ch, pv=pv, P=P):
                    v.wait_ge(sP.h, pv)
                    base = ch - 7
                    dst = AR1[hh:hh + 64, hp * 2048 + base * 64: hp * 2048 + (base + 8) * 64]
                    i = v.tensor_tensor(out=dst, in0=ROUT[hh:hh + 64, :],
                                        in1=P["OlocT"][hh:hh + 64, base * 64:(base + 8) * 64],
                                        op=ALU.add)
                    chain_out_evac[(h, ch // 8)] = sV.inc(i)
                vec(_)
        else:
            def _(v, h=h, ch=ch, pv=pv, gc=gc, P=P):
                v.wait_ge(sP.h, pv)
                i = v.tensor_scalar(out=mt, in0=slot_u, scalar1=gc, scalar2=None, op0=ALU.mult)
                chain_upd_v[(h, ch)] = (sV, sV.inc(i))
                if ch % 8 == 7:
                    base = ch - 7
                    dst = AR1[hh:hh + 64, hp * 2048 + base * 64: hp * 2048 + (base + 8) * 64]
                    i = v.tensor_tensor(out=dst, in0=ROUT[hh:hh + 64, :],
                                        in1=P["OlocT"][hh:hh + 64, base * 64:(base + 8) * 64],
                                        op=ALU.add)
                    chain_out_evac[(h, ch // 8)] = sV.inc(i)
            vec(_)

    def _interleave(a, b):
        out = []
        for i in range(max(len(a), len(b))):
            if i < len(a):
                out.append(a[i])
            if i < len(b):
                out.append(b[i])
        return out

    for h in range(solve_heads):
        steps = list(solve_steps(h))[:solve_nsteps]
        tasks = []
        if do_chain:
            a = [(h - 2, c) for c in range(16, NCH)] if h >= 2 else []
            b = [(h - 1, c) for c in range(0, 16)] if h >= 1 else []
            tasks = _interleave(a, b)
        per = (len(tasks) + len(steps) - 1) // len(steps) if tasks else 0
        ti = 0
        for s in steps:
            s()
            for _k in range(min(per, len(tasks) - ti)):
                chain_step(*tasks[ti])
                ti += 1
        while ti < len(tasks):
            chain_step(*tasks[ti])
            ti += 1
    if do_chain:
        H_ = solve_heads
        a = [(H_ - 2, c) for c in range(16, NCH)] if H_ >= 2 else []
        b = [(H_ - 1, c) for c in range(0, 16)] if H_ >= 1 else []
        for t_ in _interleave(a, b) + [(H_ - 1, c) for c in range(16, NCH)]:
            chain_step(*t_)
    attn_mix_v = sV.n
    tap("GTp", GTp[:], sV, attn_mix_v)
    tap("OlocTp", OlocTp[:], sV, attn_mix_v)
    tap("YKnegp", YKnegp[:], sV, attn_mix_v)
    tap("F0Tp", F0Tp[:], sV, attn_mix_v)
    if stop_phase <= 7.5:
        stub_out()
        st_.close()
        return nc, tapnames
    tap("outG", AR1[:], sV, attn_mix_v)
    tap("MTall", MTall[:], sV, attn_mix_v)
    tap("gCrep2end", gCrep2[:], sV, attn_mix_v)

    if stop_phase <= 8:
        stub_out()
        st_.close()
        return nc, tapnames

    sq_pe = [0]
    ro_pv = 0
    for dt in range(NDT):
        def _(v, dt=dt, attn_mix_v=attn_mix_v):
            v.wait_ge(sV.h, attn_mix_v)
            if sq_pe[0]:
                v.wait_ge(sP.h, sq_pe[0])
            i = v.tensor_tensor(out=sqB[:], in0=AR1[:, dt * 2048:(dt + 1) * 2048],
                                in1=AR1[:, dt * 2048:(dt + 1) * 2048], op=ALU.mult)
            sV.inc(i)
        vec(_)
        sqv = sV.n

        def _(t, dt=dt, sqv=sqv):
            t.wait_ge(sV.h, max(sqv, ones_v))
            if dt == 0:
                t.wait_ge(sV.h, max(last_evac["RA"], last_evac["RB"]))
                if last_evacA["RA"] or last_evacA["RB"]:
                    t.wait_ge(sAc.h, max(last_evacA["RA"], last_evacA["RB"]))
            for sc in range(4):
                i = t.matmul(out=psA[0:1, sc * 512:(sc + 1) * 512], lhsT=onescol[:],
                             rhs=sqB[:, sc * 512:(sc + 1) * 512],
                             start=(dt == 0), stop=(dt == NDT - 1))
            sq_pe[0] = sP.inc(i)
        ten(_)
        ro_pv = sP.n

    def _(v, ro_pv=ro_pv):
        v.wait_ge(sP.h, ro_pv)
        i = v.tensor_scalar(out=psA[0:1, :], in0=psA[0:1, :], scalar1=1.0 / D, scalar2=EPS_RMS,
                            op0=ALU.mult, op1=ALU.add)
        sV.inc(i)
    vec(_)
    ro1_v = sV.n

    def _(a, ro1_v=ro1_v):
        a.wait_ge(sV.h, ro1_v)
        a.activation(out=psA[32:33, :], in_=psA[0:1, :], func=AF.Ln)
        i = a.activation(out=sqB[0:1, 0:2048], in_=psA[32:33, :], func=AF.Exp, scale=-0.5)
        sAc.inc(i)
    act(_)
    ro2_v = sAc.n

    def _(v, ro2_v=ro2_v):
        v.wait_ge(sAc.h, ro2_v)
        i = v.nop()
        free["psA"] = sV.inc(i)
    vec(_)
    ro_v = sV.n

    mixn_prev = [ro_v]
    for dt in range(NDT):
        def _(t, dt=dt):
            t.wait_ge(sV.h, mixn_prev[0])
            for sc in range(4):
                i = t.matmul(out=psA[:, sc * 512:(sc + 1) * 512], lhsT=onesONE[:],
                             rhs=sqB[0:1, sc * 512:(sc + 1) * 512], start=True, stop=True)
            sP.inc(i)
        ten(_)
        rof_pv = sP.n

        def _(v, dt=dt, rof_pv=rof_pv):
            v.wait_ge(sP.h, rof_pv)
            i = v.tensor_tensor(out=AR1[:, dt * 2048:(dt + 1) * 2048],
                                in0=AR1[:, dt * 2048:(dt + 1) * 2048], in1=psA[:], op=ALU.mult)
            mixn_prev[0] = free["psA"] = sV.inc(i)
        vec(_)
    mixn_v = sV.n
    tap("mixn", AR1[:], sV, mixn_v)

    if stop_phase <= 9:
        stub_out()
        st_.close()
        return nc, tapnames

    def _(s, mixn_v=mixn_v):
        s.wait_ge(sV.h, mixn_v)
        i = s.dma_start(out=AR0[:], in_=xnT_scr[:])
        sX.inc(i, 16)
    syn(_)
    xnT2_v = sX.n

    hid_ar = [AR2, AR3, AR4]
    hidmm_done = {}
    silu_done = {}
    evac_done = {}
    psTF2 = psT[:].bitcast(F32)
    hid_v = 0
    for ht in range(NHT):
        wgt_b = wgt if ht % 2 == 0 else wgt2
        wut_b = wutl if ht % 2 == 0 else wutl2

        def _(g, ht=ht, wgt_b=wgt_b, wut_b=wut_b, mixn_v=mixn_v):
            if ht < 2:
                g.wait_ge(sV.h, mixn_v)
            if ht >= 2:
                g.wait_ge(sP.h, hidmm_done.get(ht - 2, 0))
            i = g.dma_start(out=wgt_b, in_=w["wgT"][ht * 128:(ht + 1) * 128, :])
            sD.inc(i, 16)
            i = g.dma_start(out=wut_b, in_=w["wuT"][ht * 128:(ht + 1) * 128, :])
            sD.inc(i, 16)
        gps(_)
        wl_v = sD.n

        def _(t, wl_v=wl_v, ht=ht, wgt_b=wgt_b):
            t.wait_ge(sD.h, wl_v)
            if ht == 0:
                t.wait_ge(sV.h, free["psA"])
                t.wait_ge(sX.h, xnT2_v)
            else:
                t.wait_ge(sAc.h, silu_done[ht - 1])
            for sc in range(4):
                for kt in range(NDT):
                    i = t.matmul(out=psA[:, sc * 512:(sc + 1) * 512],
                                 lhsT=wgt_b[:, kt * 128:(kt + 1) * 128],
                                 rhs=AR0[:, kt * 2048 + sc * 512: kt * 2048 + sc * 512 + 512],
                                 start=(kt == 0), stop=(kt == NDT - 1))
            sP.inc(i)
        ten(_)
        g_pv = sP.n

        def _(a, g_pv=g_pv, ht=ht):
            a.wait_ge(sP.h, g_pv)
            if ht >= 1:
                a.wait_ge(sV.h, evac_done[ht - 1])
            i = a.activation(out=siluB[:], in_=psA[:], func=AF.Silu, scale=SCALE)
            silu_done[ht] = sAc.inc(i)
        act(_)

        def _(t, ht=ht, wut_b=wut_b, mixn_v=mixn_v):
            if ht == 0:
                t.wait_ge(sV.h, mixn_v)
                t.wait_ge(sAc.h, last_upd_a[0])
            else:
                t.wait_ge(sV.h, evac_done[ht - 1])
            for sc in range(4):
                reg = psB[:, (sc % 2) * 512:(sc % 2) * 512 + 512] if sc < 2 \
                    else psTF2[:, (sc % 2) * 512:(sc % 2) * 512 + 512]
                for kt in range(NDT):
                    i = t.matmul(out=reg,
                                 lhsT=wut_b[:, kt * 128:(kt + 1) * 128],
                                 rhs=AR0[:, kt * 2048 + sc * 512: kt * 2048 + sc * 512 + 512],
                                 start=(kt == 0), stop=(kt == NDT - 1))
            hidmm_done[ht] = sP.inc(i)
        ten(_)
        u_pv = sP.n

        def _(v, u_pv=u_pv, ht=ht):
            v.wait_ge(sP.h, u_pv)
            v.wait_ge(sAc.h, silu_done[ht])
            dst = hid_ar[ht // 8][:, (ht % 8) * 2048:((ht % 8) + 1) * 2048]
            v.tensor_tensor(out=dst[:, 0:1024], in0=siluB[:, 0:1024], in1=psB[:], op=ALU.mult)
            i = v.tensor_tensor(out=dst[:, 1024:2048], in0=siluB[:, 1024:2048],
                                in1=psTF2[:], op=ALU.mult)
            evac_done[ht] = free["psA"] = sV.inc(i)
        vec(_)
        hid_v = sV.n

    tap("xnT2", AR0[:], sX, xnT2_v)
    tap("hidden0", AR2[:], sV, hid_v)
    tap("hidden1", AR3[:], sV, hid_v)
    tap("hidden2", AR4[:], sV, hid_v)

    if stop_phase <= 10:
        stub_out()
        st_.close()
        return nc, tapnames

    def _(g, hid_v=hid_v):
        g.wait_ge(sV.h, hid_v)
        i = g.dma_start(out=wbuf[:].rearrange("p (t c) -> p t c", c=D),
                        in_=w["Wo"][:].rearrange("(t p) c -> p t c", p=128))
        sD.inc(i, 16)
    gps(_)
    wo_v = sD.n

    def _(v, wo_v=wo_v):
        v.wait_ge(sD.h, wo_v)
        in1 = onwP[:].unsqueeze(2).broadcast_to([128, NDT, D])
        i = v.tensor_tensor(out=wbuf[:].rearrange("p (t c) -> p t c", c=D),
                            in0=wbuf[:].rearrange("p (t c) -> p t c", c=D), in1=in1, op=ALU.mult)
        sV.inc(i)
    vec(_)
    wofold_v = sV.n

    y_evac = {}
    out_dma = {}
    ymm_last = {}
    for dmac in range(2):
        def _(g, dmac=dmac, hid_v=hid_v):
            g.wait_ge(sV.h, hid_v)
            if dmac == 1:
                g.wait_ge(sP.h, ymm_last[0])
            i = g.dma_start(out=wdTm, in_=w["wdT"][dmac * 128:(dmac + 1) * 128, :])
            sD.inc(i, 16)
        gps(_)
        wdl_v = sD.n

        def _(a, wdl_v=wdl_v):
            a.wait_ge(sD.h, wdl_v)
            i = a.activation(out=wdTm, in_=wdTm, func=AF.Copy, scale=SCALE * SCALE)
            sAc.inc(i)
        act(_)
        wdT_a = sAc.n

        for sti in range(NST):
            slot = sti % 2
            yb = ybuf if slot == 0 else ybuf2
            xb = xfin if slot == 0 else xfin2

            def _(s, sti=sti, dmac=dmac, xb=xb, hid_v=hid_v):
                if dmac == 0 and sti < 2:
                    s.wait_ge(sV.h, hid_v)
                key = (dmac, sti - 2) if sti >= 2 else ((dmac - 1, NST - 2 + sti) if dmac else None)
                if key in y_evac:
                    s.wait_ge(sV.h, y_evac[key])
                i = s.dma_start(out=xb, in_=x_h[sti * 128:(sti + 1) * 128,
                                                dmac * 512:(dmac + 1) * 512])
                sX.inc(i, 16)
            syn(_)
            x_v = sX.n

            def _(t, sti=sti, dmac=dmac, slot=slot, wdT_a=wdT_a):
                t.wait_ge(sAc.h, wdT_a)
                if dmac == 0 and sti == 0:
                    t.wait_ge(sV.h, max(wofold_v, hid_v, mixn_v))
                key = (dmac, sti - 2) if sti >= 2 else ((dmac - 1, NST - 2 + sti) if dmac else None)
                if key in y_evac:
                    t.wait_ge(sV.h, y_evac[key])
                reg = psB[:, slot * 512:(slot + 1) * 512]
                for kt in range(NDT):
                    t.matmul(out=reg,
                             lhsT=AR1[:, kt * 2048 + sti * 128: kt * 2048 + (sti + 1) * 128],
                             rhs=wbuf[:, kt * 1024 + dmac * 512: kt * 1024 + (dmac + 1) * 512],
                             start=(kt == 0), stop=False)
                for b in range(NHT):
                    i = t.matmul(out=reg,
                                 lhsT=hid_ar[b // 8][:, (b % 8) * 2048 + sti * 128:
                                                     (b % 8) * 2048 + (sti + 1) * 128],
                                 rhs=wdTm[:, b * 512:(b + 1) * 512],
                                 start=False, stop=(b == NHT - 1))
                ymm_last[dmac] = sP.inc(i)
            ten(_)
            ymm_pv = sP.n

            def _(v, ymm_pv=ymm_pv, x_v=x_v, slot=slot, yb=yb, xb=xb, dmac=dmac, sti=sti):
                v.wait_ge(sP.h, ymm_pv)
                v.wait_ge(sX.h, x_v)
                key = (dmac, sti - 2) if sti >= 2 else ((dmac - 1, NST - 2 + sti) if dmac else None)
                if key in out_dma:
                    v.wait_ge(sO.h, out_dma[key])
                reg = psB[:, slot * 512:(slot + 1) * 512]
                i = v.tensor_tensor(out=yb, in0=reg, in1=xb, op=ALU.add)
                y_evac[(dmac, sti)] = sV.inc(i)
            vec(_)
            ye_v = sV.n

            def _(s, sti=sti, dmac=dmac, ye_v=ye_v, yb=yb):
                s.wait_ge(sV.h, ye_v)
                i = s.dma_start(out=y_h[sti * 128:(sti + 1) * 128, dmac * 512:(dmac + 1) * 512],
                                in_=yb)
                out_dma[(dmac, sti)] = sO.inc(i, 16)
            syn(_)

    total_out = sO.n

    def _(g, total_out=total_out):
        g.wait_ge(sO.h, total_out)
    gps(_)

    st_.close()
    return nc, tapnames


_cache = {}


def _get_nc(tapkey=()):
    if tapkey not in _cache:
        _cache[tapkey] = _build([(t, list(s), d) for t, s, d in tapkey])
    return _cache[tapkey]


def _host_inputs(inputs, b):
    f = lambda k: np.ascontiguousarray(np.asarray(inputs[k], dtype=np.float32))
    Wrq = np.asarray(inputs["L0_attn_Wrq"], np.float32)
    Wrk = np.asarray(inputs["L0_attn_Wrk"], np.float32)
    Wrq_p = np.zeros((D, 1024), np.float32)
    Wrk_p = np.zeros((D, 1024), np.float32)
    for h in range(H):
        Wrq_p[:, h * 64:h * 64 + RR] = Wrq[:, h * RR:(h + 1) * RR]
        Wrk_p[:, h * 64:h * 64 + RR] = Wrk[:, h * RR:(h + 1) * RR]
    Wfl = np.concatenate([np.asarray(inputs["L0_attn_Wf"], np.float32),
                          np.asarray(inputs["L0_attn_Wl"], np.float32)], axis=1)
    bfl = np.concatenate([np.asarray(inputs["L0_attn_bf"], np.float32),
                          np.asarray(inputs["L0_attn_bl"], np.float32)])[None, :]

    def cdiag_of(name):
        cw = np.asarray(inputs[name], np.float32)
        out = np.zeros((128, NDT, 4, 32), np.float32)
        for d in range(D):
            p, dt = d % 128, d // 128
            out[p, dt, :, p % 32] = cw[d, :]
        return np.ascontiguousarray(out.reshape(128, NDT * 4 * 32))

    onw = np.ascontiguousarray(np.asarray(inputs["L0_attn_on_w"], np.float32).reshape(NDT, 128).T)
    ident = np.eye(128, dtype=np.float32)
    t = np.arange(64)
    maskUi = np.ascontiguousarray(np.tile((t[:, None] <= t[None, :]).astype(np.float32), (2, 1)))
    maskUs = np.ascontiguousarray(np.tile((t[:, None] < t[None, :]).astype(np.float32), (2, 1)))
    maskLs = np.ascontiguousarray(np.tile((t[:, None] > t[None, :]).astype(np.float32), (2, 1)))
    hv4 = np.zeros((128, 128), np.float32)
    for bb_ in range(0, 128, 32):
        hv4[bb_, 0:64] = 1.0
        hv4[bb_ + 1, 64:128] = 1.0
    hv4[64, :] = 0.0
    hv4[65, :] = 0.0
    hv4[64, 0:64] = 1.0
    hv4[65, 64:128] = 1.0
    bonesAll = np.zeros((128, 8, 32), np.float32)
    for dt_ in range(8):
        bonesAll[0:64, dt_, 2 * dt_] = 1.0
        bonesAll[64:128, dt_, 2 * dt_ + 1] = 1.0
    bonesAll = np.ascontiguousarray(bonesAll.reshape(128, 256))
    p = np.arange(128)
    hsl = np.arange(16)
    selh = ((p[None, :] >= 64).astype(np.int32) == (hsl[:, None] % 2)).astype(np.float32)
    hseldt = (np.arange(8)[None, :] == (hsl[:, None] // 2)).astype(np.float32)

    return {
        "x": np.ascontiguousarray(np.asarray(inputs["L0_attn_x"], np.float32)[b]),
        "Wq": f("L0_attn_Wq"), "Wk": f("L0_attn_Wk"), "Wv": f("L0_attn_Wv"),
        "Wo": f("L0_attn_Wo"),
        "Wrq": Wrq_p, "Wrk": Wrk_p, "Wfl": np.ascontiguousarray(Wfl),
        "bfl_f": np.ascontiguousarray(bfl.reshape(32, 1)[0:16]),
        "bfl_l": np.ascontiguousarray(bfl.reshape(32, 1)[16:32]),
        "cdq": cdiag_of("L0_attn_conv_q"), "cdk": cdiag_of("L0_attn_conv_k"),
        "cdv": cdiag_of("L0_attn_conv_v"),
        "gs": np.ascontiguousarray(np.asarray(inputs["L0_attn_graph_scale"], np.float32).reshape(16, 1)),
        "qnw": np.ascontiguousarray(np.asarray(inputs["L0_attn_qn_w"], np.float32).reshape(64, 1)),
        "knw": np.ascontiguousarray(np.asarray(inputs["L0_attn_kn_w"], np.float32).reshape(64, 1)),
        "onw": onw,
        "wgT": np.ascontiguousarray(
            np.asarray(inputs["L0_ffn_w_gate"], np.int32)
            .reshape(24, 128, 8, 128).transpose(0, 3, 2, 1).reshape(3072, 1024)),
        "wuT": np.ascontiguousarray(
            np.asarray(inputs["L0_ffn_w_up"], np.int32)
            .reshape(24, 128, 8, 128).transpose(0, 3, 2, 1).reshape(3072, 1024)),
        "wdT": np.ascontiguousarray(
            np.asarray(inputs["L0_ffn_w_down"], np.int32)
            .reshape(2, 512, 24, 128).transpose(0, 3, 2, 1).reshape(256, 12288)),
        "ident": ident, "maskUi": maskUi, "maskUs": maskUs, "maskLs": maskLs,
        "hv4": hv4, "bonesAll": bonesAll,
        "onesone": np.ones((1, 128), np.float32),
        "selh": np.ascontiguousarray(selh), "hsel": np.ascontiguousarray(hseldt),
    }


def _run(inputs, taps=()):
    tapkey = tuple((t, tuple(s), d) for t, s, d in taps)
    nc, tapnames = _get_nc(tapkey)
    in_maps = [_host_inputs(inputs, b) for b in range(8)]
    res = run_bass_kernel_spmd(nc, in_maps, core_ids=list(range(8)))
    y = np.stack([r["y"] for r in res.results], axis=0)
    tapd = [{t: r["tap_" + t] for t in tapnames} for r in res.results]
    return y, tapd


def kernel(**inputs):
    y, _ = _run(inputs)
    return y
